# version 07
# baseline (speedup 1.0000x reference)
import numpy as np
import ml_dtypes
import concourse.bass as bass
import concourse.mybir as mybir
from concourse.bass_core_rust import IndirectOffsetOnAxis
from concourse.bass_utils import run_bass_kernel_spmd
from contextlib import ExitStack

F32 = mybir.dt.float32
BF16 = mybir.dt.bfloat16
I32 = mybir.dt.int32
AF = mybir.ActivationFunctionType
OP = mybir.AluOpType
ET = mybir.EngineType
BF = ml_dtypes.bfloat16

SQ2PI = 0.7978845608028654
GK = 0.044715 * SQ2PI
EPS = 1e-5
BISECT_ITERS = 36


class Cfg:
    def __init__(self, L=16384, HID=4096, CH=512, taps=False, stop_after=None):
        self.stop_after = stop_after
        self.B = 2
        self.L = L
        self.DM = 1024
        self.N = 16
        self.HID = HID
        self.HQ = HID // 4
        self.NHT = self.HQ // 128
        self.K = L // 4
        self.TOK = L // 8
        self.CH = CH
        self.NCH = L // CH
        self.NBLK = self.TOK // 128
        self.FB = self.TOK // 2
        self.PD = L // 128
        self.NTAU = self.K // 128
        self.GQ = 4 if self.NTAU % 8 == 0 else (2 if self.NTAU % 4 == 0 else 1)
        self.NG = self.NTAU // self.GQ
        self.CROW = self.K + 128
        self.QAG = max(1, self.NCH // 4)
        self.taps = taps
        assert self.NBLK % 2 == 0 and self.FB >= 128 and self.CH <= 512


def tok_of(cfg, s, f):
    r = s >> 1
    i = (s & 1) * (cfg.NBLK // 2) + (f >> 7)
    j = f & 127
    return 128 * (8 * i + r) + j


def agrow_of_token(cfg, t):
    i, r, j = t // 1024 if False else (t // 128) // 8, (t // 128) % 8, t % 128
    return cfg.TOK * r + 128 * i + j


class Sem:
    def __init__(self, nc, name):
        self.h = nc.alloc_semaphore(name)
        self.v = 0


def build_program(cfg):
    nc = bass.Bass("TRN2", target_bir_lowering=False, dynamic_dma_scratch_size=49152)
    B, L, DM, N = cfg.B, cfg.L, cfg.DM, cfg.N
    CH, NCH, K, TOK, NBLK = cfg.CH, cfg.NCH, cfg.K, cfg.TOK, cfg.NBLK
    HID, HQ, NHT, NTAU = cfg.HID, cfg.HQ, cfg.NHT, cfg.NTAU
    GQ, NG, CROW, QAG = cfg.GQ, cfg.NG, cfg.CROW, cfg.QAG
    FB, PD = cfg.FB, cfg.PD
    RG = [list(range(8))]
    AGTOT = 8 * TOK

    def ein(name, shape, dt=F32):
        return nc.dram_tensor(name, shape, dt, kind="ExternalInput").ap()

    xT = ein("xT", [B, 128, L])
    projW = ein("projW", [128, 96])
    wdt_ext = ein("wdt_ext", [65, 128])
    a_sl = ein("a_sl", [128, N])
    d_sl = ein("d_sl", [128, 1])
    gate_w = ein("gate_w", [DM, 8])
    w1e = ein("w1e", [DM, HID])
    w2e = ein("w2e", [HID, DM])
    idn32 = ein("idn32", [128, 128])
    idnbf = ein("idnbf", [128, 128], BF16)
    onehots = ein("onehots", [16, 2048])
    gmask = ein("gmask", [128, 128])
    tmask = ein("tmask", [128, 128])
    tokmap = ein("tokmap", [16, FB])
    tokmap128 = ein("tokmap128", [128, PD])
    jvals = ein("jvals", [128, NTAU], I32)
    meta = ein("meta", [1, 16], I32)

    out = nc.dram_tensor("out", [B, NBLK, 128, DM], F32, kind="ExternalOutput").ap()

    def idram(name, shape, dt=F32):
        return nc.dram_tensor(name, shape, dt, kind="Internal").ap()

    projpart = [idram(f"projpart{b}", [96, L]) for b in range(B)]
    projAR = [idram(f"projAR{b}", [96, L]) for b in range(B)]
    deltaD = [idram(f"delta{b}", [128, L]) for b in range(B)]
    zin = [idram(f"zin{b}", [128, L], BF16) for b in range(B)]
    zAG = [idram(f"zAG{b}", [DM, L], BF16) for b in range(B)]
    x1own = [idram(f"x1own{b}", [TOK, DM], BF16) for b in range(B)]
    x1AG = [idram(f"x1AG{b}", [AGTOT, DM], BF16) for b in range(B)]
    probsT = [idram(f"probsT{b}", [8, TOK]) for b in range(B)]
    probsAG = [idram(f"probsAG{b}", [64, TOK]) for b in range(B)]
    ubounce = [idram(f"ubounce{b}", [16, FB], I32) for b in range(B)]
    gbounce = [idram(f"gbounce{b}", [16, FB]) for b in range(B)]
    idg = [idram(f"idg{b}", [128, 2 * NTAU]) for b in range(B)]
    posdense = [idram(f"posdense{b}", [128, PD], I32) for b in range(B)]
    posAG = [idram(f"posAG{b}", [1024, PD], I32) for b in range(B)]
    contrib = [idram(f"contrib{b}", [CROW, DM], BF16) for b in range(B)]
    contribAG = [idram(f"contribAG{b}", [8 * CROW, DM], BF16) for b in range(B)]
    zownD = [idram(f"zownD{b}", [DM, NBLK * 128], BF16) for b in range(B)]
    probsAGx2 = [idram(f"probsAGx2{b}", [128, TOK]) for b in range(B)]
    posownD = [idram(f"posownD{b}", [1024, NBLK], I32) for b in range(B)]
    probsRot = [idram(f"probsRot{b}", [64, TOK]) for b in range(B)]

    tap = {}
    if cfg.taps:
        def tout(name, shape, dt=F32):
            tap[name] = nc.dram_tensor("tap_" + name, shape, dt, kind="ExternalOutput").ap()
        tout("delta", [B, 128, L])
        tout("y", [B, 128, L])
        tout("z", [B, DM, L])
        tout("x1", [B, AGTOT, DM])
        tout("probs", [B, 64, TOK])
        tout("bis", [B, 128, 8])
        tout("idg", [B, 128, 2 * NTAU])
        tout("contrib", [B, CROW, DM])
        tout("xgt", [1, 128, 2 * 8 * 64 * 128], BF16)
        tout("c0", [1, CROW, DM], BF16)
        tout("w1q", [1, 128, 8 * 1024], BF16)
        tout("wrow", [1, 128, 4 * 1024], BF16)
        tout("w2q", [1, 128, 8 * 1024], BF16)
        tout("hgel", [1, 128, 64 * 64 * 128], BF16)
        tout("pos", [B, 1024, PD], I32)

    ctx = ExitStack()
    sbt = lambda name, shape, dt=F32: ctx.enter_context(nc.sbuf_tensor(name, shape, dt))
    pst = lambda name, shape, dt=F32: ctx.enter_context(nc.psum_tensor(name, shape, dt))
    V, A, T, G, Y = nc.vector, nc.scalar, nc.tensor, nc.gpsimd, nc.sync

    acol = sbt("acol", [128, N])
    dcol = sbt("dcol", [128, 1])
    idb = sbt("idb", [128, 128], BF16)
    id32 = sbt("id32", [128, 128])
    onehb = sbt("onehb", [16, 2048], BF16)
    gmb = sbt("gmb", [128, 128])
    tmb = sbt("tmb", [128, 128])
    tokm = sbt("tokm", [16, FB])
    jvb = sbt("jvb", [128, NTAU], I32)
    metab = sbt("metab", [1, 16], I32)
    gwb = sbt("gwb", [128, 8 * 8], BF16)
    projwb = sbt("projwb", [128, 96])
    wdtb = sbt("wdtb", [65, 128])
    onesc = sbt("onesc", [128, max(FB, 1024)], BF16)
    zerobf = sbt("zerobf", [128, DM], BF16)
    zrow32 = sbt("zrow32", [128, PD])
    w1q = sbt("w1q", [128, 8 * HQ], BF16)
    w2q = sbt("w2q", [128, NHT * DM], BF16)
    ALIAS = (8 * HQ * 2 >= 2 * CH * 4) and (NHT * DM * 2 >= 2 * CH * 4)
    xc = sbt("xc", [128, 2 * CH])
    dc = sbt("dc", [128, 2 * CH])
    dxc = sbt("dxc", [128, 2 * CH], BF16)
    bccB = sbt("bccB", [16, 2 * CH], BF16)
    bccC = sbt("bccC", [16, 2 * CH], BF16)
    dab = sbt("dab", [128, 2 * CH], BF16)
    dbxb = sbt("dbxb", [128, 2 * CH], BF16)
    hb = sbt("hb", [128, N * CH], BF16)
    hcb = sbt("hcb", [128, 8 * CH], BF16)
    yhalf = sbt("yhalf", [128, CH])
    yb = sbt("yb", [128, 2 * CH])
    zbuf = sbt("zbuf", [128, 2 * CH], BF16)
    if ALIAS:
        w1f = w1q[:].bitcast(F32)
        w2f = w2q[:].bitcast(F32)
        prhs = w1f[0:65, 0:2 * CH]
        pevac = w2f[0:96, 0:2 * CH]
        sfp1 = w1f[:, 2 * CH:4 * CH] if 8 * HQ * 2 >= 4 * CH * 4 else None
        sfp2 = w2f[:, 2 * CH:4 * CH] if NHT * DM * 2 >= 4 * CH * 4 else None
        if sfp1 is None:
            sfp1 = sbt("sfp1", [128, 2 * CH])
        if sfp2 is None:
            sfp2 = sbt("sfp2", [128, 2 * CH])
    else:
        prhs = sbt("prhs", [65, 2 * CH])[:]
        pevac = sbt("pevac", [96, 2 * CH])[:]
        sfp1 = sbt("sfp1", [128, 2 * CH])[:]
        sfp2 = sbt("sfp2", [128, 2 * CH])[:]
    zblk = sbt("zblk", [128, 2 * 8 * 128], BF16)
    ztm = sbt("ztm", [128, 2 * DM], BF16)
    zsc = sbt("zsc", [128, DM], BF16)
    x1bf = sbt("x1bf", [128, 2 * DM], BF16)
    x1dm = sbt("x1dm", [128, 8 * 128], BF16)
    statb = sbt("statb", [128, 16])
    smv = sbt("smv", [128, 16])
    sct = sbt("sct", [8, 128])
    ptb = sbt("ptb", [128, 8], BF16)
    TALIAS = (N * CH * 2 >= 4 * FB * 4) and (8 * CH * 2 >= FB * 4 + 2 * NTAU * 4)
    if TALIAS:
        hbf = hb[:].bitcast(F32)
        hcf = hcb[:].bitcast(F32)
        pbis = hbf[:, 0:FB]
        bmask = hbf[:, FB:2 * FB]
        bmask2 = hbf[:, 2 * FB:3 * FB]
        cumb = hbf[:, 3 * FB:4 * FB]
        ucols = hcf[:, 0:max(FB, 2 * NTAU)]
    else:
        pbis = sbt("pbis", [128, FB])[:]
        bmask = sbt("bmask", [128, FB])[:]
        bmask2 = sbt("bmask2", [128, FB])[:]
        cumb = sbt("cumb", [128, FB])[:]
        ucols = sbt("ucols", [128, max(FB, 2 * NTAU)])[:]
    bcols = sbt("bcols", [128, 8])
    tk128 = sbt("tk128", [128, PD])
    g128 = sbt("g128", [128, PD])
    u32b = sbt("u32b", [128, PD], I32)
    pairb = sbt("pairb", [128, 2 * PD])
    posu = sbt("posu", [128, 2 * NTAU], I32)
    idsb = sbt("idsb", [128, NTAU], I32)
    gsb = sbt("gsb", [128, NTAU])
    xg = sbt("xg", [128, 2 * DM], BF16)
    xgt = sbt("xgt", [128, 2 * 8 * GQ * 128], BF16)
    hgel = sbt("hgel", [128, NHT * GQ * 128], BF16)
    gx = sbt("gx", [128, GQ * 128], BF16)
    gsc1 = sbt("gsc1", [128, GQ * 128], BF16)
    gsc2 = sbt("gsc2", [128, GQ * 128], BF16)
    wrow = sbt("wrow", [128, 4 * DM], BF16)
    addr8 = sbt("addr8", [128, 8 * NBLK], I32)
    gat1 = sbt("gat1", [128, 2 * DM], BF16)
    accr = sbt("accr", [128, DM])
    x1r = sbt("x1r", [128, DM], BF16)
    z2b = sbt("z2b", [128, DM])
    outb = sbt("outb", [128, DM])

    pp_brep = pst("pp_brep", [128, CH])
    pp_crep = pst("pp_crep", [128, CH])
    O1S = max(GQ * 128, 512)
    pp_o1 = pst("pp_o1", [128, 2 * O1S])
    pp_o2 = pst("pp_o2", [128, DM])
    pp_tr = pst("pp_tr", [128, 1024], BF16)
    pp_msc = pst("pp_msc", [128, CH])

    S = lambda name: Sem(nc, name)
    s_ld = S("s_ld")
    s_ldg = S("s_ldg")
    s_xc = [S("s_xc0"), S("s_xc1")]
    s_dc = [S("s_dc0"), S("s_dc1")]
    s_bc = [S("s_bc0"), S("s_bc1")]
    s_pe = S("s_pe")
    s_act = S("s_act")
    s_dve = S("s_dve")
    s_gp = S("s_gp")
    s_sy = S("s_sy")
    s_cc = S("s_cc")
    s_z = [S("s_z0"), S("s_z1")]
    s_w1 = S("s_w1")
    s_w2 = S("s_w2")
    s_ctb = [S("s_ctb0"), S("s_ctb1")]
    s_outd = S("s_outd")
    s_pv = [S("s_pv0"), S("s_pv1")]
    s_zb = [S("s_zb0"), S("s_zb1")]
    s_xg = [S("s_xg0"), S("s_xg1")]
    s_fg = [S("s_fg0"), S("s_fg1")]
    s_ty = [S("s_ty0"), S("s_ty1")] if cfg.taps else None
    s_ps = S("s_ps")

    def dma(eng, o, i_, sem):
        eng.dma_start(out=o, in_=i_).then_inc(sem.h, 16)
        sem.v += 16
        return sem.v

    def dma_acc(o, i_, sem):
        G.dma_start(out=o, in_=i_, accum_op=OP.add).then_inc(sem.h, 16)
        sem.v += 16
        return sem.v

    dma_acc2 = dma_acc

    def mark(eng_unused, instr, sem):
        instr.then_inc(sem.h, 1)
        sem.v += 1
        return sem.v

    def vv():
        V.wait_ge(s_dve.h, s_dve.v)

    def aa():
        A.wait_ge(s_act.h, s_act.v)

    def emit_rsqrt(vin, vout, scr):
        vi = vin.bitcast(I32)
        yi = vout.bitcast(I32)
        vv()
        mark(V, V.tensor_single_scalar(yi, vi, 1, OP.arith_shift_right), s_dve)
        vv()
        mark(V, V.tensor_single_scalar(yi, yi, -1, OP.mult), s_dve)
        vv()
        mark(V, V.tensor_single_scalar(yi, yi, 0x5F3759DF, OP.add), s_dve)
        for _ in range(3):
            vv()
            mark(V, V.tensor_mul(scr, vout, vout), s_dve)
            vv()
            mark(V, V.tensor_mul(scr, scr, vin), s_dve)
            vv()
            mark(V, V.tensor_scalar(scr, scr, -0.5, 1.5, OP.mult, OP.add), s_dve)
            vv()
            mark(V, V.tensor_mul(vout, vout, scr), s_dve)

    for (dst, src) in ((acol, a_sl), (dcol, d_sl), (id32, idn32), (tokm, tokmap),
                       (metab, meta), (projwb, projW), (wdtb, wdt_ext), (jvb, jvals)):
        dma(Y, dst[:], src, s_ld)
    dma(G, gwb[:].rearrange("p (dt e) -> p dt e", e=8),
        gate_w.rearrange("(dt p) e -> p dt e", p=128), s_ldg)
    for (dst, src) in ((onehb, onehots), (idb, idnbf)):
        dma(G, dst[:], src, s_ldg)
    for (dst, src) in ((gmb, gmask), (tmb, tmask), (tk128, tokmap128)):
        dma(Y, dst[:], src, s_ld)
    LD0 = s_ld.v
    LD0G = s_ldg.v

    V.wait_ge(s_ld.h, LD0)
    V.wait_ge(s_ldg.h, LD0G)
    vv()
    mark(V, V.memset(onesc[:], 1.0), s_dve)
    vv()
    mark(V, V.memset(zerobf[:], 0.0), s_dve)
    vv()
    mark(V, V.memset(zrow32[:], 0.0), s_dve)
    vv()
    mark(V, V.memset(prhs[:], 1.0), s_dve)
    vv()
    mark(V, V.memset(smv[:], 0.0), s_dve)
    G.wait_ge(s_dve.h, s_dve.v)
    for b in range(B):
        dma(G, contrib[b][0:128, :], zerobf[:], s_gp)
        dma(G, posdense[b], zrow32[:], s_gp)
        dma(G, idg[b], zrow32[:, 0:2 * NTAU], s_gp)
    ZIDG = s_gp.v
    G.wait_ge(s_gp.h, ZIDG)

    vv()
    mark(V, V.tensor_scalar(dcol[:], dcol[:], 1.0, None, OP.add), s_dve)
    A.wait_ge(s_ld.h, LD0)
    aa()
    mark(A, A.activation(acol[:], acol[:], AF.Exp), s_act)
    V.wait_ge(s_act.h, s_act.v)
    vv()
    mark(V, V.tensor_scalar_mul(acol[:], acol[:], -1.0), s_dve)
    ACOL_RDY = s_dve.v

    G.wait_ge(s_ld.h, LD0)
    G.wait_ge(s_ldg.h, LD0G)
    bc128 = G.to_reg(128 * NTAU - 1)
    bcpd = G.to_reg(128 * PD - 1)
    regs = {"c": nc.values_load(metab[0:1, 0:1], engines=[ET.Pool],
                                min_val=0, max_val=7,
                                skip_runtime_bounds_check=True)}

    PEV = [0, 0]
    PEX = [0, 0]
    PEVD = [0, 0]
    for b in range(B):
        for ci in range(NCH):
            p = ci % 2
            csl = slice(p * CH, (p + 1) * CH)
            if b > 0 or ci >= 2:
                Y.wait_ge(s_pe.h, PEX[p])
            t_x = dma(Y, xc[:, csl], xT[b, :, ci * CH:(ci + 1) * CH], s_xc[p])
            T.wait_ge(s_xc[p].h, t_x)
            T.wait_ge(s_ld.h, LD0)
            T.wait_ge(s_ldg.h, LD0G)
            if b > 0 or ci >= 2:
                T.wait_ge(s_dve.h, PEV[p])
            PEX[p] = mark(T, T.matmul(pp_msc[0:96, 0:CH], projwb[:], xc[:, csl],
                                      start=True, stop=True), s_pe)
            V.wait_ge(s_pe.h, s_pe.v)
            if b > 0 or ci >= 2:
                V.wait_ge(s_pv[p].h, PEVD[p])
            vv()
            PEV[p] = mark(V, V.tensor_copy(pevac[:, p * CH:(p + 1) * CH],
                                           pp_msc[0:96, 0:CH]), s_dve)
            Y.wait_ge(s_dve.h, PEV[p])
            PEVD[p] = dma(Y, projpart[b][:, ci * CH:(ci + 1) * CH],
                          pevac[:, p * CH:(p + 1) * CH], s_pv[p])
        G.wait_ge(s_pv[0].h, s_pv[0].v)
        G.wait_ge(s_pv[1].h, s_pv[1].v)
        G.collective_compute("AllReduce", OP.add, RG,
                             [projpart[b]], [projAR[b]]).then_inc(s_cc.h, 1)
        s_cc.v += 1
    AR_DONE = s_cc.v

    DPRE = [0, 0]
    DSF = [0, 0]
    PRH = [0, 0]
    Y.wait_ge(s_cc.h, AR_DONE)
    for b in range(B):
        for ci in range(NCH):
            p = ci % 2
            csl = slice(p * CH, (p + 1) * CH)
            if b > 0 or ci >= 2:
                Y.wait_ge(s_pe.h, PRH[p])
            t_p = dma(Y, prhs[0:64, csl], projAR[b][0:64, ci * CH:(ci + 1) * CH], s_dc[p])
            T.wait_ge(s_dc[p].h, t_p)
            T.wait_ge(s_cc.h, AR_DONE)
            if b > 0 or ci >= 2:
                T.wait_ge(s_act.h, DPRE[p])
            PRH[p] = mark(T, T.matmul(pp_msc[:, 0:CH], wdtb[:], prhs[:, csl],
                                      start=True, stop=True), s_pe)
            A.wait_ge(s_pe.h, s_pe.v)
            if b > 0 or ci >= 2:
                A.wait_ge(s_pv[p].h, DSF[p])
            aa()
            mark(A, A.activation(sfp1[:, csl], pp_msc[:, 0:CH], AF.Abs), s_act)
            A.wait_ge(s_act.h, s_act.v)
            aa()
            mark(A, A.activation(sfp1[:, csl], sfp1[:, csl], AF.Exp, scale=-1.0), s_act)
            A.wait_ge(s_act.h, s_act.v)
            aa()
            mark(A, A.activation(sfp1[:, csl], sfp1[:, csl], AF.Ln, bias=1.0), s_act)
            aa()
            DPRE[p] = mark(A, A.activation(sfp2[:, csl], pp_msc[:, 0:CH], AF.Relu), s_act)
            V.wait_ge(s_act.h, DPRE[p])
            vv()
            mark(V, V.tensor_add(sfp2[:, csl], sfp2[:, csl], sfp1[:, csl]), s_dve)
            Y.wait_ge(s_dve.h, s_dve.v)
            DSF[p] = dma(Y, deltaD[b][:, ci * CH:(ci + 1) * CH], sfp2[:, csl], s_pv[p])
            if cfg.taps:
                DSF[p] = dma(Y, tap["delta"][b, :, ci * CH:(ci + 1) * CH], sfp2[:, csl], s_pv[p])

    state = dict(XCF=[0, 0], DCF=[0, 0], BCF=[0, 0], ZBF=[0, 0], BRC=0,
                 TRT=[0, 0], ZTME=[0, 0], YT=[0, 0], X1O=[0, 0], PST=0,
                 POST=0, ZAGTOK={})

    def mamba_scan(b):
        G.wait_ge(s_cc.h, AR_DONE)
        Y.wait_ge(s_pv[0].h, s_pv[0].v)
        Y.wait_ge(s_pv[1].h, s_pv[1].v)
        vv()
        mark(V, V.memset(hb[:], 0.0), s_dve)
        for ci in range(NCH):
            p = ci % 2
            csl = slice(p * CH, (p + 1) * CH)
            if state["XCF"][p]:
                Y.wait_ge(s_dve.h, state["XCF"][p])
            t_x = dma(Y, xc[:, csl], xT[b, :, ci * CH:(ci + 1) * CH], s_xc[p])
            if state["DCF"][p]:
                Y.wait_ge(s_act.h, state["DCF"][p])
            t_d = dma(Y, dc[:, csl], deltaD[b][:, ci * CH:(ci + 1) * CH], s_dc[p])
            if state["BCF"][p]:
                G.wait_ge(s_pe.h, state["BCF"][p])
            dma(G, bccB[:, csl], projAR[b][64:80, ci * CH:(ci + 1) * CH], s_bc[p])
            t_b = dma(G, bccC[:, csl], projAR[b][80:96, ci * CH:(ci + 1) * CH], s_bc[p])
            V.wait_ge(s_xc[p].h, t_x)
            V.wait_ge(s_dc[p].h, t_d)
            vv()
            mark(V, V.tensor_mul(dxc[:, csl], dc[:, csl], xc[:, csl]), s_dve)
            DX_T = s_dve.v
            if cfg.taps and state["YT"][p]:
                V.wait_ge(s_ty[p].h, state["YT"][p])
            for half in range(2):
                for nn in range(8):
                    n = half * 8 + nn
                    npar = n % 2
                    nsl = slice(npar * CH, (npar + 1) * CH)
                    T.wait_ge(s_bc[p].h, t_b)
                    if state["BRC"]:
                        T.wait_ge(s_dve.h, state["BRC"])
                    mark(T, T.matmul(pp_brep[:, 0:CH], onehb[:, n * 128:(n + 1) * 128],
                                     bccB[:, csl], start=True, stop=True), s_pe)
                    PE_N = mark(T, T.matmul(pp_crep[:, 0:CH], onehb[:, n * 128:(n + 1) * 128],
                                            bccC[:, csl], start=True, stop=True), s_pe)
                    A.wait_ge(s_dc[p].h, t_d)
                    A.wait_ge(s_dve.h, ACOL_RDY)
                    if state["BRC"]:
                        A.wait_ge(s_dve.h, state["BRC"])
                    aa()
                    ACT_N = mark(A, A.activation(dab[:, nsl], dc[:, csl], AF.Exp,
                                                 scale=acol[:, n:n + 1]), s_act)
                    V.wait_ge(s_pe.h, PE_N)
                    vv()
                    mark(V, V.tensor_mul(dbxb[:, nsl], dxc[:, csl], pp_brep[:, 0:CH]), s_dve)
                    V.wait_ge(s_act.h, ACT_N)
                    vv()
                    mark(V, V.tensor_tensor_scan(
                        hb[:, n * CH:(n + 1) * CH], dab[:, nsl], dbxb[:, nsl],
                        hb[:, (n + 1) * CH - 1:(n + 1) * CH], OP.mult, OP.add), s_dve)
                    vv()
                    state["BRC"] = mark(V, V.tensor_mul(
                        hcb[:, nn * CH:(nn + 1) * CH],
                        hb[:, n * CH:(n + 1) * CH], pp_crep[:, 0:CH]), s_dve)
                for st in (4, 2, 1):
                    for kk in range(st):
                        vv()
                        mark(V, V.tensor_add(hcb[:, kk * CH:(kk + 1) * CH],
                                             hcb[:, kk * CH:(kk + 1) * CH],
                                             hcb[:, (kk + st) * CH:(kk + st + 1) * CH]), s_dve)
                if half == 0:
                    vv()
                    mark(V, V.tensor_copy(yhalf[:], hcb[:, 0:CH]), s_dve)
                else:
                    vv()
                    mark(V, V.tensor_add(yb[:, csl], yhalf[:], hcb[:, 0:CH]), s_dve)
            state["BCF"][p] = s_pe.v
            state["DCF"][p] = s_act.v
            if cfg.taps:
                Y.wait_ge(s_dve.h, s_dve.v)
                state["YT"][p] = dma(Y, tap["y"][b, :, ci * CH:(ci + 1) * CH],
                                     yb[:, csl], s_ty[p])
            if state["ZBF"][p]:
                V.wait_ge(s_z[p].h, state["ZBF"][p])
            vv()
            mark(V, V.scalar_tensor_tensor(zbuf[:, csl], xc[:, csl], dcol[:, 0:1],
                                           yb[:, csl], OP.mult, OP.add), s_dve)
            state["XCF"][p] = s_dve.v
            Y.wait_ge(s_dve.h, s_dve.v)
            for k in range(CH // 128):
                blk = ci * (CH // 128) + k
                dstcol = (blk % 8) * TOK + (blk // 8) * 128
                state["ZBF"][p] = dma(Y, zin[b][:, dstcol:dstcol + 128],
                                      zbuf[:, csl.start + k * 128:csl.start + (k + 1) * 128],
                                      s_z[p])
            yield
        G.wait_ge(s_z[0].h, s_z[0].v)
        G.wait_ge(s_z[1].h, s_z[1].v)
        G.collective_compute("AllGather", OP.bypass, RG,
                             [zin[b]], [zAG[b]]).then_inc(s_cc.h, 1)
        s_cc.v += 1
        state["ZAGTOK"][b] = s_cc.v

    def owner_ln1(b, zag_tok):
        G.wait_ge(s_cc.h, zag_tok)
        t_zown = dma(G, zownD[b], zAG[b][:, bass.ts(regs["c"], TOK)], s_gp)
        zsrc = zownD[b].rearrange("(dt p) t -> p dt t", p=128)
        G.wait_ge(s_gp.h, t_zown)
        for i in range(NBLK):
            p = i % 2
            sl1 = slice(p * DM, (p + 1) * DM)
            if state["TRT"][p]:
                G.wait_ge(s_pe.h, state["TRT"][p])
            t_z = dma(G, zblk[:, p * 1024:(p + 1) * 1024],
                      zsrc[:, :, i * 128:(i + 1) * 128], s_zb[p])
            T.wait_ge(s_zb[p].h, t_z)
            T.wait_ge(s_dve.h, s_dve.v)
            for dt in range(8):
                mark(T, T.transpose(pp_tr[:, dt * 128:(dt + 1) * 128],
                                    zblk[:, p * 1024 + dt * 128:p * 1024 + (dt + 1) * 128],
                                    idb[:]), s_pe)
            state["TRT"][p] = s_pe.v
            V.wait_ge(s_pe.h, s_pe.v)
            vv()
            mark(V, V.tensor_copy(ztm[:, sl1], pp_tr[:, 0:1024]), s_dve)
            A.wait_ge(s_dve.h, s_dve.v)
            aa()
            mark(A, A.activation(zsc[:], ztm[:, sl1], AF.Copy, accum_out=statb[:, 0:1]), s_act)
            aa()
            mark(A, A.activation(zsc[:], ztm[:, sl1], AF.Square, accum_out=statb[:, 1:2]), s_act)
            V.wait_ge(s_act.h, s_act.v)
            vv()
            mark(V, V.tensor_scalar_mul(statb[:, 2:3], statb[:, 0:1], 1.0 / DM), s_dve)
            vv()
            mark(V, V.tensor_mul(statb[:, 3:4], statb[:, 2:3], statb[:, 2:3]), s_dve)
            vv()
            mark(V, V.tensor_scalar(statb[:, 4:5], statb[:, 1:2], 1.0 / DM, EPS,
                                    OP.mult, OP.add), s_dve)
            vv()
            mark(V, V.tensor_sub(statb[:, 4:5], statb[:, 4:5], statb[:, 3:4]), s_dve)
            emit_rsqrt(statb[:, 4:5], statb[:, 5:6], statb[:, 6:7])
            vv()
            if state["X1O"][p]:
                V.wait_ge(s_pv[p].h, state["X1O"][p])
            mark(V, V.tensor_scalar(x1bf[:, sl1], ztm[:, sl1], statb[:, 2:3],
                                    statb[:, 5:6], OP.subtract, OP.mult), s_dve)
            X1_T = s_dve.v
            Y.wait_ge(s_dve.h, X1_T)
            state["X1O"][p] = dma(Y, x1own[b][i * 128:(i + 1) * 128, :], x1bf[:, sl1], s_pv[p])
            T.wait_ge(s_dve.h, X1_T)
            for dt in range(8):
                mark(T, T.transpose(pp_tr[:, dt * 128:(dt + 1) * 128],
                                    x1bf[:, p * DM + dt * 128:p * DM + (dt + 1) * 128],
                                    idb[:]), s_pe)
            V.wait_ge(s_pe.h, s_pe.v)
            vv()
            mark(V, V.tensor_copy(x1dm[:], pp_tr[:, 0:1024]), s_dve)
            T.wait_ge(s_dve.h, s_dve.v)
            for dt in range(8):
                mark(T, T.matmul(pp_msc[0:8, 0:128], gwb[:, dt * 8:(dt + 1) * 8],
                                 x1dm[:, dt * 128:(dt + 1) * 128],
                                 start=(dt == 0), stop=(dt == 7)), s_pe)
            V.wait_ge(s_pe.h, s_pe.v)
            vv()
            if state["PST"]:
                V.wait_ge(s_ps.h, state["PST"])
            mark(V, V.tensor_copy(sct[:], pp_msc[0:8, 0:128]), s_dve)
            T.wait_ge(s_dve.h, s_dve.v)
            mark(T, T.transpose(pp_msc[0:128, 128:136], sct[:], id32[0:8, 0:8]), s_pe)
            V.wait_ge(s_pe.h, s_pe.v)
            vv()
            mark(V, V.tensor_copy(smv[:, 0:8], pp_msc[0:128, 128:136]), s_dve)
            vv()
            mark(V, V.reduce_max(smv[:, 8:9], smv[:, 0:8], axis=mybir.AxisListType.X), s_dve)
            vv()
            mark(V, V.tensor_scalar_mul(smv[:, 9:10], smv[:, 8:9], -1.0), s_dve)
            A.wait_ge(s_dve.h, s_dve.v)
            aa()
            mark(A, A.activation(smv[:, 0:8], smv[:, 0:8], AF.Exp,
                                 bias=smv[:, 9:10], accum_out=smv[:, 10:11]), s_act)
            V.wait_ge(s_act.h, s_act.v)
            vv()
            mark(V, V.reciprocal(smv[:, 11:12], smv[:, 10:11]), s_dve)
            vv()
            mark(V, V.tensor_scalar(ptb[:, 0:8], smv[:, 0:8], smv[:, 11:12], None,
                                    OP.mult), s_dve)
            T.wait_ge(s_dve.h, s_dve.v)
            mark(T, T.transpose(pp_tr[0:8, 0:128], ptb[:, 0:8], idb[:]), s_pe)
            V.wait_ge(s_pe.h, s_pe.v)
            vv()
            mark(V, V.tensor_copy(sct[:], pp_tr[0:8, 0:128]), s_dve)
            Y.wait_ge(s_dve.h, s_dve.v)
            state["PST"] = dma(Y, probsT[b][:, i * 128:(i + 1) * 128], sct[:], s_ps)
        G.wait_ge(s_pv[0].h, s_pv[0].v)
        G.wait_ge(s_pv[1].h, s_pv[1].v)
        G.wait_ge(s_ps.h, s_ps.v)
        G.collective_compute("AllGather", OP.bypass, RG, [x1own[b]], [x1AG[b]]).then_inc(s_cc.h, 1)
        s_cc.v += 1
        G.collective_compute("AllGather", OP.bypass, RG, [probsT[b]], [probsAG[b]]).then_inc(s_cc.h, 1)
        s_cc.v += 1
        if cfg.taps:
            G.wait_ge(s_cc.h, s_cc.v)
            dma(G, tap["x1"][b], x1AG[b], s_gp)
            dma(G, tap["probs"][b], probsAG[b], s_gp)
            dma(G, tap["z"][b], zAG[b], s_gp)
            G.wait_ge(s_gp.h, s_gp.v)
        return s_cc.v

    def topk(b, ag_tok):
        G.wait_ge(s_cc.h, ag_tok)
        dma(G, probsAGx2[b][0:64, :], probsAG[b], s_gp)
        dma(G, probsAGx2[b][64:128, :], probsAG[b], s_gp)
        t_x2 = s_gp.v
        G.wait_ge(s_gp.h, t_x2)
        dma(G, probsRot[b], probsAGx2[b][bass.ds(regs["c"], 64), :], s_gp)
        G.wait_ge(s_gp.h, s_gp.v)
        srcv = probsRot[b].rearrange("(r g) (h f) -> g r h f", g=8, h=2)
        for g in range(8):
            dma(G, pbis[16 * g:16 * (g + 1), :], srcv[g:g + 1, :, :, :], s_gp)
        PB_T = s_gp.v
        V.wait_ge(s_gp.h, PB_T)
        vv()
        mark(V, V.memset(bcols[:, 0:1], 0.0), s_dve)
        vv()
        mark(V, V.memset(bcols[:, 1:2], 1.0), s_dve)
        for it in range(BISECT_ITERS):
            vv()
            mark(V, V.tensor_add(bcols[:, 2:3], bcols[:, 0:1], bcols[:, 1:2]), s_dve)
            vv()
            mark(V, V.tensor_scalar_mul(bcols[:, 2:3], bcols[:, 2:3], 0.5), s_dve)
            vv()
            mark(V, V.tensor_scalar(bmask[:], pbis[:], bcols[:, 2:3], 0.0,
                                    OP.is_gt, OP.add, accum_out=bcols[:, 3:4]), s_dve)
            T.wait_ge(s_dve.h, s_dve.v)
            mark(T, T.matmul(pp_msc[0:128, 0:1], gmb[:], bcols[:, 3:4],
                             start=True, stop=True), s_pe)
            V.wait_ge(s_pe.h, s_pe.v)
            vv()
            mark(V, V.tensor_copy(bcols[:, 4:5], pp_msc[0:128, 0:1]), s_dve)
            vv()
            mark(V, V.tensor_scalar(bmask2[:, 0:1], bcols[:, 4:5], float(K), None,
                                    OP.is_ge), s_dve)
            vv()
            mark(V, V.tensor_sub(bcols[:, 5:6], bcols[:, 2:3], bcols[:, 0:1]), s_dve)
            vv()
            mark(V, V.tensor_mul(bcols[:, 5:6], bcols[:, 5:6], bmask2[:, 0:1]), s_dve)
            vv()
            mark(V, V.tensor_add(bcols[:, 0:1], bcols[:, 0:1], bcols[:, 5:6]), s_dve)
            vv()
            mark(V, V.tensor_sub(bcols[:, 5:6], bcols[:, 1:2], bcols[:, 2:3]), s_dve)
            vv()
            mark(V, V.tensor_mul(bcols[:, 5:6], bcols[:, 5:6], bmask2[:, 0:1]), s_dve)
            vv()
            mark(V, V.tensor_add(bcols[:, 1:2], bcols[:, 2:3], bcols[:, 5:6]), s_dve)
        vv()
        mark(V, V.tensor_scalar(bmask[:], pbis[:], bcols[:, 1:2], 0.0,
                                OP.is_gt, OP.add, accum_out=bcols[:, 3:4]), s_dve)
        T.wait_ge(s_dve.h, s_dve.v)
        mark(T, T.matmul(pp_msc[0:128, 0:1], gmb[:], bcols[:, 3:4],
                         start=True, stop=True), s_pe)
        V.wait_ge(s_pe.h, s_pe.v)
        vv()
        mark(V, V.tensor_copy(bcols[:, 5:6], pp_msc[0:128, 0:1]), s_dve)
        vv()
        mark(V, V.tensor_scalar(bmask[:], pbis[:], bcols[:, 1:2], None, OP.is_gt), s_dve)
        vv()
        mark(V, V.tensor_scalar(bmask2[:], pbis[:], bcols[:, 0:1], None, OP.is_gt), s_dve)
        vv()
        mark(V, V.tensor_sub(bmask2[:], bmask2[:], bmask[:]), s_dve)
        vv()
        mark(V, V.tensor_tensor_scan(cumb[:], onesc[:, 0:FB], bmask2[:], 0.0,
                                     OP.mult, OP.add), s_dve)
        vv()
        mark(V, V.tensor_copy(bcols[:, 6:7], cumb[:, FB - 1:FB]), s_dve)
        T.wait_ge(s_dve.h, s_dve.v)
        mark(T, T.matmul(pp_msc[0:128, 0:1], tmb[:], bcols[:, 6:7],
                         start=True, stop=True), s_pe)
        V.wait_ge(s_pe.h, s_pe.v)
        vv()
        mark(V, V.tensor_copy(bcols[:, 7:8], pp_msc[0:128, 0:1]), s_dve)
        vv()
        mark(V, V.tensor_scalar(cumb[:], cumb[:], bcols[:, 7:8], -1.0, OP.add, OP.add), s_dve)
        vv()
        mark(V, V.tensor_scalar_mul(bcols[:, 6:7], bcols[:, 5:6], -1.0), s_dve)
        vv()
        mark(V, V.tensor_scalar(bcols[:, 6:7], bcols[:, 6:7], float(K), None, OP.add), s_dve)
        vv()
        mark(V, V.tensor_scalar(cumb[:], cumb[:], bcols[:, 6:7], None, OP.is_lt), s_dve)
        vv()
        mark(V, V.tensor_mul(cumb[:], cumb[:], bmask2[:]), s_dve)
        vv()
        mark(V, V.tensor_add(bmask[:], bmask[:], cumb[:]), s_dve)
        if cfg.taps:
            Y.wait_ge(s_dve.h, s_dve.v)
            dma(Y, tap["bis"][b], bcols[:], s_sy)
        vv()
        mark(V, V.tensor_tensor_scan(cumb[:], onesc[:, 0:FB], bmask[:], 0.0,
                                     OP.mult, OP.add), s_dve)
        vv()
        mark(V, V.tensor_copy(bcols[:, 6:7], cumb[:, FB - 1:FB]), s_dve)
        T.wait_ge(s_dve.h, s_dve.v)
        mark(T, T.matmul(pp_msc[0:128, 0:1], tmb[:], bcols[:, 6:7],
                         start=True, stop=True), s_pe)
        V.wait_ge(s_pe.h, s_pe.v)
        vv()
        mark(V, V.tensor_copy(bcols[:, 7:8], pp_msc[0:128, 0:1]), s_dve)
        vv()
        mark(V, V.tensor_scalar(cumb[:], cumb[:], bcols[:, 7:8], -1.0, OP.add, OP.add), s_dve)
        vv()
        mark(V, V.tensor_mul(ucols[:, 0:FB], cumb[:], bmask[:]), s_dve)
        vv()
        mark(V, V.tensor_scalar(bmask2[:], bmask[:], 0.5, 1e9, OP.is_lt, OP.mult), s_dve)
        vv()
        mark(V, V.tensor_add(ucols[:, 0:FB], ucols[:, 0:FB], bmask2[:]), s_dve)
        ui = ucols.bitcast(I32)
        ci2 = cumb.bitcast(I32)
        vv()
        mark(V, V.tensor_copy(ui[:, 0:FB], ucols[:, 0:FB]), s_dve)
        vv()
        mark(V, V.tensor_single_scalar(ci2[:, 0:FB], ui[:, 0:FB], 127, OP.bitwise_and), s_dve)
        vv()
        mark(V, V.tensor_single_scalar(ui[:, 0:FB], ui[:, 0:FB], 7, OP.arith_shift_right), s_dve)
        vv()
        mark(V, V.tensor_single_scalar(ci2[:, 0:FB], ci2[:, 0:FB], NTAU, OP.mult), s_dve)
        vv()
        mark(V, V.tensor_tensor(ui[:, 0:FB], ui[:, 0:FB], ci2[:, 0:FB], OP.add), s_dve)
        UP_T = s_dve.v
        Y.wait_ge(s_dve.h, UP_T)
        dma(Y, ubounce[b], ui[0:16, 0:FB], s_sy)
        dma(Y, gbounce[b], pbis[0:16, :], s_sy)
        Y.wait_ge(s_sy.h, s_sy.v)
        dma(Y, u32b[:], ubounce[b].rearrange("a (p c) -> (a p) c", p=8), s_sy)
        t_g128 = dma(Y, g128[:], gbounce[b].rearrange("a (p c) -> (a p) c", p=8), s_sy)
        V.wait_ge(s_sy.h, t_g128)
        V.wait_ge(s_ld.h, LD0)
        vv()
        mark(V, V.tensor_copy(pairb[:, 0:2 * PD:2], tk128[:]), s_dve)
        vv()
        mark(V, V.tensor_copy(pairb[:, 1:2 * PD:2], g128[:]), s_dve)
        t_rb = s_sy.v
        Y.wait_ge(s_dve.h, s_dve.v)
        G.wait_ge(s_sy.h, t_rb)
        G.wait_ge(s_dve.h, s_dve.v)
        G.wait_ge(s_gp.h, ZIDG)
        idgflat = idg[b].rearrange("p c -> (p c)").rearrange("(x o) -> x o", o=2)
        for q in range(PD):
            G.indirect_dma_start(
                out=idgflat, out_offset=IndirectOffsetOnAxis(u32b[:, q:q + 1], 0),
                in_=pairb[:, 2 * q:2 * q + 2], in_offset=None,
                bounds_check=bc128, oob_is_err=False,
            ).then_inc(s_gp.h, 16)
            s_gp.v += 16
        IDG_T = s_gp.v
        if cfg.taps:
            G.wait_ge(s_gp.h, IDG_T)
            dma(G, tap["idg"][b], idg[b], s_gp)
            G.wait_ge(s_gp.h, s_gp.v)
        Y.wait_ge(s_gp.h, IDG_T)
        t_ig = dma(Y, ucols[:, 0:2 * NTAU], idg[b], s_sy)
        V.wait_ge(s_sy.h, t_ig)
        vv()
        mark(V, V.tensor_copy(idsb[:, 0:NTAU], ucols[:, 0:2 * NTAU:2]), s_dve)
        vv()
        mark(V, V.tensor_copy(gsb[:, 0:NTAU], ucols[:, 1:2 * NTAU:2]), s_dve)
        return s_dve.v

    def ffn(b, ids_tok, aux=None, aux_every=1):
        step = 0
        O2C = [0]
        HGU = [0]
        WR_EV = [0, 0]
        XGU = [0, 0]
        XGP = [0, 0]
        GEL_T = [0, 0]
        O1_T = [0, 0]
        ctb_tok = [0] * (NTAU // 2)
        for hq in range(4):
            G.wait_ge(s_pe.h, s_pe.v)
            w1_tok = dma(G, w1q[:].rearrange("p (dt h) -> p dt h", dt=8),
                         w1e.rearrange("(dt p) h -> p dt h", p=128)
                         [:, :, hq * HQ:(hq + 1) * HQ], s_w1)
            w2_tok = dma(G, w2q[:].rearrange("p (a d) -> p a d", d=DM),
                         w2e.rearrange("(a p) d -> p a d", p=128)
                         [:, hq * NHT:(hq + 1) * NHT, :], s_w2)
            if cfg.taps and b == 0 and hq == 1:
                G.wait_ge(s_ctb[0].h, s_ctb[0].v)
                G.wait_ge(s_ctb[1].h, s_ctb[1].v)
                dma(G, tap["c0"][0], contrib[b], s_gp)
                G.wait_ge(s_gp.h, s_gp.v)
            if cfg.taps and b == 0 and hq == 0:
                G.wait_ge(s_w1.h, w1_tok)
                G.wait_ge(s_w2.h, w2_tok)
                dma(G, tap["w1q"][0][:, 0:8 * HQ], w1q[:], s_gp)
                dma(G, tap["w2q"][0][:, 0:NHT * DM], w2q[:], s_gp)
                G.wait_ge(s_gp.h, s_gp.v)
            for g in range(NG):
                if aux is not None and step % aux_every == 0:
                    next(aux, None)
                step += 1
                gp = g % 2
                for tt in range(GQ):
                    tau = g * GQ + tt
                    tp = tau % 2
                    G.wait_ge(s_dve.h, ids_tok)
                    if XGP[tp]:
                        G.wait_ge(s_pe.h, XGP[tp])
                    G.indirect_dma_start(
                        out=xg[:, tp * DM:(tp + 1) * DM], out_offset=None,
                        in_=x1AG[b],
                        in_offset=IndirectOffsetOnAxis(idsb[:, tau:tau + 1], 0),
                    ).then_inc(s_xg[tp].h, 16)
                    s_xg[tp].v += 16
                    t_xg = s_xg[tp].v
                    T.wait_ge(s_xg[tp].h, t_xg)
                    T.wait_ge(s_dve.h, s_dve.v)
                    for dt in range(8):
                        mark(T, T.transpose(pp_tr[:, dt * 128:(dt + 1) * 128],
                                            xg[:, tp * DM + dt * 128:tp * DM + (dt + 1) * 128],
                                            idb[:]), s_pe)
                    XGP[tp] = s_pe.v
                    V.wait_ge(s_pe.h, s_pe.v)
                    if XGU[gp]:
                        V.wait_ge(s_pe.h, XGU[gp])
                    for dt in range(8):
                        vv()
                        mark(V, V.tensor_copy(
                            xgt[:, (gp * 8 + dt) * GQ * 128 + tt * 128:
                                   (gp * 8 + dt) * GQ * 128 + (tt + 1) * 128],
                            pp_tr[:, dt * 128:(dt + 1) * 128]), s_dve)
                XGT_T = s_dve.v
                GEL_ALL = 0
                for ht in range(NHT):
                    hp = ht % 2
                    o1sl = slice(hp * O1S, hp * O1S + GQ * 128)
                    if ht == 0:
                        T.wait_ge(s_w1.h, w1_tok)
                        T.wait_ge(s_dve.h, XGT_T)
                    if GEL_T[hp]:
                        T.wait_ge(s_dve.h, GEL_T[hp])
                    for dt in range(8):
                        mark(T, T.matmul(pp_o1[:, o1sl],
                                         w1q[:, dt * HQ + ht * 128:dt * HQ + (ht + 1) * 128],
                                         xgt[:, (gp * 8 + dt) * GQ * 128:
                                                (gp * 8 + dt + 1) * GQ * 128],
                                         start=(dt == 0), stop=(dt == 7)), s_pe)
                    O1_T[hp] = s_pe.v
                    XGU[gp] = s_pe.v
                    V.wait_ge(s_pe.h, O1_T[hp])
                    vv()
                    mark(V, V.tensor_copy(gx[:], pp_o1[:, o1sl]), s_dve)
                    A.wait_ge(s_dve.h, s_dve.v)
                    aa()
                    mark(A, A.activation(gsc1[:], gx[:], AF.Square), s_act)
                    V.wait_ge(s_act.h, s_act.v)
                    vv()
                    mark(V, V.tensor_scalar(gsc1[:], gsc1[:], GK, SQ2PI, OP.mult, OP.add), s_dve)
                    vv()
                    mark(V, V.tensor_mul(gsc2[:], gx[:], gsc1[:]), s_dve)
                    A.wait_ge(s_dve.h, s_dve.v)
                    aa()
                    mark(A, A.activation(gsc2[:], gsc2[:], AF.Tanh), s_act)
                    V.wait_ge(s_act.h, s_act.v)
                    vv()
                    mark(V, V.tensor_scalar(gsc2[:], gsc2[:], 0.5, 0.5, OP.mult, OP.add), s_dve)
                    vv()
                    if HGU[0]:
                        V.wait_ge(s_pe.h, HGU[0])
                    GEL_T[hp] = mark(V, V.tensor_mul(hgel[:, ht * GQ * 128:(ht + 1) * GQ * 128],
                                                     gx[:], gsc2[:]), s_dve)
                    GEL_ALL = GEL_T[hp]
                if cfg.taps and b == 0 and hq == 0 and g == 0:
                    Y.wait_ge(s_dve.h, s_dve.v)
                    dma(Y, tap["xgt"][0][:, 0:8 * GQ * 128],
                        xgt[:, 0:8 * GQ * 128], s_sy)
                    dma(Y, tap["hgel"][0][:, 0:NHT * GQ * 128], hgel[:], s_sy)
                    Y.wait_ge(s_sy.h, s_sy.v)
                    V.wait_ge(s_sy.h, s_sy.v)
                for tt in range(GQ):
                    tau = g * GQ + tt
                    pr = (tau // 2) % 2
                    pi = tau % 2
                    for ht in range(NHT):
                        if ht == 0:
                            T.wait_ge(s_dve.h, GEL_ALL)
                            T.wait_ge(s_w2.h, w2_tok)
                            if O2C[0]:
                                T.wait_ge(s_dve.h, O2C[0])
                            if WR_EV[pr] and pi == 0:
                                T.wait_ge(s_ctb[pr].h, WR_EV[pr])
                        for h2 in range(2):
                            mark(T, T.matmul(pp_o2[:, h2 * 512:(h2 + 1) * 512],
                                             hgel[:, ht * GQ * 128 + tt * 128:
                                                    ht * GQ * 128 + (tt + 1) * 128],
                                             w2q[:, ht * DM + h2 * 512:ht * DM + (h2 + 1) * 512],
                                             start=(ht == 0), stop=(ht == NHT - 1)), s_pe)
                    HGU[0] = s_pe.v
                    V.wait_ge(s_pe.h, s_pe.v)
                    if WR_EV[pr] and pi == 0:
                        V.wait_ge(s_ctb[pr].h, WR_EV[pr])
                    vv()
                    O2C[0] = mark(V, V.tensor_copy(
                        wrow[:, (pr * 2 + pi) * DM:(pr * 2 + pi + 1) * DM],
                        pp_o2[:, 0:DM]), s_dve)
                    if cfg.taps and b == 0 and hq == 0 and g == 0 and tau == 0:
                        Y.wait_ge(s_dve.h, s_dve.v)
                        dma(Y, tap["wrow"][0][:, 0:DM], wrow[:, 0:DM], s_sy)
                        Y.wait_ge(s_sy.h, s_sy.v)
                        V.wait_ge(s_sy.h, s_sy.v)
                    if pi == 1:
                        G.wait_ge(s_dve.h, s_dve.v)
                        G.wait_ge(s_ctb[pr].h, s_ctb[pr].v)
                        rows = slice(128 + (tau - 1) * 128, 128 + (tau + 1) * 128)
                        srcv = wrow[:, pr * 2 * DM:(pr + 1) * 2 * DM].rearrange(
                            "p (t d) -> p t d", d=DM)
                        dstv = contrib[b][rows, :].rearrange("(t p) d -> p t d", p=128)
                        if hq == 0:
                            ctb_tok[tau // 2] = dma(G, dstv, srcv, s_ctb[pr])
                        else:
                            ctb_tok[tau // 2] = dma_acc(dstv, srcv, s_ctb[pr])
                        WR_EV[pr] = ctb_tok[tau // 2]
        GT_T = [0, 0]
        for pp2 in range(NTAU // 2):
            pr = pp2 % 2
            rows = slice(128 + pp2 * 256, 128 + (pp2 + 1) * 256)
            G.wait_ge(s_ctb[pr].h, s_ctb[pr].v)
            t_r = dma(G, wrow[:, pr * 2 * DM:(pr + 1) * 2 * DM].rearrange(
                "p (t d) -> p t d", d=DM),
                contrib[b][rows, :].rearrange("(t p) d -> p t d", p=128), s_gp)
            V.wait_ge(s_gp.h, t_r)
            for tt in range(2):
                tau = pp2 * 2 + tt
                vv()
                mark(V, V.tensor_scalar(
                    wrow[:, (pr * 2 + tt) * DM:(pr * 2 + tt + 1) * DM],
                    wrow[:, (pr * 2 + tt) * DM:(pr * 2 + tt + 1) * DM],
                    gsb[:, tau:tau + 1], None, OP.mult), s_dve)
            G.wait_ge(s_dve.h, s_dve.v)
            GT_T[pr] = dma(G, contrib[b][rows, :].rearrange("(t p) d -> p t d", p=128),
                           wrow[:, pr * 2 * DM:(pr + 1) * 2 * DM].rearrange(
                               "p (t d) -> p t d", d=DM), s_ctb[pr])
            ctb_tok[pp2] = GT_T[pr]
        uu = posu[:]
        if state["POST"]:
            V.wait_ge(s_gp.h, state["POST"])
        vv()
        mark(V, V.tensor_single_scalar(uu[:, 0:NTAU], idsb[:, 0:NTAU], 127, OP.bitwise_and), s_dve)
        vv()
        mark(V, V.tensor_single_scalar(uu[:, 0:NTAU], uu[:, 0:NTAU], PD, OP.mult), s_dve)
        vv()
        mark(V, V.tensor_single_scalar(uu[:, NTAU:2 * NTAU], idsb[:, 0:NTAU], 7,
                                       OP.arith_shift_right), s_dve)
        vv()
        mark(V, V.tensor_tensor(uu[:, 0:NTAU], uu[:, 0:NTAU], uu[:, NTAU:2 * NTAU],
                                OP.add), s_dve)
        PJ_T = s_dve.v
        G.wait_ge(s_dve.h, PJ_T)
        pdflat = posdense[b].rearrange("p c -> (p c)").rearrange("(x o) -> x o", o=1)
        for tau in range(NTAU):
            G.indirect_dma_start(
                out=pdflat, out_offset=IndirectOffsetOnAxis(uu[:, tau:tau + 1], 0),
                in_=jvb[:, tau:tau + 1], in_offset=None,
                bounds_check=bcpd, oob_is_err=False,
            ).then_inc(s_gp.h, 16)
            s_gp.v += 16
        POS_T = s_gp.v
        state["POST"] = POS_T
        G.wait_ge(s_ctb[0].h, s_ctb[0].v)
        G.wait_ge(s_ctb[1].h, s_ctb[1].v)
        G.collective_compute("AllGather", OP.bypass, RG,
                             [contrib[b]], [contribAG[b]]).then_inc(s_cc.h, 1)
        s_cc.v += 1
        G.wait_ge(s_gp.h, POS_T)
        G.collective_compute("AllGather", OP.bypass, RG,
                             [posdense[b]], [posAG[b]]).then_inc(s_cc.h, 1)
        s_cc.v += 1
        if cfg.taps:
            G.wait_ge(s_cc.h, s_cc.v)
            dma(G, tap["contrib"][b], contrib[b], s_gp)
            dma(G, tap["pos"][b], posAG[b], s_gp)
            G.wait_ge(s_gp.h, s_gp.v)
        return s_cc.v

    def finalize(b, cc_tok):
        Y.wait_ge(s_sy.h, s_sy.v)
        G.wait_ge(s_cc.h, cc_tok)
        t_pos = dma(G, posownD[b], posAG[b][:, bass.ts(regs["c"], NBLK)], s_gp)
        G.wait_ge(s_gp.h, t_pos)
        t_a8 = dma(G, addr8[:].rearrange("p (e n) -> p e n", n=NBLK),
                   posownD[b].rearrange("(e p) n -> p e n", p=128), s_gp)
        G.wait_ge(s_gp.h, t_a8)
        ADD_T = [0, 0]
        PRV = 0
        for i in range(NBLK):
            if PRV:
                Y.wait_ge(s_dve.h, PRV)
            t_x1 = dma(Y, x1r[:, 0:DM], x1own[b][i * 128:(i + 1) * 128, :], s_sy)
            for e in range(8):
                ip = e % 2
                sl = slice(ip * DM, (ip + 1) * DM)
                if ADD_T[ip]:
                    G.wait_ge(s_dve.h, ADD_T[ip])
                G.indirect_dma_start(
                    out=gat1[:, sl], out_offset=None,
                    in_=contribAG[b],
                    in_offset=IndirectOffsetOnAxis(addr8[:, e * NBLK + i:e * NBLK + i + 1], 0),
                ).then_inc(s_fg[ip].h, 16)
                s_fg[ip].v += 16
                V.wait_ge(s_fg[ip].h, s_fg[ip].v)
                if e == 0:
                    vv()
                    ADD_T[ip] = mark(V, V.tensor_copy(accr[:], gat1[:, sl]), s_dve)
                else:
                    vv()
                    ADD_T[ip] = mark(V, V.tensor_add(accr[:], accr[:], gat1[:, sl]), s_dve)
            V.wait_ge(s_sy.h, t_x1)
            V.wait_ge(s_act.h, s_act.v)
            V.wait_ge(s_outd.h, s_outd.v)
            vv()
            PRV = mark(V, V.tensor_add(z2b[:], x1r[:, 0:DM], accr[:]), s_dve)
            A.wait_ge(s_dve.h, s_dve.v)
            aa()
            mark(A, A.activation(zsc[:], z2b[:], AF.Copy, accum_out=statb[:, 0:1]), s_act)
            aa()
            mark(A, A.activation(zsc[:], z2b[:], AF.Square, accum_out=statb[:, 1:2]), s_act)
            V.wait_ge(s_act.h, s_act.v)
            vv()
            mark(V, V.tensor_scalar_mul(statb[:, 2:3], statb[:, 0:1], 1.0 / DM), s_dve)
            vv()
            mark(V, V.tensor_mul(statb[:, 3:4], statb[:, 2:3], statb[:, 2:3]), s_dve)
            vv()
            mark(V, V.tensor_scalar(statb[:, 4:5], statb[:, 1:2], 1.0 / DM, EPS,
                                    OP.mult, OP.add), s_dve)
            vv()
            mark(V, V.tensor_sub(statb[:, 4:5], statb[:, 4:5], statb[:, 3:4]), s_dve)
            emit_rsqrt(statb[:, 4:5], statb[:, 5:6], statb[:, 6:7])
            vv()
            mark(V, V.tensor_scalar(outb[:], z2b[:], statb[:, 2:3],
                                    statb[:, 5:6], OP.subtract, OP.mult), s_dve)
            Y.wait_ge(s_dve.h, s_dve.v)
            dma(Y, out[b, i], outb[:], s_outd)
            yield

    SA = cfg.stop_after
    if SA is not None:
        for b in range(B):
            for _ in mamba_scan(b):
                pass
            if SA == "mamba":
                continue
            agtok = owner_ln1(b, state["ZAGTOK"][b])
            if SA == "owner":
                continue
            itok = topk(b, agtok)
            if SA == "topk":
                continue
            cctok = ffn(b, itok)
            if SA == "ffn":
                continue
            for _ in finalize(b, cctok):
                pass
    else:
        g0 = mamba_scan(0)
        for _ in g0:
            pass
        a0 = owner_ln1(0, state["ZAGTOK"][0])
        i0 = topk(0, a0)
        g1 = mamba_scan(1)
        c0 = ffn(0, i0, aux=g1)
        for _ in g1:
            pass
        a1 = owner_ln1(1, state["ZAGTOK"][1])
        i1 = topk(1, a1)
        f0 = finalize(0, c0)
        c1 = ffn(1, i1, aux=f0, aux_every=2)
        for _ in f0:
            pass
        for _ in finalize(1, c1):
            pass

    if SA is not None:
        for e_, sm in ((V, s_dve), (A, s_act), (T, s_pe), (G, s_gp), (Y, s_sy)):
            e_.wait_ge(sm.h, sm.v)
        G.wait_ge(s_cc.h, s_cc.v)
        Y.wait_ge(s_dve.h, s_dve.v)
        dma(Y, out[0, 0], outb[:], s_outd)
    Y.wait_ge(s_outd.h, s_outd.v)
    ctx.close()
    return nc, tap




def make_inputs(cfg, x, A_log, W_delta, W_dt, b_dt, W_B, W_C, Dv, gate_w, w1, w2):
    x = np.asarray(x, np.float32)
    onehots = np.kron(np.eye(16, dtype=np.float32), np.ones((1, 128), np.float32))
    gm = np.zeros((128, 128), np.float32)
    for p in range(128):
        gm[(p // 16) * 16:(p // 16) * 16 + 16, p] = 1.0
    tm = np.zeros((128, 128), np.float32)
    for p2 in range(128):
        for p in range((p2 // 16) * 16, p2):
            tm[p, p2] = 1.0
    tokmap = np.zeros((16, cfg.FB), np.float32)
    for s in range(16):
        for f in range(cfg.FB):
            tokmap[s, f] = agrow_of_token(cfg, tok_of(cfg, s, f))
    tokmap128 = np.ascontiguousarray(
        tokmap.reshape(16, 8, cfg.FB // 8).transpose(0, 1, 2).reshape(128, cfg.FB // 8))
    maps = []
    for c in range(8):
        jv = np.zeros((128, cfg.NTAU), np.int32)
        for tau in range(cfg.NTAU):
            for p in range(128):
                jv[p, tau] = cfg.CROW * c + 128 + 128 * tau + p
        sl = slice(128 * c, 128 * (c + 1))
        meta = np.zeros((1, 16), np.int32)
        meta[0, 0] = c
        for g in range(8):
            meta[0, 1 + g] = (g + c) % 8
        maps.append(dict(
            xT=np.ascontiguousarray(x[:, :, sl].transpose(0, 2, 1)),
            projW=np.ascontiguousarray(np.concatenate(
                [np.asarray(W_delta, np.float32)[sl],
                 np.asarray(W_B, np.float32)[sl],
                 np.asarray(W_C, np.float32)[sl]], axis=1)),
            wdt_ext=np.ascontiguousarray(np.concatenate(
                [np.asarray(W_dt, np.float32)[:, sl],
                 np.asarray(b_dt, np.float32)[None, sl]], axis=0)),
            a_sl=np.ascontiguousarray(np.asarray(A_log, np.float32)[sl]),
            d_sl=np.ascontiguousarray(np.asarray(Dv, np.float32)[sl, None]),
            gate_w=np.asarray(gate_w, np.float32),
            w1e=np.ascontiguousarray(np.asarray(w1, np.float32)[c]),
            w2e=np.ascontiguousarray(np.asarray(w2, np.float32)[c]),
            idn32=np.eye(128, dtype=np.float32),
            idnbf=np.eye(128, dtype=BF),
            onehots=onehots, gmask=gm, tmask=tm, tokmap=tokmap,
            tokmap128=tokmap128, jvals=jv,
            meta=meta,
        ))
    return maps


def assemble_out(cfg, outs):
    full = np.zeros((cfg.B, cfg.L, cfg.DM), np.float32)
    for c in range(8):
        o = outs[c]
        for i in range(cfg.NBLK):
            blk = 8 * i + c
            full[:, blk * 128:(blk + 1) * 128, :] = o[:, i]
    return full


def kernel(L0_attn_x, L0_attn_A_log, L0_attn_W_delta, L0_attn_W_dt, L0_attn_b_dt,
           L0_attn_W_B, L0_attn_W_C, L0_attn_D, L0_ffn_gate_w, L0_ffn_w1, L0_ffn_w2):
    cfg = Cfg()
    nc, _ = build_program(cfg)
    in_maps = make_inputs(cfg, L0_attn_x, L0_attn_A_log, L0_attn_W_delta, L0_attn_W_dt,
                          L0_attn_b_dt, L0_attn_W_B, L0_attn_W_C, L0_attn_D,
                          L0_ffn_gate_w, L0_ffn_w1, L0_ffn_w2)
    res = run_bass_kernel_spmd(nc, in_maps, core_ids=list(range(8)))
    return assemble_out(cfg, [r["out"] for r in res.results])


# version 08
# speedup vs baseline: 1.0059x; 1.0059x over previous
import numpy as np
import ml_dtypes
import concourse.bass as bass
import concourse.mybir as mybir
from concourse.bass_core_rust import IndirectOffsetOnAxis
from concourse.bass_utils import run_bass_kernel_spmd
from contextlib import ExitStack

F32 = mybir.dt.float32
BF16 = mybir.dt.bfloat16
I32 = mybir.dt.int32
AF = mybir.ActivationFunctionType
OP = mybir.AluOpType
ET = mybir.EngineType
BF = ml_dtypes.bfloat16

SQ2PI = 0.7978845608028654
GK = 0.044715 * SQ2PI
EPS = 1e-5
BISECT_ITERS = 36


class Cfg:
    def __init__(self, L=16384, HID=4096, CH=512, taps=False, stop_after=None):
        self.stop_after = stop_after
        self.B = 2
        self.L = L
        self.DM = 1024
        self.N = 16
        self.HID = HID
        self.HQ = HID // 4
        self.NHT = self.HQ // 128
        self.K = L // 4
        self.TOK = L // 8
        self.CH = CH
        self.NCH = L // CH
        self.NBLK = self.TOK // 128
        self.FB = self.TOK // 2
        self.PD = L // 128
        self.NTAU = self.K // 128
        self.GQ = 4 if self.NTAU % 8 == 0 else (2 if self.NTAU % 4 == 0 else 1)
        self.NG = self.NTAU // self.GQ
        self.CROW = self.K + 128
        self.QAG = max(1, self.NCH // 4)
        self.taps = taps
        assert self.NBLK % 2 == 0 and self.FB >= 128 and self.CH <= 512


def tok_of(cfg, s, f):
    r = s >> 1
    i = (s & 1) * (cfg.NBLK // 2) + (f >> 7)
    j = f & 127
    return 128 * (8 * i + r) + j


def agrow_of_token(cfg, t):
    i, r, j = t // 1024 if False else (t // 128) // 8, (t // 128) % 8, t % 128
    return cfg.TOK * r + 128 * i + j


class Sem:
    def __init__(self, nc, name):
        self.h = nc.alloc_semaphore(name)
        self.v = 0


def build_program(cfg):
    nc = bass.Bass("TRN2", target_bir_lowering=False, dynamic_dma_scratch_size=49152)
    B, L, DM, N = cfg.B, cfg.L, cfg.DM, cfg.N
    CH, NCH, K, TOK, NBLK = cfg.CH, cfg.NCH, cfg.K, cfg.TOK, cfg.NBLK
    HID, HQ, NHT, NTAU = cfg.HID, cfg.HQ, cfg.NHT, cfg.NTAU
    GQ, NG, CROW, QAG = cfg.GQ, cfg.NG, cfg.CROW, cfg.QAG
    FB, PD = cfg.FB, cfg.PD
    RG = [list(range(8))]
    AGTOT = 8 * TOK

    def ein(name, shape, dt=F32):
        return nc.dram_tensor(name, shape, dt, kind="ExternalInput").ap()

    xT = ein("xT", [B, 128, L])
    projW = ein("projW", [128, 96])
    wdt_ext = ein("wdt_ext", [65, 128])
    a_sl = ein("a_sl", [128, N])
    d_sl = ein("d_sl", [128, 1])
    gate_w = ein("gate_w", [DM, 8])
    w1e = ein("w1e", [DM, HID])
    w2e = ein("w2e", [HID, DM])
    idn32 = ein("idn32", [128, 128])
    idnbf = ein("idnbf", [128, 128], BF16)
    onehots = ein("onehots", [16, 2048])
    gmask = ein("gmask", [128, 128])
    tmask = ein("tmask", [128, 128])
    tokmap = ein("tokmap", [16, FB])
    tokmap128 = ein("tokmap128", [128, PD])
    jvals = ein("jvals", [128, NTAU], I32)
    meta = ein("meta", [1, 16], I32)

    out = nc.dram_tensor("out", [B, NBLK, 128, DM], F32, kind="ExternalOutput").ap()

    def idram(name, shape, dt=F32):
        return nc.dram_tensor(name, shape, dt, kind="Internal").ap()

    projpart = [idram(f"projpart{b}", [96, L]) for b in range(B)]
    projAR = [idram(f"projAR{b}", [96, L]) for b in range(B)]
    deltaD = [idram(f"delta{b}", [128, L]) for b in range(B)]
    zin = [idram(f"zin{b}", [128, L], BF16) for b in range(B)]
    zAG = [idram(f"zAG{b}", [DM, L], BF16) for b in range(B)]
    x1own = [idram(f"x1own{b}", [TOK, DM], BF16) for b in range(B)]
    x1AG = [idram(f"x1AG{b}", [AGTOT, DM], BF16) for b in range(B)]
    probsT = [idram(f"probsT{b}", [8, TOK]) for b in range(B)]
    probsAG = [idram(f"probsAG{b}", [64, TOK]) for b in range(B)]
    ubounce = [idram(f"ubounce{b}", [16, FB], I32) for b in range(B)]
    gbounce = [idram(f"gbounce{b}", [16, FB]) for b in range(B)]
    idg = [idram(f"idg{b}", [128, 2 * NTAU]) for b in range(B)]
    posdense = [idram(f"posdense{b}", [128, PD], I32) for b in range(B)]
    posAG = [idram(f"posAG{b}", [1024, PD], I32) for b in range(B)]
    contrib = [idram(f"contrib{b}", [CROW, DM], BF16) for b in range(B)]
    contribAG = [idram(f"contribAG{b}", [8 * CROW, DM], BF16) for b in range(B)]
    zownD = [idram(f"zownD{b}", [DM, NBLK * 128], BF16) for b in range(B)]
    probsAGx2 = [idram(f"probsAGx2{b}", [128, TOK]) for b in range(B)]
    posownD = [idram(f"posownD{b}", [1024, NBLK], I32) for b in range(B)]
    probsRot = [idram(f"probsRot{b}", [64, TOK]) for b in range(B)]

    tap = {}
    if cfg.taps:
        def tout(name, shape, dt=F32):
            tap[name] = nc.dram_tensor("tap_" + name, shape, dt, kind="ExternalOutput").ap()
        tout("delta", [B, 128, L])
        tout("y", [B, 128, L])
        tout("z", [B, DM, L])
        tout("x1", [B, AGTOT, DM])
        tout("probs", [B, 64, TOK])
        tout("bis", [B, 128, 8])
        tout("idg", [B, 128, 2 * NTAU])
        tout("contrib", [B, CROW, DM])
        tout("xgt", [1, 128, 2 * 8 * 64 * 128], BF16)
        tout("c0", [1, CROW, DM], BF16)
        tout("w1q", [1, 128, 8 * 1024], BF16)
        tout("wrow", [1, 128, 4 * 1024], BF16)
        tout("w2q", [1, 128, 8 * 1024], BF16)
        tout("hgel", [1, 128, 64 * 64 * 128], BF16)
        tout("pos", [B, 1024, PD], I32)

    ctx = ExitStack()
    sbt = lambda name, shape, dt=F32: ctx.enter_context(nc.sbuf_tensor(name, shape, dt))
    pst = lambda name, shape, dt=F32: ctx.enter_context(nc.psum_tensor(name, shape, dt))
    V, A, T, G, Y = nc.vector, nc.scalar, nc.tensor, nc.gpsimd, nc.sync

    acol = sbt("acol", [128, N])
    dcol = sbt("dcol", [128, 1])
    idb = sbt("idb", [128, 128], BF16)
    id32 = sbt("id32", [128, 128])
    onehb = sbt("onehb", [16, 2048], BF16)
    gmb = sbt("gmb", [128, 128])
    tmb = sbt("tmb", [128, 128])
    tokm = sbt("tokm", [16, FB])
    jvb = sbt("jvb", [128, NTAU], I32)
    metab = sbt("metab", [1, 16], I32)
    gwb = sbt("gwb", [128, 8 * 8], BF16)
    projwb = sbt("projwb", [128, 96])
    wdtb = sbt("wdtb", [65, 128])
    onesc = sbt("onesc", [128, max(FB, 1024)], BF16)
    zerobf = sbt("zerobf", [128, DM], BF16)
    zrow32 = sbt("zrow32", [128, PD])
    w1q = sbt("w1q", [128, 8 * HQ], BF16)
    w2q = sbt("w2q", [128, NHT * DM], BF16)
    ALIAS = (8 * HQ * 2 >= 2 * CH * 4) and (NHT * DM * 2 >= 2 * CH * 4)
    xc = sbt("xc", [128, 2 * CH])
    dc = sbt("dc", [128, 2 * CH])
    dxc = sbt("dxc", [128, 2 * CH], BF16)
    bccB = sbt("bccB", [16, 2 * CH], BF16)
    bccC = sbt("bccC", [16, 2 * CH], BF16)
    dab = sbt("dab", [128, 2 * CH], BF16)
    dbxb = sbt("dbxb", [128, 2 * CH], BF16)
    hb = sbt("hb", [128, N * CH], BF16)
    hcb = sbt("hcb", [128, 8 * CH], BF16)
    yhalf = sbt("yhalf", [128, CH])
    yb = sbt("yb", [128, 2 * CH])
    zbuf = sbt("zbuf", [128, 2 * CH], BF16)
    if ALIAS:
        w1f = w1q[:].bitcast(F32)
        w2f = w2q[:].bitcast(F32)
        prhs = w1f[0:65, 0:2 * CH]
        pevac = w2f[0:96, 0:2 * CH]
        sfp1 = w1f[:, 2 * CH:4 * CH] if 8 * HQ * 2 >= 4 * CH * 4 else None
        sfp2 = w2f[:, 2 * CH:4 * CH] if NHT * DM * 2 >= 4 * CH * 4 else None
        if sfp1 is None:
            sfp1 = sbt("sfp1", [128, 2 * CH])
        if sfp2 is None:
            sfp2 = sbt("sfp2", [128, 2 * CH])
    else:
        prhs = sbt("prhs", [65, 2 * CH])[:]
        pevac = sbt("pevac", [96, 2 * CH])[:]
        sfp1 = sbt("sfp1", [128, 2 * CH])[:]
        sfp2 = sbt("sfp2", [128, 2 * CH])[:]
    zblk = sbt("zblk", [128, 2 * 8 * 128], BF16)
    ztm = sbt("ztm", [128, 2 * DM], BF16)
    zsc = sbt("zsc", [128, DM], BF16)
    x1bf = sbt("x1bf", [128, 2 * DM], BF16)
    x1dm = sbt("x1dm", [128, 8 * 128], BF16)
    statb = sbt("statb", [128, 16])
    smv = sbt("smv", [128, 16])
    sct = sbt("sct", [8, 128])
    ptb = sbt("ptb", [128, 8], BF16)
    TALIAS = (N * CH * 2 >= 4 * FB * 4) and (8 * CH * 2 >= FB * 4 + 2 * NTAU * 4)
    if TALIAS:
        hbf = hb[:].bitcast(F32)
        hcf = hcb[:].bitcast(F32)
        pbis = hbf[:, 0:FB]
        bmask = hbf[:, FB:2 * FB]
        bmask2 = hbf[:, 2 * FB:3 * FB]
        cumb = hbf[:, 3 * FB:4 * FB]
        ucols = hcf[:, 0:max(FB, 2 * NTAU)]
    else:
        pbis = sbt("pbis", [128, FB])[:]
        bmask = sbt("bmask", [128, FB])[:]
        bmask2 = sbt("bmask2", [128, FB])[:]
        cumb = sbt("cumb", [128, FB])[:]
        ucols = sbt("ucols", [128, max(FB, 2 * NTAU)])[:]
    bcols = sbt("bcols", [128, 8])
    tk128 = sbt("tk128", [128, PD])
    g128 = sbt("g128", [128, PD])
    u32b = sbt("u32b", [128, PD], I32)
    pairb = sbt("pairb", [128, 2 * PD])
    posu = sbt("posu", [128, 2 * NTAU], I32)
    idsb = sbt("idsb", [128, NTAU], I32)
    gsb = sbt("gsb", [128, NTAU])
    xg = sbt("xg", [128, 2 * DM], BF16)
    xgt = sbt("xgt", [128, 2 * 8 * GQ * 128], BF16)
    hgel = sbt("hgel", [128, NHT * GQ * 128], BF16)
    gx = sbt("gx", [128, GQ * 128], BF16)
    gsc1 = sbt("gsc1", [128, GQ * 128], BF16)
    gsc2 = sbt("gsc2", [128, GQ * 128], BF16)
    wrow = sbt("wrow", [128, 4 * DM], BF16)
    addr8 = sbt("addr8", [128, 8 * NBLK], I32)
    gat1 = sbt("gat1", [128, 2 * DM], BF16)
    accr = sbt("accr", [128, DM])
    x1r = sbt("x1r", [128, DM], BF16)
    z2b = sbt("z2b", [128, DM])
    outb = sbt("outb", [128, DM])

    pp_brep = pst("pp_brep", [128, CH])
    pp_crep = pst("pp_crep", [128, CH])
    O1S = max(GQ * 128, 512)
    pp_o1 = pst("pp_o1", [128, 2 * O1S])
    pp_o2 = pst("pp_o2", [128, DM])
    pp_tr = pst("pp_tr", [128, 1024], BF16)
    pp_msc = pst("pp_msc", [128, CH])

    S = lambda name: Sem(nc, name)
    s_ld = S("s_ld")
    s_ldg = S("s_ldg")
    s_xc = [S("s_xc0"), S("s_xc1")]
    s_dc = [S("s_dc0"), S("s_dc1")]
    s_bc = [S("s_bc0"), S("s_bc1")]
    s_pe = S("s_pe")
    s_act = S("s_act")
    s_dve = S("s_dve")
    s_gp = S("s_gp")
    s_sy = S("s_sy")
    s_cc = S("s_cc")
    s_z = [S("s_z0"), S("s_z1")]
    s_w1 = S("s_w1")
    s_w2 = S("s_w2")
    s_ctb = [S("s_ctb0"), S("s_ctb1")]
    s_outd = S("s_outd")
    s_pv = [S("s_pv0"), S("s_pv1")]
    s_zb = [S("s_zb0"), S("s_zb1")]
    s_xg = [S("s_xg0"), S("s_xg1")]
    s_fg = [S("s_fg0"), S("s_fg1")]
    s_ty = [S("s_ty0"), S("s_ty1")] if cfg.taps else None
    s_ps = S("s_ps")

    def dma(eng, o, i_, sem):
        eng.dma_start(out=o, in_=i_).then_inc(sem.h, 16)
        sem.v += 16
        return sem.v

    def dma_acc(o, i_, sem):
        G.dma_start(out=o, in_=i_, accum_op=OP.add).then_inc(sem.h, 16)
        sem.v += 16
        return sem.v

    dma_acc2 = dma_acc

    def mark(eng_unused, instr, sem):
        instr.then_inc(sem.h, 1)
        sem.v += 1
        return sem.v

    def vv():
        V.wait_ge(s_dve.h, s_dve.v)

    def aa():
        A.wait_ge(s_act.h, s_act.v)

    def emit_rsqrt(vin, vout, scr):
        vi = vin.bitcast(I32)
        yi = vout.bitcast(I32)
        vv()
        mark(V, V.tensor_single_scalar(yi, vi, 1, OP.arith_shift_right), s_dve)
        vv()
        mark(V, V.tensor_single_scalar(yi, yi, -1, OP.mult), s_dve)
        vv()
        mark(V, V.tensor_single_scalar(yi, yi, 0x5F3759DF, OP.add), s_dve)
        for _ in range(3):
            vv()
            mark(V, V.tensor_mul(scr, vout, vout), s_dve)
            vv()
            mark(V, V.tensor_mul(scr, scr, vin), s_dve)
            vv()
            mark(V, V.tensor_scalar(scr, scr, -0.5, 1.5, OP.mult, OP.add), s_dve)
            vv()
            mark(V, V.tensor_mul(vout, vout, scr), s_dve)

    for (dst, src) in ((acol, a_sl), (dcol, d_sl), (id32, idn32), (tokm, tokmap),
                       (metab, meta), (projwb, projW), (wdtb, wdt_ext), (jvb, jvals)):
        dma(Y, dst[:], src, s_ld)
    dma(G, gwb[:].rearrange("p (dt e) -> p dt e", e=8),
        gate_w.rearrange("(dt p) e -> p dt e", p=128), s_ldg)
    for (dst, src) in ((onehb, onehots), (idb, idnbf)):
        dma(G, dst[:], src, s_ldg)
    for (dst, src) in ((gmb, gmask), (tmb, tmask), (tk128, tokmap128)):
        dma(Y, dst[:], src, s_ld)
    LD0 = s_ld.v
    LD0G = s_ldg.v

    V.wait_ge(s_ld.h, LD0)
    V.wait_ge(s_ldg.h, LD0G)
    vv()
    mark(V, V.memset(onesc[:], 1.0), s_dve)
    vv()
    mark(V, V.memset(zerobf[:], 0.0), s_dve)
    vv()
    mark(V, V.memset(zrow32[:], 0.0), s_dve)
    vv()
    mark(V, V.memset(prhs[:], 1.0), s_dve)
    vv()
    mark(V, V.memset(smv[:], 0.0), s_dve)
    G.wait_ge(s_dve.h, s_dve.v)
    for b in range(B):
        dma(G, contrib[b][0:128, :], zerobf[:], s_gp)
        dma(G, posdense[b], zrow32[:], s_gp)
        dma(G, idg[b], zrow32[:, 0:2 * NTAU], s_gp)
    ZIDG = s_gp.v
    G.wait_ge(s_gp.h, ZIDG)

    vv()
    mark(V, V.tensor_scalar(dcol[:], dcol[:], 1.0, None, OP.add), s_dve)
    A.wait_ge(s_ld.h, LD0)
    aa()
    mark(A, A.activation(acol[:], acol[:], AF.Exp), s_act)
    V.wait_ge(s_act.h, s_act.v)
    vv()
    mark(V, V.tensor_scalar_mul(acol[:], acol[:], -1.0), s_dve)
    ACOL_RDY = s_dve.v

    G.wait_ge(s_ld.h, LD0)
    G.wait_ge(s_ldg.h, LD0G)
    bc128 = G.to_reg(128 * NTAU - 1)
    bcpd = G.to_reg(128 * PD - 1)
    regs = {"c": nc.values_load(metab[0:1, 0:1], engines=[ET.Pool],
                                min_val=0, max_val=7,
                                skip_runtime_bounds_check=True)}

    PEV = [0, 0]
    PEX = [0, 0]
    PEVD = [0, 0]
    for b in range(B):
        for ci in range(NCH):
            p = ci % 2
            csl = slice(p * CH, (p + 1) * CH)
            if b > 0 or ci >= 2:
                Y.wait_ge(s_pe.h, PEX[p])
            t_x = dma(Y, xc[:, csl], xT[b, :, ci * CH:(ci + 1) * CH], s_xc[p])
            T.wait_ge(s_xc[p].h, t_x)
            T.wait_ge(s_ld.h, LD0)
            T.wait_ge(s_ldg.h, LD0G)
            if b > 0 or ci >= 2:
                T.wait_ge(s_dve.h, PEV[p])
            PEX[p] = mark(T, T.matmul(pp_msc[0:96, 0:CH], projwb[:], xc[:, csl],
                                      start=True, stop=True), s_pe)
            V.wait_ge(s_pe.h, s_pe.v)
            if b > 0 or ci >= 2:
                V.wait_ge(s_pv[p].h, PEVD[p])
            vv()
            PEV[p] = mark(V, V.tensor_copy(pevac[:, p * CH:(p + 1) * CH],
                                           pp_msc[0:96, 0:CH]), s_dve)
            Y.wait_ge(s_dve.h, PEV[p])
            PEVD[p] = dma(Y, projpart[b][:, ci * CH:(ci + 1) * CH],
                          pevac[:, p * CH:(p + 1) * CH], s_pv[p])
        G.wait_ge(s_pv[0].h, s_pv[0].v)
        G.wait_ge(s_pv[1].h, s_pv[1].v)
        G.collective_compute("AllReduce", OP.add, RG,
                             [projpart[b]], [projAR[b]]).then_inc(s_cc.h, 1)
        s_cc.v += 1
    AR_DONE = s_cc.v

    DPRE = [0, 0]
    DSF = [0, 0]
    PRH = [0, 0]
    Y.wait_ge(s_cc.h, AR_DONE)
    for b in range(B):
        for ci in range(NCH):
            p = ci % 2
            csl = slice(p * CH, (p + 1) * CH)
            if b > 0 or ci >= 2:
                Y.wait_ge(s_pe.h, PRH[p])
            t_p = dma(Y, prhs[0:64, csl], projAR[b][0:64, ci * CH:(ci + 1) * CH], s_dc[p])
            T.wait_ge(s_dc[p].h, t_p)
            T.wait_ge(s_cc.h, AR_DONE)
            if b > 0 or ci >= 2:
                T.wait_ge(s_act.h, DPRE[p])
            PRH[p] = mark(T, T.matmul(pp_msc[:, 0:CH], wdtb[:], prhs[:, csl],
                                      start=True, stop=True), s_pe)
            A.wait_ge(s_pe.h, s_pe.v)
            if b > 0 or ci >= 2:
                A.wait_ge(s_pv[p].h, DSF[p])
            aa()
            mark(A, A.activation(sfp1[:, csl], pp_msc[:, 0:CH], AF.Abs), s_act)
            A.wait_ge(s_act.h, s_act.v)
            aa()
            mark(A, A.activation(sfp1[:, csl], sfp1[:, csl], AF.Exp, scale=-1.0), s_act)
            A.wait_ge(s_act.h, s_act.v)
            aa()
            mark(A, A.activation(sfp1[:, csl], sfp1[:, csl], AF.Ln, bias=1.0), s_act)
            aa()
            DPRE[p] = mark(A, A.activation(sfp2[:, csl], pp_msc[:, 0:CH], AF.Relu), s_act)
            V.wait_ge(s_act.h, DPRE[p])
            vv()
            mark(V, V.tensor_add(sfp2[:, csl], sfp2[:, csl], sfp1[:, csl]), s_dve)
            Y.wait_ge(s_dve.h, s_dve.v)
            DSF[p] = dma(Y, deltaD[b][:, ci * CH:(ci + 1) * CH], sfp2[:, csl], s_pv[p])
            if cfg.taps:
                DSF[p] = dma(Y, tap["delta"][b, :, ci * CH:(ci + 1) * CH], sfp2[:, csl], s_pv[p])

    state = dict(XCF=[0, 0], DCF=[0, 0], BCF=[0, 0], ZBF=[0, 0], BRC=0,
                 TRT=[0, 0], ZTME=[0, 0], YT=[0, 0], X1O=[0, 0], PST=0,
                 POST=0, ZAGTOK={})

    def mamba_scan(b):
        G.wait_ge(s_cc.h, AR_DONE)
        Y.wait_ge(s_pv[0].h, s_pv[0].v)
        Y.wait_ge(s_pv[1].h, s_pv[1].v)
        vv()
        mark(V, V.memset(hb[:], 0.0), s_dve)
        for ci in range(NCH):
            p = ci % 2
            csl = slice(p * CH, (p + 1) * CH)
            if state["XCF"][p]:
                Y.wait_ge(s_dve.h, state["XCF"][p])
            t_x = dma(Y, xc[:, csl], xT[b, :, ci * CH:(ci + 1) * CH], s_xc[p])
            if state["DCF"][p]:
                Y.wait_ge(s_act.h, state["DCF"][p])
            t_d = dma(Y, dc[:, csl], deltaD[b][:, ci * CH:(ci + 1) * CH], s_dc[p])
            if state["BCF"][p]:
                G.wait_ge(s_pe.h, state["BCF"][p])
            dma(G, bccB[:, csl], projAR[b][64:80, ci * CH:(ci + 1) * CH], s_bc[p])
            t_b = dma(G, bccC[:, csl], projAR[b][80:96, ci * CH:(ci + 1) * CH], s_bc[p])
            V.wait_ge(s_xc[p].h, t_x)
            V.wait_ge(s_dc[p].h, t_d)
            vv()
            mark(V, V.tensor_mul(dxc[:, csl], dc[:, csl], xc[:, csl]), s_dve)
            DX_T = s_dve.v
            if cfg.taps and state["YT"][p]:
                V.wait_ge(s_ty[p].h, state["YT"][p])
            for half in range(2):
                for nn in range(8):
                    n = half * 8 + nn
                    npar = n % 2
                    nsl = slice(npar * CH, (npar + 1) * CH)
                    T.wait_ge(s_bc[p].h, t_b)
                    if state["BRC"]:
                        T.wait_ge(s_dve.h, state["BRC"])
                    mark(T, T.matmul(pp_brep[:, 0:CH], onehb[:, n * 128:(n + 1) * 128],
                                     bccB[:, csl], start=True, stop=True), s_pe)
                    PE_N = mark(T, T.matmul(pp_crep[:, 0:CH], onehb[:, n * 128:(n + 1) * 128],
                                            bccC[:, csl], start=True, stop=True), s_pe)
                    A.wait_ge(s_dc[p].h, t_d)
                    A.wait_ge(s_dve.h, ACOL_RDY)
                    if state["BRC"]:
                        A.wait_ge(s_dve.h, state["BRC"])
                    aa()
                    ACT_N = mark(A, A.activation(dab[:, nsl], dc[:, csl], AF.Exp,
                                                 scale=acol[:, n:n + 1]), s_act)
                    V.wait_ge(s_pe.h, PE_N)
                    vv()
                    mark(V, V.tensor_mul(dbxb[:, nsl], dxc[:, csl], pp_brep[:, 0:CH]), s_dve)
                    V.wait_ge(s_act.h, ACT_N)
                    vv()
                    mark(V, V.tensor_tensor_scan(
                        hb[:, n * CH:(n + 1) * CH], dab[:, nsl], dbxb[:, nsl],
                        hb[:, (n + 1) * CH - 1:(n + 1) * CH], OP.mult, OP.add), s_dve)
                    vv()
                    state["BRC"] = mark(V, V.tensor_mul(
                        hcb[:, nn * CH:(nn + 1) * CH],
                        hb[:, n * CH:(n + 1) * CH], pp_crep[:, 0:CH]), s_dve)
                for st in (4, 2, 1):
                    for kk in range(st):
                        vv()
                        mark(V, V.tensor_add(hcb[:, kk * CH:(kk + 1) * CH],
                                             hcb[:, kk * CH:(kk + 1) * CH],
                                             hcb[:, (kk + st) * CH:(kk + st + 1) * CH]), s_dve)
                if half == 0:
                    vv()
                    mark(V, V.tensor_copy(yhalf[:], hcb[:, 0:CH]), s_dve)
                else:
                    vv()
                    mark(V, V.tensor_add(yb[:, csl], yhalf[:], hcb[:, 0:CH]), s_dve)
            state["BCF"][p] = s_pe.v
            state["DCF"][p] = s_act.v
            if cfg.taps:
                Y.wait_ge(s_dve.h, s_dve.v)
                state["YT"][p] = dma(Y, tap["y"][b, :, ci * CH:(ci + 1) * CH],
                                     yb[:, csl], s_ty[p])
            if state["ZBF"][p]:
                V.wait_ge(s_z[p].h, state["ZBF"][p])
            vv()
            mark(V, V.scalar_tensor_tensor(zbuf[:, csl], xc[:, csl], dcol[:, 0:1],
                                           yb[:, csl], OP.mult, OP.add), s_dve)
            state["XCF"][p] = s_dve.v
            Y.wait_ge(s_dve.h, s_dve.v)
            for k in range(CH // 128):
                blk = ci * (CH // 128) + k
                dstcol = (blk % 8) * TOK + (blk // 8) * 128
                state["ZBF"][p] = dma(Y, zin[b][:, dstcol:dstcol + 128],
                                      zbuf[:, csl.start + k * 128:csl.start + (k + 1) * 128],
                                      s_z[p])
            yield
        G.wait_ge(s_z[0].h, s_z[0].v)
        G.wait_ge(s_z[1].h, s_z[1].v)
        G.collective_compute("AllGather", OP.bypass, RG,
                             [zin[b]], [zAG[b]]).then_inc(s_cc.h, 1)
        s_cc.v += 1
        state["ZAGTOK"][b] = s_cc.v

    def owner_ln1(b, zag_tok):
        G.wait_ge(s_cc.h, zag_tok)
        t_zown = dma(G, zownD[b], zAG[b][:, bass.ts(regs["c"], TOK)], s_gp)
        zsrc = zownD[b].rearrange("(dt p) t -> p dt t", p=128)
        G.wait_ge(s_gp.h, t_zown)
        for i in range(NBLK):
            p = i % 2
            sl1 = slice(p * DM, (p + 1) * DM)
            if state["TRT"][p]:
                G.wait_ge(s_pe.h, state["TRT"][p])
            t_z = dma(G, zblk[:, p * 1024:(p + 1) * 1024],
                      zsrc[:, :, i * 128:(i + 1) * 128], s_zb[p])
            T.wait_ge(s_zb[p].h, t_z)
            T.wait_ge(s_dve.h, s_dve.v)
            for dt in range(8):
                mark(T, T.transpose(pp_tr[:, dt * 128:(dt + 1) * 128],
                                    zblk[:, p * 1024 + dt * 128:p * 1024 + (dt + 1) * 128],
                                    idb[:]), s_pe)
            state["TRT"][p] = s_pe.v
            V.wait_ge(s_pe.h, s_pe.v)
            vv()
            mark(V, V.tensor_copy(ztm[:, sl1], pp_tr[:, 0:1024]), s_dve)
            A.wait_ge(s_dve.h, s_dve.v)
            aa()
            mark(A, A.activation(zsc[:], ztm[:, sl1], AF.Copy, accum_out=statb[:, 0:1]), s_act)
            aa()
            mark(A, A.activation(zsc[:], ztm[:, sl1], AF.Square, accum_out=statb[:, 1:2]), s_act)
            V.wait_ge(s_act.h, s_act.v)
            vv()
            mark(V, V.tensor_scalar_mul(statb[:, 2:3], statb[:, 0:1], 1.0 / DM), s_dve)
            vv()
            mark(V, V.tensor_mul(statb[:, 3:4], statb[:, 2:3], statb[:, 2:3]), s_dve)
            vv()
            mark(V, V.tensor_scalar(statb[:, 4:5], statb[:, 1:2], 1.0 / DM, EPS,
                                    OP.mult, OP.add), s_dve)
            vv()
            mark(V, V.tensor_sub(statb[:, 4:5], statb[:, 4:5], statb[:, 3:4]), s_dve)
            emit_rsqrt(statb[:, 4:5], statb[:, 5:6], statb[:, 6:7])
            vv()
            if state["X1O"][p]:
                V.wait_ge(s_pv[p].h, state["X1O"][p])
            mark(V, V.tensor_scalar(x1bf[:, sl1], ztm[:, sl1], statb[:, 2:3],
                                    statb[:, 5:6], OP.subtract, OP.mult), s_dve)
            X1_T = s_dve.v
            Y.wait_ge(s_dve.h, X1_T)
            state["X1O"][p] = dma(Y, x1own[b][i * 128:(i + 1) * 128, :], x1bf[:, sl1], s_pv[p])
            T.wait_ge(s_dve.h, X1_T)
            for dt in range(8):
                mark(T, T.transpose(pp_tr[:, dt * 128:(dt + 1) * 128],
                                    x1bf[:, p * DM + dt * 128:p * DM + (dt + 1) * 128],
                                    idb[:]), s_pe)
            V.wait_ge(s_pe.h, s_pe.v)
            vv()
            mark(V, V.tensor_copy(x1dm[:], pp_tr[:, 0:1024]), s_dve)
            T.wait_ge(s_dve.h, s_dve.v)
            for dt in range(8):
                mark(T, T.matmul(pp_msc[0:8, 0:128], gwb[:, dt * 8:(dt + 1) * 8],
                                 x1dm[:, dt * 128:(dt + 1) * 128],
                                 start=(dt == 0), stop=(dt == 7)), s_pe)
            V.wait_ge(s_pe.h, s_pe.v)
            vv()
            if state["PST"]:
                V.wait_ge(s_ps.h, state["PST"])
            mark(V, V.tensor_copy(sct[:], pp_msc[0:8, 0:128]), s_dve)
            T.wait_ge(s_dve.h, s_dve.v)
            mark(T, T.transpose(pp_msc[0:128, 128:136], sct[:], id32[0:8, 0:8]), s_pe)
            V.wait_ge(s_pe.h, s_pe.v)
            vv()
            mark(V, V.tensor_copy(smv[:, 0:8], pp_msc[0:128, 128:136]), s_dve)
            vv()
            mark(V, V.reduce_max(smv[:, 8:9], smv[:, 0:8], axis=mybir.AxisListType.X), s_dve)
            vv()
            mark(V, V.tensor_scalar_mul(smv[:, 9:10], smv[:, 8:9], -1.0), s_dve)
            A.wait_ge(s_dve.h, s_dve.v)
            aa()
            mark(A, A.activation(smv[:, 0:8], smv[:, 0:8], AF.Exp,
                                 bias=smv[:, 9:10], accum_out=smv[:, 10:11]), s_act)
            V.wait_ge(s_act.h, s_act.v)
            vv()
            mark(V, V.reciprocal(smv[:, 11:12], smv[:, 10:11]), s_dve)
            vv()
            mark(V, V.tensor_scalar(ptb[:, 0:8], smv[:, 0:8], smv[:, 11:12], None,
                                    OP.mult), s_dve)
            T.wait_ge(s_dve.h, s_dve.v)
            mark(T, T.transpose(pp_tr[0:8, 0:128], ptb[:, 0:8], idb[:]), s_pe)
            V.wait_ge(s_pe.h, s_pe.v)
            vv()
            mark(V, V.tensor_copy(sct[:], pp_tr[0:8, 0:128]), s_dve)
            Y.wait_ge(s_dve.h, s_dve.v)
            state["PST"] = dma(Y, probsT[b][:, i * 128:(i + 1) * 128], sct[:], s_ps)
        G.wait_ge(s_pv[0].h, s_pv[0].v)
        G.wait_ge(s_pv[1].h, s_pv[1].v)
        G.wait_ge(s_ps.h, s_ps.v)
        G.collective_compute("AllGather", OP.bypass, RG, [x1own[b]], [x1AG[b]]).then_inc(s_cc.h, 1)
        s_cc.v += 1
        G.collective_compute("AllGather", OP.bypass, RG, [probsT[b]], [probsAG[b]]).then_inc(s_cc.h, 1)
        s_cc.v += 1
        if cfg.taps:
            G.wait_ge(s_cc.h, s_cc.v)
            dma(G, tap["x1"][b], x1AG[b], s_gp)
            dma(G, tap["probs"][b], probsAG[b], s_gp)
            dma(G, tap["z"][b], zAG[b], s_gp)
            G.wait_ge(s_gp.h, s_gp.v)
        return s_cc.v

    def topk(b, ag_tok):
        G.wait_ge(s_cc.h, ag_tok)
        dma(G, probsAGx2[b][0:64, :], probsAG[b], s_gp)
        dma(G, probsAGx2[b][64:128, :], probsAG[b], s_gp)
        t_x2 = s_gp.v
        G.wait_ge(s_gp.h, t_x2)
        dma(G, probsRot[b], probsAGx2[b][bass.ds(regs["c"], 64), :], s_gp)
        G.wait_ge(s_gp.h, s_gp.v)
        srcv = probsRot[b].rearrange("(r g) (h f) -> g r h f", g=8, h=2)
        for g in range(8):
            dma(G, pbis[16 * g:16 * (g + 1), :], srcv[g:g + 1, :, :, :], s_gp)
        PB_T = s_gp.v
        V.wait_ge(s_gp.h, PB_T)
        vv()
        mark(V, V.memset(bcols[:, 0:1], 0.0), s_dve)
        vv()
        mark(V, V.memset(bcols[:, 1:2], 1.0), s_dve)
        for it in range(BISECT_ITERS):
            vv()
            mark(V, V.tensor_add(bcols[:, 2:3], bcols[:, 0:1], bcols[:, 1:2]), s_dve)
            vv()
            mark(V, V.tensor_scalar_mul(bcols[:, 2:3], bcols[:, 2:3], 0.5), s_dve)
            vv()
            mark(V, V.tensor_scalar(bmask[:], pbis[:], bcols[:, 2:3], 0.0,
                                    OP.is_gt, OP.add, accum_out=bcols[:, 3:4]), s_dve)
            T.wait_ge(s_dve.h, s_dve.v)
            mark(T, T.matmul(pp_msc[0:128, 0:1], gmb[:], bcols[:, 3:4],
                             start=True, stop=True), s_pe)
            V.wait_ge(s_pe.h, s_pe.v)
            vv()
            mark(V, V.tensor_copy(bcols[:, 4:5], pp_msc[0:128, 0:1]), s_dve)
            vv()
            mark(V, V.tensor_scalar(bmask2[:, 0:1], bcols[:, 4:5], float(K), None,
                                    OP.is_ge), s_dve)
            vv()
            mark(V, V.tensor_sub(bcols[:, 5:6], bcols[:, 2:3], bcols[:, 0:1]), s_dve)
            vv()
            mark(V, V.tensor_mul(bcols[:, 5:6], bcols[:, 5:6], bmask2[:, 0:1]), s_dve)
            vv()
            mark(V, V.tensor_add(bcols[:, 0:1], bcols[:, 0:1], bcols[:, 5:6]), s_dve)
            vv()
            mark(V, V.tensor_sub(bcols[:, 5:6], bcols[:, 1:2], bcols[:, 2:3]), s_dve)
            vv()
            mark(V, V.tensor_mul(bcols[:, 5:6], bcols[:, 5:6], bmask2[:, 0:1]), s_dve)
            vv()
            mark(V, V.tensor_add(bcols[:, 1:2], bcols[:, 2:3], bcols[:, 5:6]), s_dve)
        vv()
        mark(V, V.tensor_scalar(bmask[:], pbis[:], bcols[:, 1:2], 0.0,
                                OP.is_gt, OP.add, accum_out=bcols[:, 3:4]), s_dve)
        T.wait_ge(s_dve.h, s_dve.v)
        mark(T, T.matmul(pp_msc[0:128, 0:1], gmb[:], bcols[:, 3:4],
                         start=True, stop=True), s_pe)
        V.wait_ge(s_pe.h, s_pe.v)
        vv()
        mark(V, V.tensor_copy(bcols[:, 5:6], pp_msc[0:128, 0:1]), s_dve)
        vv()
        mark(V, V.tensor_scalar(bmask[:], pbis[:], bcols[:, 1:2], None, OP.is_gt), s_dve)
        vv()
        mark(V, V.tensor_scalar(bmask2[:], pbis[:], bcols[:, 0:1], None, OP.is_gt), s_dve)
        vv()
        mark(V, V.tensor_sub(bmask2[:], bmask2[:], bmask[:]), s_dve)
        vv()
        mark(V, V.tensor_tensor_scan(cumb[:], onesc[:, 0:FB], bmask2[:], 0.0,
                                     OP.mult, OP.add), s_dve)
        vv()
        mark(V, V.tensor_copy(bcols[:, 6:7], cumb[:, FB - 1:FB]), s_dve)
        T.wait_ge(s_dve.h, s_dve.v)
        mark(T, T.matmul(pp_msc[0:128, 0:1], tmb[:], bcols[:, 6:7],
                         start=True, stop=True), s_pe)
        V.wait_ge(s_pe.h, s_pe.v)
        vv()
        mark(V, V.tensor_copy(bcols[:, 7:8], pp_msc[0:128, 0:1]), s_dve)
        vv()
        mark(V, V.tensor_scalar(cumb[:], cumb[:], bcols[:, 7:8], -1.0, OP.add, OP.add), s_dve)
        vv()
        mark(V, V.tensor_scalar_mul(bcols[:, 6:7], bcols[:, 5:6], -1.0), s_dve)
        vv()
        mark(V, V.tensor_scalar(bcols[:, 6:7], bcols[:, 6:7], float(K), None, OP.add), s_dve)
        vv()
        mark(V, V.tensor_scalar(cumb[:], cumb[:], bcols[:, 6:7], None, OP.is_lt), s_dve)
        vv()
        mark(V, V.tensor_mul(cumb[:], cumb[:], bmask2[:]), s_dve)
        vv()
        mark(V, V.tensor_add(bmask[:], bmask[:], cumb[:]), s_dve)
        if cfg.taps:
            Y.wait_ge(s_dve.h, s_dve.v)
            dma(Y, tap["bis"][b], bcols[:], s_sy)
        vv()
        mark(V, V.tensor_tensor_scan(cumb[:], onesc[:, 0:FB], bmask[:], 0.0,
                                     OP.mult, OP.add), s_dve)
        vv()
        mark(V, V.tensor_copy(bcols[:, 6:7], cumb[:, FB - 1:FB]), s_dve)
        T.wait_ge(s_dve.h, s_dve.v)
        mark(T, T.matmul(pp_msc[0:128, 0:1], tmb[:], bcols[:, 6:7],
                         start=True, stop=True), s_pe)
        V.wait_ge(s_pe.h, s_pe.v)
        vv()
        mark(V, V.tensor_copy(bcols[:, 7:8], pp_msc[0:128, 0:1]), s_dve)
        vv()
        mark(V, V.tensor_scalar(cumb[:], cumb[:], bcols[:, 7:8], -1.0, OP.add, OP.add), s_dve)
        vv()
        mark(V, V.tensor_mul(ucols[:, 0:FB], cumb[:], bmask[:]), s_dve)
        vv()
        mark(V, V.tensor_scalar(bmask2[:], bmask[:], 0.5, 1e9, OP.is_lt, OP.mult), s_dve)
        vv()
        mark(V, V.tensor_add(ucols[:, 0:FB], ucols[:, 0:FB], bmask2[:]), s_dve)
        ui = ucols.bitcast(I32)
        ci2 = cumb.bitcast(I32)
        vv()
        mark(V, V.tensor_copy(ui[:, 0:FB], ucols[:, 0:FB]), s_dve)
        vv()
        mark(V, V.tensor_single_scalar(ci2[:, 0:FB], ui[:, 0:FB], 127, OP.bitwise_and), s_dve)
        vv()
        mark(V, V.tensor_single_scalar(ui[:, 0:FB], ui[:, 0:FB], 7, OP.arith_shift_right), s_dve)
        vv()
        mark(V, V.tensor_single_scalar(ci2[:, 0:FB], ci2[:, 0:FB], NTAU, OP.mult), s_dve)
        vv()
        mark(V, V.tensor_tensor(ui[:, 0:FB], ui[:, 0:FB], ci2[:, 0:FB], OP.add), s_dve)
        UP_T = s_dve.v
        Y.wait_ge(s_dve.h, UP_T)
        dma(Y, ubounce[b], ui[0:16, 0:FB], s_sy)
        dma(Y, gbounce[b], pbis[0:16, :], s_sy)
        Y.wait_ge(s_sy.h, s_sy.v)
        dma(Y, u32b[:], ubounce[b].rearrange("a (p c) -> (a p) c", p=8), s_sy)
        t_g128 = dma(Y, g128[:], gbounce[b].rearrange("a (p c) -> (a p) c", p=8), s_sy)
        V.wait_ge(s_sy.h, t_g128)
        V.wait_ge(s_ld.h, LD0)
        vv()
        mark(V, V.tensor_copy(pairb[:, 0:2 * PD:2], tk128[:]), s_dve)
        vv()
        mark(V, V.tensor_copy(pairb[:, 1:2 * PD:2], g128[:]), s_dve)
        t_rb = s_sy.v
        Y.wait_ge(s_dve.h, s_dve.v)
        G.wait_ge(s_sy.h, t_rb)
        G.wait_ge(s_dve.h, s_dve.v)
        G.wait_ge(s_gp.h, ZIDG)
        idgflat = idg[b].rearrange("p c -> (p c)").rearrange("(x o) -> x o", o=2)
        for q in range(PD):
            G.indirect_dma_start(
                out=idgflat, out_offset=IndirectOffsetOnAxis(u32b[:, q:q + 1], 0),
                in_=pairb[:, 2 * q:2 * q + 2], in_offset=None,
                bounds_check=bc128, oob_is_err=False,
            ).then_inc(s_gp.h, 16)
            s_gp.v += 16
        IDG_T = s_gp.v
        if cfg.taps:
            G.wait_ge(s_gp.h, IDG_T)
            dma(G, tap["idg"][b], idg[b], s_gp)
            G.wait_ge(s_gp.h, s_gp.v)
        Y.wait_ge(s_gp.h, IDG_T)
        t_ig = dma(Y, ucols[:, 0:2 * NTAU], idg[b], s_sy)
        V.wait_ge(s_sy.h, t_ig)
        vv()
        mark(V, V.tensor_copy(idsb[:, 0:NTAU], ucols[:, 0:2 * NTAU:2]), s_dve)
        vv()
        mark(V, V.tensor_copy(gsb[:, 0:NTAU], ucols[:, 1:2 * NTAU:2]), s_dve)
        return s_dve.v

    def ffn(b, ids_tok, aux=None, aux_every=1):
        step = 0
        O2C = [0]
        HGU = [0]
        WR_EV = [0, 0]
        XGU = [0, 0]
        XGP = [0, 0]
        GEL_T = [0, 0]
        PSR_T = [0, 0]
        GEL_ALL = 0
        O1_T = [0, 0]
        ctb_tok = [0] * (NTAU // 2)
        for hq in range(4):
            G.wait_ge(s_pe.h, s_pe.v)
            w1_tok = dma(G, w1q[:].rearrange("p (dt h) -> p dt h", dt=8),
                         w1e.rearrange("(dt p) h -> p dt h", p=128)
                         [:, :, hq * HQ:(hq + 1) * HQ], s_w1)
            w2_tok = dma(G, w2q[:].rearrange("p (a d) -> p a d", d=DM),
                         w2e.rearrange("(a p) d -> p a d", p=128)
                         [:, hq * NHT:(hq + 1) * NHT, :], s_w2)
            if cfg.taps and b == 0 and hq == 1:
                G.wait_ge(s_ctb[0].h, s_ctb[0].v)
                G.wait_ge(s_ctb[1].h, s_ctb[1].v)
                dma(G, tap["c0"][0], contrib[b], s_gp)
                G.wait_ge(s_gp.h, s_gp.v)
            if cfg.taps and b == 0 and hq == 0:
                G.wait_ge(s_w1.h, w1_tok)
                G.wait_ge(s_w2.h, w2_tok)
                dma(G, tap["w1q"][0][:, 0:8 * HQ], w1q[:], s_gp)
                dma(G, tap["w2q"][0][:, 0:NHT * DM], w2q[:], s_gp)
                G.wait_ge(s_gp.h, s_gp.v)
            for g in range(NG):
                if aux is not None and step % aux_every == 0:
                    next(aux, None)
                step += 1
                gp = g % 2
                for tt in range(GQ):
                    tau = g * GQ + tt
                    tp = tau % 2
                    G.wait_ge(s_dve.h, ids_tok)
                    if XGP[tp]:
                        G.wait_ge(s_pe.h, XGP[tp])
                    G.indirect_dma_start(
                        out=xg[:, tp * DM:(tp + 1) * DM], out_offset=None,
                        in_=x1AG[b],
                        in_offset=IndirectOffsetOnAxis(idsb[:, tau:tau + 1], 0),
                    ).then_inc(s_xg[tp].h, 16)
                    s_xg[tp].v += 16
                    t_xg = s_xg[tp].v
                    T.wait_ge(s_xg[tp].h, t_xg)
                    T.wait_ge(s_dve.h, s_dve.v)
                    for dt in range(8):
                        mark(T, T.transpose(pp_tr[:, dt * 128:(dt + 1) * 128],
                                            xg[:, tp * DM + dt * 128:tp * DM + (dt + 1) * 128],
                                            idb[:]), s_pe)
                    XGP[tp] = s_pe.v
                    V.wait_ge(s_pe.h, s_pe.v)
                    if XGU[gp]:
                        V.wait_ge(s_pe.h, XGU[gp])
                    vv()
                    xgt_dst = xgt[:, gp * 8 * GQ * 128:(gp + 1) * 8 * GQ * 128] \
                        .rearrange("p (dt t c) -> p dt t c", dt=8, t=GQ, c=128)[:, :, tt, :]
                    mark(V, V.tensor_copy(
                        xgt_dst,
                        pp_tr[:, 0:1024].rearrange("p (dt c) -> p dt c", dt=8, c=128)), s_dve)
                XGT_T = s_dve.v
                for ht in range(NHT):
                    hp = ht % 2
                    o1sl = slice(hp * O1S, hp * O1S + GQ * 128)
                    if ht == 0:
                        T.wait_ge(s_w1.h, w1_tok)
                        T.wait_ge(s_dve.h, XGT_T)
                    if PSR_T[hp]:
                        T.wait_ge(s_act.h, PSR_T[hp])
                    for dt in range(8):
                        mark(T, T.matmul(pp_o1[:, o1sl],
                                         w1q[:, dt * HQ + ht * 128:dt * HQ + (ht + 1) * 128],
                                         xgt[:, (gp * 8 + dt) * GQ * 128:
                                                (gp * 8 + dt + 1) * GQ * 128],
                                         start=(dt == 0), stop=(dt == 7)), s_pe)
                    O1_T[hp] = s_pe.v
                    XGU[gp] = s_pe.v
                    A.wait_ge(s_pe.h, O1_T[hp])
                    if GEL_ALL:
                        A.wait_ge(s_dve.h, GEL_ALL)
                    aa()
                    mark(A, A.activation(gx[:], pp_o1[:, o1sl], AF.Copy), s_act)
                    aa()
                    PSR_T[hp] = mark(A, A.activation(gsc1[:], pp_o1[:, o1sl], AF.Square), s_act)
                    aa()
                    mark(A, A.activation(gsc1[:], gsc1[:], AF.Copy, scale=GK, bias=SQ2PI), s_act)
                    V.wait_ge(s_act.h, s_act.v)
                    vv()
                    mark(V, V.tensor_mul(gsc2[:], gx[:], gsc1[:]), s_dve)
                    A.wait_ge(s_dve.h, s_dve.v)
                    aa()
                    mark(A, A.activation(gsc2[:], gsc2[:], AF.Tanh), s_act)
                    aa()
                    TH_T = mark(A, A.activation(gsc2[:], gsc2[:], AF.Copy,
                                                scale=0.5, bias=0.5), s_act)
                    V.wait_ge(s_act.h, TH_T)
                    vv()
                    if HGU[0]:
                        V.wait_ge(s_pe.h, HGU[0])
                    GEL_T[hp] = mark(V, V.tensor_mul(hgel[:, ht * GQ * 128:(ht + 1) * GQ * 128],
                                                     gx[:], gsc2[:]), s_dve)
                    GEL_ALL = GEL_T[hp]
                if cfg.taps and b == 0 and hq == 0 and g == 0:
                    Y.wait_ge(s_dve.h, s_dve.v)
                    dma(Y, tap["xgt"][0][:, 0:8 * GQ * 128],
                        xgt[:, 0:8 * GQ * 128], s_sy)
                    dma(Y, tap["hgel"][0][:, 0:NHT * GQ * 128], hgel[:], s_sy)
                    Y.wait_ge(s_sy.h, s_sy.v)
                    V.wait_ge(s_sy.h, s_sy.v)
                for tt in range(GQ):
                    tau = g * GQ + tt
                    pr = (tau // 2) % 2
                    pi = tau % 2
                    for ht in range(NHT):
                        if ht == 0:
                            T.wait_ge(s_dve.h, GEL_ALL)
                            T.wait_ge(s_w2.h, w2_tok)
                            if O2C[0]:
                                T.wait_ge(s_act.h, O2C[0])
                            if WR_EV[pr] and pi == 0:
                                T.wait_ge(s_ctb[pr].h, WR_EV[pr])
                        for h2 in range(2):
                            mark(T, T.matmul(pp_o2[:, h2 * 512:(h2 + 1) * 512],
                                             hgel[:, ht * GQ * 128 + tt * 128:
                                                    ht * GQ * 128 + (tt + 1) * 128],
                                             w2q[:, ht * DM + h2 * 512:ht * DM + (h2 + 1) * 512],
                                             start=(ht == 0), stop=(ht == NHT - 1)), s_pe)
                    HGU[0] = s_pe.v
                    A.wait_ge(s_pe.h, s_pe.v)
                    if WR_EV[pr] and pi == 0:
                        A.wait_ge(s_ctb[pr].h, WR_EV[pr])
                    aa()
                    O2C[0] = mark(A, A.activation(
                        wrow[:, (pr * 2 + pi) * DM:(pr * 2 + pi + 1) * DM],
                        pp_o2[:, 0:DM], AF.Copy), s_act)
                    if cfg.taps and b == 0 and hq == 0 and g == 0 and tau == 0:
                        Y.wait_ge(s_dve.h, s_dve.v)
                        dma(Y, tap["wrow"][0][:, 0:DM], wrow[:, 0:DM], s_sy)
                        Y.wait_ge(s_sy.h, s_sy.v)
                        V.wait_ge(s_sy.h, s_sy.v)
                    if pi == 1:
                        G.wait_ge(s_act.h, s_act.v)
                        G.wait_ge(s_ctb[pr].h, s_ctb[pr].v)
                        rows = slice(128 + (tau - 1) * 128, 128 + (tau + 1) * 128)
                        srcv = wrow[:, pr * 2 * DM:(pr + 1) * 2 * DM].rearrange(
                            "p (t d) -> p t d", d=DM)
                        dstv = contrib[b][rows, :].rearrange("(t p) d -> p t d", p=128)
                        if hq == 0:
                            ctb_tok[tau // 2] = dma(G, dstv, srcv, s_ctb[pr])
                        else:
                            ctb_tok[tau // 2] = dma_acc(dstv, srcv, s_ctb[pr])
                        WR_EV[pr] = ctb_tok[tau // 2]
        GT_T = [0, 0]
        for pp2 in range(NTAU // 2):
            pr = pp2 % 2
            rows = slice(128 + pp2 * 256, 128 + (pp2 + 1) * 256)
            G.wait_ge(s_ctb[pr].h, s_ctb[pr].v)
            t_r = dma(G, wrow[:, pr * 2 * DM:(pr + 1) * 2 * DM].rearrange(
                "p (t d) -> p t d", d=DM),
                contrib[b][rows, :].rearrange("(t p) d -> p t d", p=128), s_gp)
            V.wait_ge(s_gp.h, t_r)
            for tt in range(2):
                tau = pp2 * 2 + tt
                vv()
                mark(V, V.tensor_scalar(
                    wrow[:, (pr * 2 + tt) * DM:(pr * 2 + tt + 1) * DM],
                    wrow[:, (pr * 2 + tt) * DM:(pr * 2 + tt + 1) * DM],
                    gsb[:, tau:tau + 1], None, OP.mult), s_dve)
            G.wait_ge(s_dve.h, s_dve.v)
            GT_T[pr] = dma(G, contrib[b][rows, :].rearrange("(t p) d -> p t d", p=128),
                           wrow[:, pr * 2 * DM:(pr + 1) * 2 * DM].rearrange(
                               "p (t d) -> p t d", d=DM), s_ctb[pr])
            ctb_tok[pp2] = GT_T[pr]
        uu = posu[:]
        if state["POST"]:
            V.wait_ge(s_gp.h, state["POST"])
        vv()
        mark(V, V.tensor_single_scalar(uu[:, 0:NTAU], idsb[:, 0:NTAU], 127, OP.bitwise_and), s_dve)
        vv()
        mark(V, V.tensor_single_scalar(uu[:, 0:NTAU], uu[:, 0:NTAU], PD, OP.mult), s_dve)
        vv()
        mark(V, V.tensor_single_scalar(uu[:, NTAU:2 * NTAU], idsb[:, 0:NTAU], 7,
                                       OP.arith_shift_right), s_dve)
        vv()
        mark(V, V.tensor_tensor(uu[:, 0:NTAU], uu[:, 0:NTAU], uu[:, NTAU:2 * NTAU],
                                OP.add), s_dve)
        PJ_T = s_dve.v
        G.wait_ge(s_dve.h, PJ_T)
        pdflat = posdense[b].rearrange("p c -> (p c)").rearrange("(x o) -> x o", o=1)
        for tau in range(NTAU):
            G.indirect_dma_start(
                out=pdflat, out_offset=IndirectOffsetOnAxis(uu[:, tau:tau + 1], 0),
                in_=jvb[:, tau:tau + 1], in_offset=None,
                bounds_check=bcpd, oob_is_err=False,
            ).then_inc(s_gp.h, 16)
            s_gp.v += 16
        POS_T = s_gp.v
        state["POST"] = POS_T
        G.wait_ge(s_ctb[0].h, s_ctb[0].v)
        G.wait_ge(s_ctb[1].h, s_ctb[1].v)
        G.collective_compute("AllGather", OP.bypass, RG,
                             [contrib[b]], [contribAG[b]]).then_inc(s_cc.h, 1)
        s_cc.v += 1
        G.wait_ge(s_gp.h, POS_T)
        G.collective_compute("AllGather", OP.bypass, RG,
                             [posdense[b]], [posAG[b]]).then_inc(s_cc.h, 1)
        s_cc.v += 1
        if cfg.taps:
            G.wait_ge(s_cc.h, s_cc.v)
            dma(G, tap["contrib"][b], contrib[b], s_gp)
            dma(G, tap["pos"][b], posAG[b], s_gp)
            G.wait_ge(s_gp.h, s_gp.v)
        return s_cc.v

    def finalize(b, cc_tok):
        Y.wait_ge(s_sy.h, s_sy.v)
        G.wait_ge(s_cc.h, cc_tok)
        t_pos = dma(G, posownD[b], posAG[b][:, bass.ts(regs["c"], NBLK)], s_gp)
        G.wait_ge(s_gp.h, t_pos)
        t_a8 = dma(G, addr8[:].rearrange("p (e n) -> p e n", n=NBLK),
                   posownD[b].rearrange("(e p) n -> p e n", p=128), s_gp)
        G.wait_ge(s_gp.h, t_a8)
        ADD_T = [0, 0]
        PRV = 0
        for i in range(NBLK):
            if PRV:
                Y.wait_ge(s_dve.h, PRV)
            t_x1 = dma(Y, x1r[:, 0:DM], x1own[b][i * 128:(i + 1) * 128, :], s_sy)
            for e in range(8):
                ip = e % 2
                sl = slice(ip * DM, (ip + 1) * DM)
                if ADD_T[ip]:
                    G.wait_ge(s_dve.h, ADD_T[ip])
                G.indirect_dma_start(
                    out=gat1[:, sl], out_offset=None,
                    in_=contribAG[b],
                    in_offset=IndirectOffsetOnAxis(addr8[:, e * NBLK + i:e * NBLK + i + 1], 0),
                ).then_inc(s_fg[ip].h, 16)
                s_fg[ip].v += 16
                V.wait_ge(s_fg[ip].h, s_fg[ip].v)
                if e == 0:
                    vv()
                    ADD_T[ip] = mark(V, V.tensor_copy(accr[:], gat1[:, sl]), s_dve)
                else:
                    vv()
                    ADD_T[ip] = mark(V, V.tensor_add(accr[:], accr[:], gat1[:, sl]), s_dve)
            V.wait_ge(s_sy.h, t_x1)
            V.wait_ge(s_act.h, s_act.v)
            V.wait_ge(s_outd.h, s_outd.v)
            vv()
            PRV = mark(V, V.tensor_add(z2b[:], x1r[:, 0:DM], accr[:]), s_dve)
            A.wait_ge(s_dve.h, s_dve.v)
            aa()
            mark(A, A.activation(zsc[:], z2b[:], AF.Copy, accum_out=statb[:, 0:1]), s_act)
            aa()
            mark(A, A.activation(zsc[:], z2b[:], AF.Square, accum_out=statb[:, 1:2]), s_act)
            V.wait_ge(s_act.h, s_act.v)
            vv()
            mark(V, V.tensor_scalar_mul(statb[:, 2:3], statb[:, 0:1], 1.0 / DM), s_dve)
            vv()
            mark(V, V.tensor_mul(statb[:, 3:4], statb[:, 2:3], statb[:, 2:3]), s_dve)
            vv()
            mark(V, V.tensor_scalar(statb[:, 4:5], statb[:, 1:2], 1.0 / DM, EPS,
                                    OP.mult, OP.add), s_dve)
            vv()
            mark(V, V.tensor_sub(statb[:, 4:5], statb[:, 4:5], statb[:, 3:4]), s_dve)
            emit_rsqrt(statb[:, 4:5], statb[:, 5:6], statb[:, 6:7])
            vv()
            mark(V, V.tensor_scalar(outb[:], z2b[:], statb[:, 2:3],
                                    statb[:, 5:6], OP.subtract, OP.mult), s_dve)
            Y.wait_ge(s_dve.h, s_dve.v)
            dma(Y, out[b, i], outb[:], s_outd)
            yield

    SA = cfg.stop_after
    if SA is not None:
        for b in range(B):
            for _ in mamba_scan(b):
                pass
            if SA == "mamba":
                continue
            agtok = owner_ln1(b, state["ZAGTOK"][b])
            if SA == "owner":
                continue
            itok = topk(b, agtok)
            if SA == "topk":
                continue
            cctok = ffn(b, itok)
            if SA == "ffn":
                continue
            for _ in finalize(b, cctok):
                pass
    else:
        g0 = mamba_scan(0)
        for _ in g0:
            pass
        a0 = owner_ln1(0, state["ZAGTOK"][0])
        i0 = topk(0, a0)
        g1 = mamba_scan(1)
        c0 = ffn(0, i0, aux=g1)
        for _ in g1:
            pass
        a1 = owner_ln1(1, state["ZAGTOK"][1])
        i1 = topk(1, a1)
        f0 = finalize(0, c0)
        c1 = ffn(1, i1, aux=f0, aux_every=2)
        for _ in f0:
            pass
        for _ in finalize(1, c1):
            pass

    if SA is not None:
        for e_, sm in ((V, s_dve), (A, s_act), (T, s_pe), (G, s_gp), (Y, s_sy)):
            e_.wait_ge(sm.h, sm.v)
        G.wait_ge(s_cc.h, s_cc.v)
        Y.wait_ge(s_dve.h, s_dve.v)
        dma(Y, out[0, 0], outb[:], s_outd)
    Y.wait_ge(s_outd.h, s_outd.v)
    ctx.close()
    return nc, tap




def make_inputs(cfg, x, A_log, W_delta, W_dt, b_dt, W_B, W_C, Dv, gate_w, w1, w2):
    x = np.asarray(x, np.float32)
    onehots = np.kron(np.eye(16, dtype=np.float32), np.ones((1, 128), np.float32))
    gm = np.zeros((128, 128), np.float32)
    for p in range(128):
        gm[(p // 16) * 16:(p // 16) * 16 + 16, p] = 1.0
    tm = np.zeros((128, 128), np.float32)
    for p2 in range(128):
        for p in range((p2 // 16) * 16, p2):
            tm[p, p2] = 1.0
    tokmap = np.zeros((16, cfg.FB), np.float32)
    for s in range(16):
        for f in range(cfg.FB):
            tokmap[s, f] = agrow_of_token(cfg, tok_of(cfg, s, f))
    tokmap128 = np.ascontiguousarray(
        tokmap.reshape(16, 8, cfg.FB // 8).transpose(0, 1, 2).reshape(128, cfg.FB // 8))
    maps = []
    for c in range(8):
        jv = np.zeros((128, cfg.NTAU), np.int32)
        for tau in range(cfg.NTAU):
            for p in range(128):
                jv[p, tau] = cfg.CROW * c + 128 + 128 * tau + p
        sl = slice(128 * c, 128 * (c + 1))
        meta = np.zeros((1, 16), np.int32)
        meta[0, 0] = c
        for g in range(8):
            meta[0, 1 + g] = (g + c) % 8
        maps.append(dict(
            xT=np.ascontiguousarray(x[:, :, sl].transpose(0, 2, 1)),
            projW=np.ascontiguousarray(np.concatenate(
                [np.asarray(W_delta, np.float32)[sl],
                 np.asarray(W_B, np.float32)[sl],
                 np.asarray(W_C, np.float32)[sl]], axis=1)),
            wdt_ext=np.ascontiguousarray(np.concatenate(
                [np.asarray(W_dt, np.float32)[:, sl],
                 np.asarray(b_dt, np.float32)[None, sl]], axis=0)),
            a_sl=np.ascontiguousarray(np.asarray(A_log, np.float32)[sl]),
            d_sl=np.ascontiguousarray(np.asarray(Dv, np.float32)[sl, None]),
            gate_w=np.asarray(gate_w, np.float32),
            w1e=np.ascontiguousarray(np.asarray(w1, np.float32)[c]),
            w2e=np.ascontiguousarray(np.asarray(w2, np.float32)[c]),
            idn32=np.eye(128, dtype=np.float32),
            idnbf=np.eye(128, dtype=BF),
            onehots=onehots, gmask=gm, tmask=tm, tokmap=tokmap,
            tokmap128=tokmap128, jvals=jv,
            meta=meta,
        ))
    return maps


def assemble_out(cfg, outs):
    full = np.zeros((cfg.B, cfg.L, cfg.DM), np.float32)
    for c in range(8):
        o = outs[c]
        for i in range(cfg.NBLK):
            blk = 8 * i + c
            full[:, blk * 128:(blk + 1) * 128, :] = o[:, i]
    return full


def kernel(L0_attn_x, L0_attn_A_log, L0_attn_W_delta, L0_attn_W_dt, L0_attn_b_dt,
           L0_attn_W_B, L0_attn_W_C, L0_attn_D, L0_ffn_gate_w, L0_ffn_w1, L0_ffn_w2):
    cfg = Cfg()
    nc, _ = build_program(cfg)
    in_maps = make_inputs(cfg, L0_attn_x, L0_attn_A_log, L0_attn_W_delta, L0_attn_W_dt,
                          L0_attn_b_dt, L0_attn_W_B, L0_attn_W_C, L0_attn_D,
                          L0_ffn_gate_w, L0_ffn_w1, L0_ffn_w2)
    res = run_bass_kernel_spmd(nc, in_maps, core_ids=list(range(8)))
    return assemble_out(cfg, [r["out"] for r in res.results])


# version 09
# speedup vs baseline: 1.0114x; 1.0055x over previous
import numpy as np
import ml_dtypes
import concourse.bass as bass
import concourse.mybir as mybir
from concourse.bass_core_rust import IndirectOffsetOnAxis
from concourse.bass_utils import run_bass_kernel_spmd
from contextlib import ExitStack

F32 = mybir.dt.float32
BF16 = mybir.dt.bfloat16
I32 = mybir.dt.int32
AF = mybir.ActivationFunctionType
OP = mybir.AluOpType
ET = mybir.EngineType
BF = ml_dtypes.bfloat16

SQ2PI = 0.7978845608028654
GK = 0.044715 * SQ2PI
EPS = 1e-5
BISECT_ITERS = 36


class Cfg:
    def __init__(self, L=16384, HID=4096, CH=512, taps=False, stop_after=None):
        self.stop_after = stop_after
        self.B = 2
        self.L = L
        self.DM = 1024
        self.N = 16
        self.HID = HID
        self.HQ = HID // 4
        self.NHT = self.HQ // 128
        self.K = L // 4
        self.TOK = L // 8
        self.CH = CH
        self.NCH = L // CH
        self.NBLK = self.TOK // 128
        self.FB = self.TOK // 2
        self.PD = L // 128
        self.NTAU = self.K // 128
        self.GQ = 4 if self.NTAU % 8 == 0 else (2 if self.NTAU % 4 == 0 else 1)
        self.NG = self.NTAU // self.GQ
        self.CROW = self.K + 128
        self.QAG = max(1, self.NCH // 4)
        self.taps = taps
        assert self.NBLK % 2 == 0 and self.FB >= 128 and self.CH <= 512


def tok_of(cfg, s, f):
    r = s >> 1
    i = (s & 1) * (cfg.NBLK // 2) + (f >> 7)
    j = f & 127
    return 128 * (8 * i + r) + j


def agrow_of_token(cfg, t):
    i, r, j = t // 1024 if False else (t // 128) // 8, (t // 128) % 8, t % 128
    return cfg.TOK * r + 128 * i + j


class Sem:
    def __init__(self, nc, name):
        self.h = nc.alloc_semaphore(name)
        self.v = 0


def build_program(cfg):
    nc = bass.Bass("TRN2", target_bir_lowering=False, dynamic_dma_scratch_size=49152)
    B, L, DM, N = cfg.B, cfg.L, cfg.DM, cfg.N
    CH, NCH, K, TOK, NBLK = cfg.CH, cfg.NCH, cfg.K, cfg.TOK, cfg.NBLK
    HID, HQ, NHT, NTAU = cfg.HID, cfg.HQ, cfg.NHT, cfg.NTAU
    GQ, NG, CROW, QAG = cfg.GQ, cfg.NG, cfg.CROW, cfg.QAG
    FB, PD = cfg.FB, cfg.PD
    RG = [list(range(8))]
    AGTOT = 8 * TOK

    def ein(name, shape, dt=F32):
        return nc.dram_tensor(name, shape, dt, kind="ExternalInput").ap()

    xT = ein("xT", [B, 128, L])
    projW = ein("projW", [128, 96])
    wdt_ext = ein("wdt_ext", [65, 128])
    a_sl = ein("a_sl", [128, N])
    d_sl = ein("d_sl", [128, 1])
    gate_w = ein("gate_w", [DM, 8])
    w1e = ein("w1e", [DM, HID])
    w2e = ein("w2e", [HID, DM])
    idn32 = ein("idn32", [128, 128])
    idnbf = ein("idnbf", [128, 128], BF16)
    onehots = ein("onehots", [16, 2048])
    gmask = ein("gmask", [128, 128])
    tmask = ein("tmask", [128, 128])
    tokmap = ein("tokmap", [16, FB])
    tokmap128 = ein("tokmap128", [128, PD])
    jvals = ein("jvals", [128, NTAU], I32)
    meta = ein("meta", [1, 16], I32)

    out = nc.dram_tensor("out", [B, NBLK, 128, DM], F32, kind="ExternalOutput").ap()

    def idram(name, shape, dt=F32):
        return nc.dram_tensor(name, shape, dt, kind="Internal").ap()

    projpart = [idram(f"projpart{b}", [96, L]) for b in range(B)]
    projAR = [idram(f"projAR{b}", [96, L]) for b in range(B)]
    deltaD = [idram(f"delta{b}", [128, L]) for b in range(B)]
    zin = [idram(f"zin{b}", [128, L], BF16) for b in range(B)]
    zAG = [idram(f"zAG{b}", [DM, L], BF16) for b in range(B)]
    x1own = [idram(f"x1own{b}", [TOK, DM], BF16) for b in range(B)]
    x1AG = [idram(f"x1AG{b}", [AGTOT, DM], BF16) for b in range(B)]
    probsT = [idram(f"probsT{b}", [8, TOK]) for b in range(B)]
    probsAG = [idram(f"probsAG{b}", [64, TOK]) for b in range(B)]
    ubounce = [idram(f"ubounce{b}", [16, FB], I32) for b in range(B)]
    gbounce = [idram(f"gbounce{b}", [16, FB]) for b in range(B)]
    idg = [idram(f"idg{b}", [128, 2 * NTAU]) for b in range(B)]
    posdense = [idram(f"posdense{b}", [128, PD], I32) for b in range(B)]
    posAG = [idram(f"posAG{b}", [1024, PD], I32) for b in range(B)]
    contrib = [idram(f"contrib{b}", [CROW, DM], BF16) for b in range(B)]
    contribAG = [idram(f"contribAG{b}", [8 * CROW, DM], BF16) for b in range(B)]
    zownD = [idram(f"zownD{b}", [DM, NBLK * 128], BF16) for b in range(B)]
    probsAGx2 = [idram(f"probsAGx2{b}", [128, TOK]) for b in range(B)]
    posownD = [idram(f"posownD{b}", [1024, NBLK], I32) for b in range(B)]
    probsRot = [idram(f"probsRot{b}", [64, TOK]) for b in range(B)]

    tap = {}
    if cfg.taps:
        def tout(name, shape, dt=F32):
            tap[name] = nc.dram_tensor("tap_" + name, shape, dt, kind="ExternalOutput").ap()
        tout("delta", [B, 128, L])
        tout("y", [B, 128, L])
        tout("z", [B, DM, L])
        tout("x1", [B, AGTOT, DM])
        tout("probs", [B, 64, TOK])
        tout("bis", [B, 128, 8])
        tout("idg", [B, 128, 2 * NTAU])
        tout("contrib", [B, CROW, DM])
        tout("xgt", [1, 128, 2 * 8 * 64 * 128], BF16)
        tout("c0", [1, CROW, DM], BF16)
        tout("w1q", [1, 128, 8 * 1024], BF16)
        tout("wrow", [1, 128, 4 * 1024], BF16)
        tout("w2q", [1, 128, 8 * 1024], BF16)
        tout("hgel", [1, 128, 64 * 64 * 128], BF16)
        tout("pos", [B, 1024, PD], I32)

    ctx = ExitStack()
    sbt = lambda name, shape, dt=F32: ctx.enter_context(nc.sbuf_tensor(name, shape, dt))
    pst = lambda name, shape, dt=F32: ctx.enter_context(nc.psum_tensor(name, shape, dt))
    V, A, T, G, Y = nc.vector, nc.scalar, nc.tensor, nc.gpsimd, nc.sync

    acol = sbt("acol", [128, N])
    dcol = sbt("dcol", [128, 1])
    idb = sbt("idb", [128, 128], BF16)
    id32 = sbt("id32", [128, 128])
    onehb = sbt("onehb", [16, 2048], BF16)
    gmb = sbt("gmb", [128, 128])
    tmb = sbt("tmb", [128, 128])
    tokm = sbt("tokm", [16, FB])
    jvb = sbt("jvb", [128, NTAU], I32)
    metab = sbt("metab", [1, 16], I32)
    gwb = sbt("gwb", [128, 8 * 8], BF16)
    projwb = sbt("projwb", [128, 96])
    wdtb = sbt("wdtb", [65, 128])
    onesc = sbt("onesc", [128, max(FB, 1024)], BF16)
    zerobf = sbt("zerobf", [128, DM], BF16)
    zrow32 = sbt("zrow32", [128, PD])
    w1q = sbt("w1q", [128, 8 * HQ], BF16)
    w2q = sbt("w2q", [128, NHT * DM], BF16)
    ALIAS = (8 * HQ * 2 >= 2 * CH * 4) and (NHT * DM * 2 >= 2 * CH * 4)
    xc = sbt("xc", [128, 2 * CH])
    dc = sbt("dc", [128, 2 * CH])
    dxc = sbt("dxc", [128, 2 * CH], BF16)
    bccB = sbt("bccB", [16, 2 * CH], BF16)
    bccC = sbt("bccC", [16, 2 * CH], BF16)
    dab = sbt("dab", [128, 2 * CH], BF16)
    dbxb = sbt("dbxb", [128, 2 * CH], BF16)
    hb = sbt("hb", [128, N * CH], BF16)
    hcb = sbt("hcb", [128, 8 * CH], BF16)
    yhalf = sbt("yhalf", [128, CH])
    yb = sbt("yb", [128, 2 * CH])
    zbuf = sbt("zbuf", [128, 2 * CH], BF16)
    if ALIAS:
        w1f = w1q[:].bitcast(F32)
        w2f = w2q[:].bitcast(F32)
        prhs = w1f[0:65, 0:2 * CH]
        pevac = w2f[0:96, 0:2 * CH]
        sfp1 = w1f[:, 2 * CH:4 * CH] if 8 * HQ * 2 >= 4 * CH * 4 else None
        sfp2 = w2f[:, 2 * CH:4 * CH] if NHT * DM * 2 >= 4 * CH * 4 else None
        if sfp1 is None:
            sfp1 = sbt("sfp1", [128, 2 * CH])
        if sfp2 is None:
            sfp2 = sbt("sfp2", [128, 2 * CH])
    else:
        prhs = sbt("prhs", [65, 2 * CH])[:]
        pevac = sbt("pevac", [96, 2 * CH])[:]
        sfp1 = sbt("sfp1", [128, 2 * CH])[:]
        sfp2 = sbt("sfp2", [128, 2 * CH])[:]
    zblk = sbt("zblk", [128, 2 * 8 * 128], BF16)
    ztm = sbt("ztm", [128, 2 * DM], BF16)
    zsc = sbt("zsc", [128, DM], BF16)
    x1bf = sbt("x1bf", [128, 2 * DM], BF16)
    x1dm = sbt("x1dm", [128, 8 * 128], BF16)
    statb = sbt("statb", [128, 16])
    smv = sbt("smv", [128, 16])
    sct = sbt("sct", [8, 128])
    ptb = sbt("ptb", [128, 8], BF16)
    TALIAS = (N * CH * 2 >= 4 * FB * 4) and (8 * CH * 2 >= FB * 4 + 2 * NTAU * 4)
    if TALIAS:
        hbf = hb[:].bitcast(F32)
        hcf = hcb[:].bitcast(F32)
        pbis = hbf[:, 0:FB]
        bmask = hbf[:, FB:2 * FB]
        bmask2 = hbf[:, 2 * FB:3 * FB]
        cumb = hbf[:, 3 * FB:4 * FB]
        ucols = hcf[:, 0:max(FB, 2 * NTAU)]
    else:
        pbis = sbt("pbis", [128, FB])[:]
        bmask = sbt("bmask", [128, FB])[:]
        bmask2 = sbt("bmask2", [128, FB])[:]
        cumb = sbt("cumb", [128, FB])[:]
        ucols = sbt("ucols", [128, max(FB, 2 * NTAU)])[:]
    bcols = sbt("bcols", [128, 8])
    tk128 = sbt("tk128", [128, PD])
    g128 = sbt("g128", [128, PD])
    u32b = sbt("u32b", [128, PD], I32)
    pairb = sbt("pairb", [128, 2 * PD])
    posu = sbt("posu", [128, 2 * NTAU], I32)
    idsb = sbt("idsb", [128, 2 * NTAU], I32)
    gsb = sbt("gsb", [128, 2 * NTAU])
    xg = sbt("xg", [128, 2 * DM], BF16)
    xgt = sbt("xgt", [128, 2 * 8 * GQ * 128], BF16)
    hgel = sbt("hgel", [128, NHT * GQ * 128], BF16)
    gx = sbt("gx", [128, GQ * 128], BF16)
    gsc1 = sbt("gsc1", [128, GQ * 128], BF16)
    gsc2 = sbt("gsc2", [128, GQ * 128], BF16)
    wrow = sbt("wrow", [128, 4 * DM], BF16)
    addr8 = sbt("addr8", [128, 8 * NBLK], I32)
    gat1 = sbt("gat1", [128, 2 * DM], BF16)
    accr = sbt("accr", [128, DM])
    x1r = sbt("x1r", [128, DM], BF16)
    z2b = sbt("z2b", [128, DM])
    outb = sbt("outb", [128, DM])

    pp_brep = pst("pp_brep", [128, CH])
    pp_crep = pst("pp_crep", [128, CH])
    O1S = max(GQ * 128, 512)
    pp_o1 = pst("pp_o1", [128, 2 * O1S])
    pp_o2 = pst("pp_o2", [128, DM])
    pp_tr = pst("pp_tr", [128, 1024], BF16)
    pp_msc = pst("pp_msc", [128, CH])

    S = lambda name: Sem(nc, name)
    s_ld = S("s_ld")
    s_ldg = S("s_ldg")
    s_xc = [S("s_xc0"), S("s_xc1")]
    s_dc = [S("s_dc0"), S("s_dc1")]
    s_bc = [S("s_bc0"), S("s_bc1")]
    s_pe = S("s_pe")
    s_act = S("s_act")
    s_dve = S("s_dve")
    s_gp = S("s_gp")
    s_sy = S("s_sy")
    s_cc = S("s_cc")
    s_z = [S("s_z0"), S("s_z1")]
    s_w1 = S("s_w1")
    s_w2 = S("s_w2")
    s_ctb = [S("s_ctb0"), S("s_ctb1")]
    s_outd = S("s_outd")
    s_pv = [S("s_pv0"), S("s_pv1")]
    s_zb = [S("s_zb0"), S("s_zb1")]
    s_xg = [S("s_xg0"), S("s_xg1")]
    s_fg = [S("s_fg0"), S("s_fg1")]
    s_ty = [S("s_ty0"), S("s_ty1")] if cfg.taps else None
    s_ps = S("s_ps")

    def dma(eng, o, i_, sem):
        eng.dma_start(out=o, in_=i_).then_inc(sem.h, 16)
        sem.v += 16
        return sem.v

    def dma_acc(o, i_, sem):
        G.dma_start(out=o, in_=i_, accum_op=OP.add).then_inc(sem.h, 16)
        sem.v += 16
        return sem.v

    dma_acc2 = dma_acc

    def mark(eng_unused, instr, sem):
        instr.then_inc(sem.h, 1)
        sem.v += 1
        return sem.v

    def vv():
        V.wait_ge(s_dve.h, s_dve.v)

    def aa():
        A.wait_ge(s_act.h, s_act.v)

    def emit_rsqrt(vin, vout, scr):
        vi = vin.bitcast(I32)
        yi = vout.bitcast(I32)
        vv()
        mark(V, V.tensor_single_scalar(yi, vi, 1, OP.arith_shift_right), s_dve)
        vv()
        mark(V, V.tensor_single_scalar(yi, yi, -1, OP.mult), s_dve)
        vv()
        mark(V, V.tensor_single_scalar(yi, yi, 0x5F3759DF, OP.add), s_dve)
        for _ in range(3):
            vv()
            mark(V, V.tensor_mul(scr, vout, vout), s_dve)
            vv()
            mark(V, V.tensor_mul(scr, scr, vin), s_dve)
            vv()
            mark(V, V.tensor_scalar(scr, scr, -0.5, 1.5, OP.mult, OP.add), s_dve)
            vv()
            mark(V, V.tensor_mul(vout, vout, scr), s_dve)

    for (dst, src) in ((acol, a_sl), (dcol, d_sl), (id32, idn32), (tokm, tokmap),
                       (metab, meta), (projwb, projW), (wdtb, wdt_ext), (jvb, jvals)):
        dma(Y, dst[:], src, s_ld)
    dma(G, gwb[:].rearrange("p (dt e) -> p dt e", e=8),
        gate_w.rearrange("(dt p) e -> p dt e", p=128), s_ldg)
    for (dst, src) in ((onehb, onehots), (idb, idnbf)):
        dma(G, dst[:], src, s_ldg)
    for (dst, src) in ((gmb, gmask), (tmb, tmask), (tk128, tokmap128)):
        dma(Y, dst[:], src, s_ld)
    LD0 = s_ld.v
    LD0G = s_ldg.v

    V.wait_ge(s_ld.h, LD0)
    V.wait_ge(s_ldg.h, LD0G)
    vv()
    mark(V, V.memset(onesc[:], 1.0), s_dve)
    vv()
    mark(V, V.memset(zerobf[:], 0.0), s_dve)
    vv()
    mark(V, V.memset(zrow32[:], 0.0), s_dve)
    vv()
    mark(V, V.memset(prhs[:], 1.0), s_dve)
    vv()
    mark(V, V.memset(smv[:], 0.0), s_dve)
    G.wait_ge(s_dve.h, s_dve.v)
    for b in range(B):
        dma(G, contrib[b][0:128, :], zerobf[:], s_gp)
        dma(G, posdense[b], zrow32[:], s_gp)
        dma(G, idg[b], zrow32[:, 0:2 * NTAU], s_gp)
    ZIDG = s_gp.v
    G.wait_ge(s_gp.h, ZIDG)

    vv()
    mark(V, V.tensor_scalar(dcol[:], dcol[:], 1.0, None, OP.add), s_dve)
    A.wait_ge(s_ld.h, LD0)
    aa()
    mark(A, A.activation(acol[:], acol[:], AF.Exp), s_act)
    V.wait_ge(s_act.h, s_act.v)
    vv()
    mark(V, V.tensor_scalar_mul(acol[:], acol[:], -1.0), s_dve)
    ACOL_RDY = s_dve.v

    G.wait_ge(s_ld.h, LD0)
    G.wait_ge(s_ldg.h, LD0G)
    bc128 = G.to_reg(128 * NTAU - 1)
    bcpd = G.to_reg(128 * PD - 1)
    regs = {"c": nc.values_load(metab[0:1, 0:1], engines=[ET.Pool],
                                min_val=0, max_val=7,
                                skip_runtime_bounds_check=True)}

    PEV = [0, 0]
    PEX = [0, 0]
    PEVD = [0, 0]
    for b in range(B):
        for ci in range(NCH):
            p = ci % 2
            csl = slice(p * CH, (p + 1) * CH)
            if b > 0 or ci >= 2:
                Y.wait_ge(s_pe.h, PEX[p])
            t_x = dma(Y, xc[:, csl], xT[b, :, ci * CH:(ci + 1) * CH], s_xc[p])
            T.wait_ge(s_xc[p].h, t_x)
            T.wait_ge(s_ld.h, LD0)
            T.wait_ge(s_ldg.h, LD0G)
            if b > 0 or ci >= 2:
                T.wait_ge(s_dve.h, PEV[p])
            PEX[p] = mark(T, T.matmul(pp_msc[0:96, 0:CH], projwb[:], xc[:, csl],
                                      start=True, stop=True), s_pe)
            V.wait_ge(s_pe.h, s_pe.v)
            if b > 0 or ci >= 2:
                V.wait_ge(s_pv[p].h, PEVD[p])
            vv()
            PEV[p] = mark(V, V.tensor_copy(pevac[:, p * CH:(p + 1) * CH],
                                           pp_msc[0:96, 0:CH]), s_dve)
            Y.wait_ge(s_dve.h, PEV[p])
            PEVD[p] = dma(Y, projpart[b][:, ci * CH:(ci + 1) * CH],
                          pevac[:, p * CH:(p + 1) * CH], s_pv[p])
        G.wait_ge(s_pv[0].h, s_pv[0].v)
        G.wait_ge(s_pv[1].h, s_pv[1].v)
        G.collective_compute("AllReduce", OP.add, RG,
                             [projpart[b]], [projAR[b]]).then_inc(s_cc.h, 1)
        s_cc.v += 1
    AR_DONE = s_cc.v

    DPRE = [0, 0]
    DSF = [0, 0]
    PRH = [0, 0]
    Y.wait_ge(s_cc.h, AR_DONE)
    for b in range(B):
        for ci in range(NCH):
            p = ci % 2
            csl = slice(p * CH, (p + 1) * CH)
            if b > 0 or ci >= 2:
                Y.wait_ge(s_pe.h, PRH[p])
            t_p = dma(Y, prhs[0:64, csl], projAR[b][0:64, ci * CH:(ci + 1) * CH], s_dc[p])
            T.wait_ge(s_dc[p].h, t_p)
            T.wait_ge(s_cc.h, AR_DONE)
            if b > 0 or ci >= 2:
                T.wait_ge(s_act.h, DPRE[p])
            PRH[p] = mark(T, T.matmul(pp_msc[:, 0:CH], wdtb[:], prhs[:, csl],
                                      start=True, stop=True), s_pe)
            A.wait_ge(s_pe.h, s_pe.v)
            if b > 0 or ci >= 2:
                A.wait_ge(s_pv[p].h, DSF[p])
            aa()
            mark(A, A.activation(sfp1[:, csl], pp_msc[:, 0:CH], AF.Abs), s_act)
            A.wait_ge(s_act.h, s_act.v)
            aa()
            mark(A, A.activation(sfp1[:, csl], sfp1[:, csl], AF.Exp, scale=-1.0), s_act)
            A.wait_ge(s_act.h, s_act.v)
            aa()
            mark(A, A.activation(sfp1[:, csl], sfp1[:, csl], AF.Ln, bias=1.0), s_act)
            aa()
            DPRE[p] = mark(A, A.activation(sfp2[:, csl], pp_msc[:, 0:CH], AF.Relu), s_act)
            V.wait_ge(s_act.h, DPRE[p])
            vv()
            mark(V, V.tensor_add(sfp2[:, csl], sfp2[:, csl], sfp1[:, csl]), s_dve)
            Y.wait_ge(s_dve.h, s_dve.v)
            DSF[p] = dma(Y, deltaD[b][:, ci * CH:(ci + 1) * CH], sfp2[:, csl], s_pv[p])
            if cfg.taps:
                DSF[p] = dma(Y, tap["delta"][b, :, ci * CH:(ci + 1) * CH], sfp2[:, csl], s_pv[p])

    state = dict(XCF=[0, 0], DCF=[0, 0], BCF=[0, 0], ZBF=[0, 0], BRC=0,
                 TRT=[0, 0], ZTME=[0, 0], YT=[0, 0], X1O=[0, 0], PST=0,
                 POST=0, ZAGTOK={}, AGTOK={}, IDSTOK={})

    def mamba_scan(b):
        G.wait_ge(s_cc.h, AR_DONE)
        Y.wait_ge(s_pv[0].h, s_pv[0].v)
        Y.wait_ge(s_pv[1].h, s_pv[1].v)
        vv()
        mark(V, V.memset(hb[:], 0.0), s_dve)
        for ci in range(NCH):
            p = ci % 2
            csl = slice(p * CH, (p + 1) * CH)
            if state["XCF"][p]:
                Y.wait_ge(s_dve.h, state["XCF"][p])
            t_x = dma(Y, xc[:, csl], xT[b, :, ci * CH:(ci + 1) * CH], s_xc[p])
            if state["DCF"][p]:
                Y.wait_ge(s_act.h, state["DCF"][p])
            t_d = dma(Y, dc[:, csl], deltaD[b][:, ci * CH:(ci + 1) * CH], s_dc[p])
            if state["BCF"][p]:
                G.wait_ge(s_pe.h, state["BCF"][p])
            dma(G, bccB[:, csl], projAR[b][64:80, ci * CH:(ci + 1) * CH], s_bc[p])
            t_b = dma(G, bccC[:, csl], projAR[b][80:96, ci * CH:(ci + 1) * CH], s_bc[p])
            V.wait_ge(s_xc[p].h, t_x)
            V.wait_ge(s_dc[p].h, t_d)
            vv()
            mark(V, V.tensor_mul(dxc[:, csl], dc[:, csl], xc[:, csl]), s_dve)
            DX_T = s_dve.v
            if cfg.taps and state["YT"][p]:
                V.wait_ge(s_ty[p].h, state["YT"][p])
            for half in range(2):
                for nn in range(8):
                    n = half * 8 + nn
                    npar = n % 2
                    nsl = slice(npar * CH, (npar + 1) * CH)
                    T.wait_ge(s_bc[p].h, t_b)
                    if state["BRC"]:
                        T.wait_ge(s_dve.h, state["BRC"])
                    mark(T, T.matmul(pp_brep[:, 0:CH], onehb[:, n * 128:(n + 1) * 128],
                                     bccB[:, csl], start=True, stop=True), s_pe)
                    PE_N = mark(T, T.matmul(pp_crep[:, 0:CH], onehb[:, n * 128:(n + 1) * 128],
                                            bccC[:, csl], start=True, stop=True), s_pe)
                    A.wait_ge(s_dc[p].h, t_d)
                    A.wait_ge(s_dve.h, ACOL_RDY)
                    if state["BRC"]:
                        A.wait_ge(s_dve.h, state["BRC"])
                    aa()
                    ACT_N = mark(A, A.activation(dab[:, nsl], dc[:, csl], AF.Exp,
                                                 scale=acol[:, n:n + 1]), s_act)
                    V.wait_ge(s_pe.h, PE_N)
                    vv()
                    mark(V, V.tensor_mul(dbxb[:, nsl], dxc[:, csl], pp_brep[:, 0:CH]), s_dve)
                    V.wait_ge(s_act.h, ACT_N)
                    vv()
                    mark(V, V.tensor_tensor_scan(
                        hb[:, n * CH:(n + 1) * CH], dab[:, nsl], dbxb[:, nsl],
                        hb[:, (n + 1) * CH - 1:(n + 1) * CH], OP.mult, OP.add), s_dve)
                    vv()
                    state["BRC"] = mark(V, V.tensor_mul(
                        hcb[:, nn * CH:(nn + 1) * CH],
                        hb[:, n * CH:(n + 1) * CH], pp_crep[:, 0:CH]), s_dve)
                for st in (4, 2, 1):
                    for kk in range(st):
                        vv()
                        mark(V, V.tensor_add(hcb[:, kk * CH:(kk + 1) * CH],
                                             hcb[:, kk * CH:(kk + 1) * CH],
                                             hcb[:, (kk + st) * CH:(kk + st + 1) * CH]), s_dve)
                if half == 0:
                    vv()
                    mark(V, V.tensor_copy(yhalf[:], hcb[:, 0:CH]), s_dve)
                else:
                    vv()
                    mark(V, V.tensor_add(yb[:, csl], yhalf[:], hcb[:, 0:CH]), s_dve)
            state["BCF"][p] = s_pe.v
            state["DCF"][p] = s_act.v
            if cfg.taps:
                Y.wait_ge(s_dve.h, s_dve.v)
                state["YT"][p] = dma(Y, tap["y"][b, :, ci * CH:(ci + 1) * CH],
                                     yb[:, csl], s_ty[p])
            if state["ZBF"][p]:
                V.wait_ge(s_z[p].h, state["ZBF"][p])
            vv()
            mark(V, V.scalar_tensor_tensor(zbuf[:, csl], xc[:, csl], dcol[:, 0:1],
                                           yb[:, csl], OP.mult, OP.add), s_dve)
            state["XCF"][p] = s_dve.v
            Y.wait_ge(s_dve.h, s_dve.v)
            for k in range(CH // 128):
                blk = ci * (CH // 128) + k
                dstcol = (blk % 8) * TOK + (blk // 8) * 128
                state["ZBF"][p] = dma(Y, zin[b][:, dstcol:dstcol + 128],
                                      zbuf[:, csl.start + k * 128:csl.start + (k + 1) * 128],
                                      s_z[p])
            yield
        G.wait_ge(s_z[0].h, s_z[0].v)
        G.wait_ge(s_z[1].h, s_z[1].v)
        G.collective_compute("AllGather", OP.bypass, RG,
                             [zin[b]], [zAG[b]]).then_inc(s_cc.h, 1)
        s_cc.v += 1
        state["ZAGTOK"][b] = s_cc.v

    def owner_ln1(b):
        yield
        G.wait_ge(s_cc.h, state["ZAGTOK"][b])
        t_zown = dma(G, zownD[b], zAG[b][:, bass.ts(regs["c"], TOK)], s_gp)
        zsrc = zownD[b].rearrange("(dt p) t -> p dt t", p=128)
        G.wait_ge(s_gp.h, t_zown)
        for i in range(NBLK):
            yield
            p = i % 2
            sl1 = slice(p * DM, (p + 1) * DM)
            if state["TRT"][p]:
                G.wait_ge(s_pe.h, state["TRT"][p])
            t_z = dma(G, zblk[:, p * 1024:(p + 1) * 1024],
                      zsrc[:, :, i * 128:(i + 1) * 128], s_zb[p])
            T.wait_ge(s_zb[p].h, t_z)
            T.wait_ge(s_dve.h, s_dve.v)
            for dt in range(8):
                mark(T, T.transpose(pp_tr[:, dt * 128:(dt + 1) * 128],
                                    zblk[:, p * 1024 + dt * 128:p * 1024 + (dt + 1) * 128],
                                    idb[:]), s_pe)
            state["TRT"][p] = s_pe.v
            V.wait_ge(s_pe.h, s_pe.v)
            vv()
            mark(V, V.tensor_copy(ztm[:, sl1], pp_tr[:, 0:1024]), s_dve)
            A.wait_ge(s_dve.h, s_dve.v)
            aa()
            mark(A, A.activation(zsc[:], ztm[:, sl1], AF.Copy, accum_out=statb[:, 0:1]), s_act)
            aa()
            mark(A, A.activation(zsc[:], ztm[:, sl1], AF.Square, accum_out=statb[:, 1:2]), s_act)
            V.wait_ge(s_act.h, s_act.v)
            vv()
            mark(V, V.tensor_scalar_mul(statb[:, 2:3], statb[:, 0:1], 1.0 / DM), s_dve)
            vv()
            mark(V, V.tensor_mul(statb[:, 3:4], statb[:, 2:3], statb[:, 2:3]), s_dve)
            vv()
            mark(V, V.tensor_scalar(statb[:, 4:5], statb[:, 1:2], 1.0 / DM, EPS,
                                    OP.mult, OP.add), s_dve)
            vv()
            mark(V, V.tensor_sub(statb[:, 4:5], statb[:, 4:5], statb[:, 3:4]), s_dve)
            emit_rsqrt(statb[:, 4:5], statb[:, 5:6], statb[:, 6:7])
            vv()
            if state["X1O"][p]:
                V.wait_ge(s_pv[p].h, state["X1O"][p])
            mark(V, V.tensor_scalar(x1bf[:, sl1], ztm[:, sl1], statb[:, 2:3],
                                    statb[:, 5:6], OP.subtract, OP.mult), s_dve)
            X1_T = s_dve.v
            Y.wait_ge(s_dve.h, X1_T)
            state["X1O"][p] = dma(Y, x1own[b][i * 128:(i + 1) * 128, :], x1bf[:, sl1], s_pv[p])
            T.wait_ge(s_dve.h, X1_T)
            for dt in range(8):
                mark(T, T.transpose(pp_tr[:, dt * 128:(dt + 1) * 128],
                                    x1bf[:, p * DM + dt * 128:p * DM + (dt + 1) * 128],
                                    idb[:]), s_pe)
            V.wait_ge(s_pe.h, s_pe.v)
            vv()
            mark(V, V.tensor_copy(x1dm[:], pp_tr[:, 0:1024]), s_dve)
            T.wait_ge(s_dve.h, s_dve.v)
            for dt in range(8):
                mark(T, T.matmul(pp_msc[0:8, 0:128], gwb[:, dt * 8:(dt + 1) * 8],
                                 x1dm[:, dt * 128:(dt + 1) * 128],
                                 start=(dt == 0), stop=(dt == 7)), s_pe)
            V.wait_ge(s_pe.h, s_pe.v)
            vv()
            if state["PST"]:
                V.wait_ge(s_ps.h, state["PST"])
            mark(V, V.tensor_copy(sct[:], pp_msc[0:8, 0:128]), s_dve)
            T.wait_ge(s_dve.h, s_dve.v)
            mark(T, T.transpose(pp_msc[0:128, 128:136], sct[:], id32[0:8, 0:8]), s_pe)
            V.wait_ge(s_pe.h, s_pe.v)
            vv()
            mark(V, V.tensor_copy(smv[:, 0:8], pp_msc[0:128, 128:136]), s_dve)
            vv()
            mark(V, V.reduce_max(smv[:, 8:9], smv[:, 0:8], axis=mybir.AxisListType.X), s_dve)
            vv()
            mark(V, V.tensor_scalar_mul(smv[:, 9:10], smv[:, 8:9], -1.0), s_dve)
            A.wait_ge(s_dve.h, s_dve.v)
            aa()
            mark(A, A.activation(smv[:, 0:8], smv[:, 0:8], AF.Exp,
                                 bias=smv[:, 9:10], accum_out=smv[:, 10:11]), s_act)
            V.wait_ge(s_act.h, s_act.v)
            vv()
            mark(V, V.reciprocal(smv[:, 11:12], smv[:, 10:11]), s_dve)
            vv()
            mark(V, V.tensor_scalar(ptb[:, 0:8], smv[:, 0:8], smv[:, 11:12], None,
                                    OP.mult), s_dve)
            T.wait_ge(s_dve.h, s_dve.v)
            mark(T, T.transpose(pp_tr[0:8, 0:128], ptb[:, 0:8], idb[:]), s_pe)
            V.wait_ge(s_pe.h, s_pe.v)
            vv()
            mark(V, V.tensor_copy(sct[:], pp_tr[0:8, 0:128]), s_dve)
            Y.wait_ge(s_dve.h, s_dve.v)
            state["PST"] = dma(Y, probsT[b][:, i * 128:(i + 1) * 128], sct[:], s_ps)
        G.wait_ge(s_pv[0].h, s_pv[0].v)
        G.wait_ge(s_pv[1].h, s_pv[1].v)
        G.wait_ge(s_ps.h, s_ps.v)
        G.collective_compute("AllGather", OP.bypass, RG, [x1own[b]], [x1AG[b]]).then_inc(s_cc.h, 1)
        s_cc.v += 1
        G.collective_compute("AllGather", OP.bypass, RG, [probsT[b]], [probsAG[b]]).then_inc(s_cc.h, 1)
        s_cc.v += 1
        if cfg.taps:
            G.wait_ge(s_cc.h, s_cc.v)
            dma(G, tap["x1"][b], x1AG[b], s_gp)
            dma(G, tap["probs"][b], probsAG[b], s_gp)
            dma(G, tap["z"][b], zAG[b], s_gp)
            G.wait_ge(s_gp.h, s_gp.v)
        state["AGTOK"][b] = s_cc.v

    def topk(b):
        boff = (b % 2) * NTAU
        yield
        G.wait_ge(s_cc.h, state["AGTOK"][b])
        dma(G, probsAGx2[b][0:64, :], probsAG[b], s_gp)
        dma(G, probsAGx2[b][64:128, :], probsAG[b], s_gp)
        t_x2 = s_gp.v
        G.wait_ge(s_gp.h, t_x2)
        dma(G, probsRot[b], probsAGx2[b][bass.ds(regs["c"], 64), :], s_gp)
        G.wait_ge(s_gp.h, s_gp.v)
        srcv = probsRot[b].rearrange("(r g) (h f) -> g r h f", g=8, h=2)
        for g in range(8):
            dma(G, pbis[16 * g:16 * (g + 1), :], srcv[g:g + 1, :, :, :], s_gp)
        PB_T = s_gp.v
        V.wait_ge(s_gp.h, PB_T)
        vv()
        mark(V, V.memset(bcols[:, 0:1], 0.0), s_dve)
        vv()
        mark(V, V.memset(bcols[:, 1:2], 1.0), s_dve)
        for it in range(BISECT_ITERS):
            yield
            vv()
            mark(V, V.tensor_add(bcols[:, 2:3], bcols[:, 0:1], bcols[:, 1:2]), s_dve)
            vv()
            mark(V, V.tensor_scalar_mul(bcols[:, 2:3], bcols[:, 2:3], 0.5), s_dve)
            vv()
            mark(V, V.tensor_scalar(bmask[:], pbis[:], bcols[:, 2:3], 0.0,
                                    OP.is_gt, OP.add, accum_out=bcols[:, 3:4]), s_dve)
            T.wait_ge(s_dve.h, s_dve.v)
            mark(T, T.matmul(pp_msc[0:128, 0:1], gmb[:], bcols[:, 3:4],
                             start=True, stop=True), s_pe)
            V.wait_ge(s_pe.h, s_pe.v)
            vv()
            mark(V, V.tensor_copy(bcols[:, 4:5], pp_msc[0:128, 0:1]), s_dve)
            vv()
            mark(V, V.tensor_scalar(bmask2[:, 0:1], bcols[:, 4:5], float(K), None,
                                    OP.is_ge), s_dve)
            vv()
            mark(V, V.tensor_sub(bcols[:, 5:6], bcols[:, 2:3], bcols[:, 0:1]), s_dve)
            vv()
            mark(V, V.tensor_mul(bcols[:, 5:6], bcols[:, 5:6], bmask2[:, 0:1]), s_dve)
            vv()
            mark(V, V.tensor_add(bcols[:, 0:1], bcols[:, 0:1], bcols[:, 5:6]), s_dve)
            vv()
            mark(V, V.tensor_sub(bcols[:, 5:6], bcols[:, 1:2], bcols[:, 2:3]), s_dve)
            vv()
            mark(V, V.tensor_mul(bcols[:, 5:6], bcols[:, 5:6], bmask2[:, 0:1]), s_dve)
            vv()
            mark(V, V.tensor_add(bcols[:, 1:2], bcols[:, 2:3], bcols[:, 5:6]), s_dve)
        yield
        vv()
        mark(V, V.tensor_scalar(bmask[:], pbis[:], bcols[:, 1:2], 0.0,
                                OP.is_gt, OP.add, accum_out=bcols[:, 3:4]), s_dve)
        T.wait_ge(s_dve.h, s_dve.v)
        mark(T, T.matmul(pp_msc[0:128, 0:1], gmb[:], bcols[:, 3:4],
                         start=True, stop=True), s_pe)
        V.wait_ge(s_pe.h, s_pe.v)
        vv()
        mark(V, V.tensor_copy(bcols[:, 5:6], pp_msc[0:128, 0:1]), s_dve)
        vv()
        mark(V, V.tensor_scalar(bmask[:], pbis[:], bcols[:, 1:2], None, OP.is_gt), s_dve)
        vv()
        mark(V, V.tensor_scalar(bmask2[:], pbis[:], bcols[:, 0:1], None, OP.is_gt), s_dve)
        vv()
        mark(V, V.tensor_sub(bmask2[:], bmask2[:], bmask[:]), s_dve)
        vv()
        mark(V, V.tensor_tensor_scan(cumb[:], onesc[:, 0:FB], bmask2[:], 0.0,
                                     OP.mult, OP.add), s_dve)
        vv()
        mark(V, V.tensor_copy(bcols[:, 6:7], cumb[:, FB - 1:FB]), s_dve)
        T.wait_ge(s_dve.h, s_dve.v)
        mark(T, T.matmul(pp_msc[0:128, 0:1], tmb[:], bcols[:, 6:7],
                         start=True, stop=True), s_pe)
        V.wait_ge(s_pe.h, s_pe.v)
        vv()
        mark(V, V.tensor_copy(bcols[:, 7:8], pp_msc[0:128, 0:1]), s_dve)
        vv()
        mark(V, V.tensor_scalar(cumb[:], cumb[:], bcols[:, 7:8], -1.0, OP.add, OP.add), s_dve)
        vv()
        mark(V, V.tensor_scalar_mul(bcols[:, 6:7], bcols[:, 5:6], -1.0), s_dve)
        vv()
        mark(V, V.tensor_scalar(bcols[:, 6:7], bcols[:, 6:7], float(K), None, OP.add), s_dve)
        vv()
        mark(V, V.tensor_scalar(cumb[:], cumb[:], bcols[:, 6:7], None, OP.is_lt), s_dve)
        vv()
        mark(V, V.tensor_mul(cumb[:], cumb[:], bmask2[:]), s_dve)
        vv()
        mark(V, V.tensor_add(bmask[:], bmask[:], cumb[:]), s_dve)
        if cfg.taps:
            Y.wait_ge(s_dve.h, s_dve.v)
            dma(Y, tap["bis"][b], bcols[:], s_sy)
        yield
        vv()
        mark(V, V.tensor_tensor_scan(cumb[:], onesc[:, 0:FB], bmask[:], 0.0,
                                     OP.mult, OP.add), s_dve)
        vv()
        mark(V, V.tensor_copy(bcols[:, 6:7], cumb[:, FB - 1:FB]), s_dve)
        T.wait_ge(s_dve.h, s_dve.v)
        mark(T, T.matmul(pp_msc[0:128, 0:1], tmb[:], bcols[:, 6:7],
                         start=True, stop=True), s_pe)
        V.wait_ge(s_pe.h, s_pe.v)
        vv()
        mark(V, V.tensor_copy(bcols[:, 7:8], pp_msc[0:128, 0:1]), s_dve)
        vv()
        mark(V, V.tensor_scalar(cumb[:], cumb[:], bcols[:, 7:8], -1.0, OP.add, OP.add), s_dve)
        vv()
        mark(V, V.tensor_mul(ucols[:, 0:FB], cumb[:], bmask[:]), s_dve)
        vv()
        mark(V, V.tensor_scalar(bmask2[:], bmask[:], 0.5, 1e9, OP.is_lt, OP.mult), s_dve)
        vv()
        mark(V, V.tensor_add(ucols[:, 0:FB], ucols[:, 0:FB], bmask2[:]), s_dve)
        ui = ucols.bitcast(I32)
        ci2 = cumb.bitcast(I32)
        vv()
        mark(V, V.tensor_copy(ui[:, 0:FB], ucols[:, 0:FB]), s_dve)
        vv()
        mark(V, V.tensor_single_scalar(ci2[:, 0:FB], ui[:, 0:FB], 127, OP.bitwise_and), s_dve)
        vv()
        mark(V, V.tensor_single_scalar(ui[:, 0:FB], ui[:, 0:FB], 7, OP.arith_shift_right), s_dve)
        vv()
        mark(V, V.tensor_single_scalar(ci2[:, 0:FB], ci2[:, 0:FB], NTAU, OP.mult), s_dve)
        vv()
        mark(V, V.tensor_tensor(ui[:, 0:FB], ui[:, 0:FB], ci2[:, 0:FB], OP.add), s_dve)
        UP_T = s_dve.v
        Y.wait_ge(s_dve.h, UP_T)
        dma(Y, ubounce[b], ui[0:16, 0:FB], s_sy)
        dma(Y, gbounce[b], pbis[0:16, :], s_sy)
        Y.wait_ge(s_sy.h, s_sy.v)
        dma(Y, u32b[:], ubounce[b].rearrange("a (p c) -> (a p) c", p=8), s_sy)
        t_g128 = dma(Y, g128[:], gbounce[b].rearrange("a (p c) -> (a p) c", p=8), s_sy)
        V.wait_ge(s_sy.h, t_g128)
        V.wait_ge(s_ld.h, LD0)
        vv()
        mark(V, V.tensor_copy(pairb[:, 0:2 * PD:2], tk128[:]), s_dve)
        vv()
        mark(V, V.tensor_copy(pairb[:, 1:2 * PD:2], g128[:]), s_dve)
        t_rb = s_sy.v
        Y.wait_ge(s_dve.h, s_dve.v)
        G.wait_ge(s_sy.h, t_rb)
        G.wait_ge(s_dve.h, s_dve.v)
        G.wait_ge(s_gp.h, ZIDG)
        idgflat = idg[b].rearrange("p c -> (p c)").rearrange("(x o) -> x o", o=2)
        for q in range(PD):
            if q % 16 == 0:
                yield
            G.indirect_dma_start(
                out=idgflat, out_offset=IndirectOffsetOnAxis(u32b[:, q:q + 1], 0),
                in_=pairb[:, 2 * q:2 * q + 2], in_offset=None,
                bounds_check=bc128, oob_is_err=False,
            ).then_inc(s_gp.h, 16)
            s_gp.v += 16
        IDG_T = s_gp.v
        if cfg.taps:
            G.wait_ge(s_gp.h, IDG_T)
            dma(G, tap["idg"][b], idg[b], s_gp)
            G.wait_ge(s_gp.h, s_gp.v)
        Y.wait_ge(s_gp.h, IDG_T)
        t_ig = dma(Y, ucols[:, 0:2 * NTAU], idg[b], s_sy)
        V.wait_ge(s_sy.h, t_ig)
        vv()
        mark(V, V.tensor_copy(idsb[:, boff:boff + NTAU], ucols[:, 0:2 * NTAU:2]), s_dve)
        vv()
        mark(V, V.tensor_copy(gsb[:, boff:boff + NTAU], ucols[:, 1:2 * NTAU:2]), s_dve)
        state["IDSTOK"][b] = s_dve.v

    def ffn(b, ids_tok, aux=None, aux_every=1):
        boff = (b % 2) * NTAU
        step = 0
        O2C = [0]
        HGU = [0]
        WR_EV = [0, 0]
        XGU = [0, 0]
        XGP = [0, 0]
        GEL_T = [0, 0]
        PSR_T = [0, 0]
        GEL_ALL = 0
        O1_T = [0, 0]
        ctb_tok = [0] * (NTAU // 2)
        for hq in range(4):
            G.wait_ge(s_pe.h, s_pe.v)
            w1_tok = dma(G, w1q[:].rearrange("p (dt h) -> p dt h", dt=8),
                         w1e.rearrange("(dt p) h -> p dt h", p=128)
                         [:, :, hq * HQ:(hq + 1) * HQ], s_w1)
            w2_tok = dma(G, w2q[:].rearrange("p (a d) -> p a d", d=DM),
                         w2e.rearrange("(a p) d -> p a d", p=128)
                         [:, hq * NHT:(hq + 1) * NHT, :], s_w2)
            if cfg.taps and b == 0 and hq == 1:
                G.wait_ge(s_ctb[0].h, s_ctb[0].v)
                G.wait_ge(s_ctb[1].h, s_ctb[1].v)
                dma(G, tap["c0"][0], contrib[b], s_gp)
                G.wait_ge(s_gp.h, s_gp.v)
            if cfg.taps and b == 0 and hq == 0:
                G.wait_ge(s_w1.h, w1_tok)
                G.wait_ge(s_w2.h, w2_tok)
                dma(G, tap["w1q"][0][:, 0:8 * HQ], w1q[:], s_gp)
                dma(G, tap["w2q"][0][:, 0:NHT * DM], w2q[:], s_gp)
                G.wait_ge(s_gp.h, s_gp.v)
            for g in range(NG):
                if aux is not None and step % aux_every == 0:
                    next(aux, None)
                step += 1
                gp = g % 2
                for tt in range(GQ):
                    tau = g * GQ + tt
                    tp = tau % 2
                    if aux is not None and hq >= 2:
                        next(aux, None)
                    G.wait_ge(s_dve.h, ids_tok)
                    if XGP[tp]:
                        G.wait_ge(s_pe.h, XGP[tp])
                    G.indirect_dma_start(
                        out=xg[:, tp * DM:(tp + 1) * DM], out_offset=None,
                        in_=x1AG[b],
                        in_offset=IndirectOffsetOnAxis(idsb[:, boff + tau:boff + tau + 1], 0),
                    ).then_inc(s_xg[tp].h, 16)
                    s_xg[tp].v += 16
                    t_xg = s_xg[tp].v
                    T.wait_ge(s_xg[tp].h, t_xg)
                    T.wait_ge(s_dve.h, s_dve.v)
                    for dt in range(8):
                        mark(T, T.transpose(pp_tr[:, dt * 128:(dt + 1) * 128],
                                            xg[:, tp * DM + dt * 128:tp * DM + (dt + 1) * 128],
                                            idb[:]), s_pe)
                    XGP[tp] = s_pe.v
                    V.wait_ge(s_pe.h, s_pe.v)
                    if XGU[gp]:
                        V.wait_ge(s_pe.h, XGU[gp])
                    vv()
                    xgt_dst = xgt[:, gp * 8 * GQ * 128:(gp + 1) * 8 * GQ * 128] \
                        .rearrange("p (dt t c) -> p dt t c", dt=8, t=GQ, c=128)[:, :, tt, :]
                    mark(V, V.tensor_copy(
                        xgt_dst,
                        pp_tr[:, 0:1024].rearrange("p (dt c) -> p dt c", dt=8, c=128)), s_dve)
                XGT_T = s_dve.v
                for ht in range(NHT):
                    hp = ht % 2
                    o1sl = slice(hp * O1S, hp * O1S + GQ * 128)
                    if ht == 0:
                        T.wait_ge(s_w1.h, w1_tok)
                        T.wait_ge(s_dve.h, XGT_T)
                    if PSR_T[hp]:
                        T.wait_ge(s_act.h, PSR_T[hp])
                    for dt in range(8):
                        mark(T, T.matmul(pp_o1[:, o1sl],
                                         w1q[:, dt * HQ + ht * 128:dt * HQ + (ht + 1) * 128],
                                         xgt[:, (gp * 8 + dt) * GQ * 128:
                                                (gp * 8 + dt + 1) * GQ * 128],
                                         start=(dt == 0), stop=(dt == 7)), s_pe)
                    O1_T[hp] = s_pe.v
                    XGU[gp] = s_pe.v
                    A.wait_ge(s_pe.h, O1_T[hp])
                    if GEL_ALL:
                        A.wait_ge(s_dve.h, GEL_ALL)
                    aa()
                    mark(A, A.activation(gx[:], pp_o1[:, o1sl], AF.Copy), s_act)
                    aa()
                    PSR_T[hp] = mark(A, A.activation(gsc1[:], pp_o1[:, o1sl], AF.Square), s_act)
                    aa()
                    mark(A, A.activation(gsc1[:], gsc1[:], AF.Copy, scale=GK, bias=SQ2PI), s_act)
                    V.wait_ge(s_act.h, s_act.v)
                    vv()
                    mark(V, V.tensor_mul(gsc2[:], gx[:], gsc1[:]), s_dve)
                    A.wait_ge(s_dve.h, s_dve.v)
                    aa()
                    mark(A, A.activation(gsc2[:], gsc2[:], AF.Tanh), s_act)
                    aa()
                    TH_T = mark(A, A.activation(gsc2[:], gsc2[:], AF.Copy,
                                                scale=0.5, bias=0.5), s_act)
                    V.wait_ge(s_act.h, TH_T)
                    vv()
                    if HGU[0]:
                        V.wait_ge(s_pe.h, HGU[0])
                    GEL_T[hp] = mark(V, V.tensor_mul(hgel[:, ht * GQ * 128:(ht + 1) * GQ * 128],
                                                     gx[:], gsc2[:]), s_dve)
                    GEL_ALL = GEL_T[hp]
                if cfg.taps and b == 0 and hq == 0 and g == 0:
                    Y.wait_ge(s_dve.h, s_dve.v)
                    dma(Y, tap["xgt"][0][:, 0:8 * GQ * 128],
                        xgt[:, 0:8 * GQ * 128], s_sy)
                    dma(Y, tap["hgel"][0][:, 0:NHT * GQ * 128], hgel[:], s_sy)
                    Y.wait_ge(s_sy.h, s_sy.v)
                    V.wait_ge(s_sy.h, s_sy.v)
                for tt in range(GQ):
                    tau = g * GQ + tt
                    pr = (tau // 2) % 2
                    pi = tau % 2
                    for ht in range(NHT):
                        if ht == 0:
                            T.wait_ge(s_dve.h, GEL_ALL)
                            T.wait_ge(s_w2.h, w2_tok)
                            if O2C[0]:
                                T.wait_ge(s_act.h, O2C[0])
                            if WR_EV[pr] and pi == 0:
                                T.wait_ge(s_ctb[pr].h, WR_EV[pr])
                        for h2 in range(2):
                            mark(T, T.matmul(pp_o2[:, h2 * 512:(h2 + 1) * 512],
                                             hgel[:, ht * GQ * 128 + tt * 128:
                                                    ht * GQ * 128 + (tt + 1) * 128],
                                             w2q[:, ht * DM + h2 * 512:ht * DM + (h2 + 1) * 512],
                                             start=(ht == 0), stop=(ht == NHT - 1)), s_pe)
                    HGU[0] = s_pe.v
                    A.wait_ge(s_pe.h, s_pe.v)
                    if WR_EV[pr] and pi == 0:
                        A.wait_ge(s_ctb[pr].h, WR_EV[pr])
                    aa()
                    O2C[0] = mark(A, A.activation(
                        wrow[:, (pr * 2 + pi) * DM:(pr * 2 + pi + 1) * DM],
                        pp_o2[:, 0:DM], AF.Copy), s_act)
                    if cfg.taps and b == 0 and hq == 0 and g == 0 and tau == 0:
                        Y.wait_ge(s_dve.h, s_dve.v)
                        dma(Y, tap["wrow"][0][:, 0:DM], wrow[:, 0:DM], s_sy)
                        Y.wait_ge(s_sy.h, s_sy.v)
                        V.wait_ge(s_sy.h, s_sy.v)
                    if pi == 1:
                        G.wait_ge(s_act.h, s_act.v)
                        G.wait_ge(s_ctb[pr].h, s_ctb[pr].v)
                        rows = slice(128 + (tau - 1) * 128, 128 + (tau + 1) * 128)
                        srcv = wrow[:, pr * 2 * DM:(pr + 1) * 2 * DM].rearrange(
                            "p (t d) -> p t d", d=DM)
                        dstv = contrib[b][rows, :].rearrange("(t p) d -> p t d", p=128)
                        if hq == 0:
                            ctb_tok[tau // 2] = dma(G, dstv, srcv, s_ctb[pr])
                        else:
                            ctb_tok[tau // 2] = dma_acc(dstv, srcv, s_ctb[pr])
                        WR_EV[pr] = ctb_tok[tau // 2]
        GT_T = [0, 0]
        for pp2 in range(NTAU // 2):
            pr = pp2 % 2
            rows = slice(128 + pp2 * 256, 128 + (pp2 + 1) * 256)
            G.wait_ge(s_ctb[pr].h, s_ctb[pr].v)
            t_r = dma(G, wrow[:, pr * 2 * DM:(pr + 1) * 2 * DM].rearrange(
                "p (t d) -> p t d", d=DM),
                contrib[b][rows, :].rearrange("(t p) d -> p t d", p=128), s_gp)
            V.wait_ge(s_gp.h, t_r)
            for tt in range(2):
                tau = pp2 * 2 + tt
                vv()
                mark(V, V.tensor_scalar(
                    wrow[:, (pr * 2 + tt) * DM:(pr * 2 + tt + 1) * DM],
                    wrow[:, (pr * 2 + tt) * DM:(pr * 2 + tt + 1) * DM],
                    gsb[:, boff + tau:boff + tau + 1], None, OP.mult), s_dve)
            G.wait_ge(s_dve.h, s_dve.v)
            GT_T[pr] = dma(G, contrib[b][rows, :].rearrange("(t p) d -> p t d", p=128),
                           wrow[:, pr * 2 * DM:(pr + 1) * 2 * DM].rearrange(
                               "p (t d) -> p t d", d=DM), s_ctb[pr])
            ctb_tok[pp2] = GT_T[pr]
        uu = posu[:]
        if state["POST"]:
            V.wait_ge(s_gp.h, state["POST"])
        vv()
        mark(V, V.tensor_single_scalar(uu[:, 0:NTAU], idsb[:, boff:boff + NTAU],
                                       127, OP.bitwise_and), s_dve)
        vv()
        mark(V, V.tensor_single_scalar(uu[:, 0:NTAU], uu[:, 0:NTAU], PD, OP.mult), s_dve)
        vv()
        mark(V, V.tensor_single_scalar(uu[:, NTAU:2 * NTAU], idsb[:, boff:boff + NTAU], 7,
                                       OP.arith_shift_right), s_dve)
        vv()
        mark(V, V.tensor_tensor(uu[:, 0:NTAU], uu[:, 0:NTAU], uu[:, NTAU:2 * NTAU],
                                OP.add), s_dve)
        PJ_T = s_dve.v
        G.wait_ge(s_dve.h, PJ_T)
        pdflat = posdense[b].rearrange("p c -> (p c)").rearrange("(x o) -> x o", o=1)
        for tau in range(NTAU):
            G.indirect_dma_start(
                out=pdflat, out_offset=IndirectOffsetOnAxis(uu[:, tau:tau + 1], 0),
                in_=jvb[:, tau:tau + 1], in_offset=None,
                bounds_check=bcpd, oob_is_err=False,
            ).then_inc(s_gp.h, 16)
            s_gp.v += 16
        POS_T = s_gp.v
        state["POST"] = POS_T
        G.wait_ge(s_ctb[0].h, s_ctb[0].v)
        G.wait_ge(s_ctb[1].h, s_ctb[1].v)
        G.collective_compute("AllGather", OP.bypass, RG,
                             [contrib[b]], [contribAG[b]]).then_inc(s_cc.h, 1)
        s_cc.v += 1
        G.wait_ge(s_gp.h, POS_T)
        G.collective_compute("AllGather", OP.bypass, RG,
                             [posdense[b]], [posAG[b]]).then_inc(s_cc.h, 1)
        s_cc.v += 1
        if cfg.taps:
            G.wait_ge(s_cc.h, s_cc.v)
            dma(G, tap["contrib"][b], contrib[b], s_gp)
            dma(G, tap["pos"][b], posAG[b], s_gp)
            G.wait_ge(s_gp.h, s_gp.v)
        return s_cc.v

    def finalize(b, cc_tok):
        Y.wait_ge(s_sy.h, s_sy.v)
        G.wait_ge(s_cc.h, cc_tok)
        t_pos = dma(G, posownD[b], posAG[b][:, bass.ts(regs["c"], NBLK)], s_gp)
        G.wait_ge(s_gp.h, t_pos)
        t_a8 = dma(G, addr8[:].rearrange("p (e n) -> p e n", n=NBLK),
                   posownD[b].rearrange("(e p) n -> p e n", p=128), s_gp)
        G.wait_ge(s_gp.h, t_a8)
        ADD_T = [0, 0]
        PRV = 0
        for i in range(NBLK):
            if PRV:
                Y.wait_ge(s_dve.h, PRV)
            t_x1 = dma(Y, x1r[:, 0:DM], x1own[b][i * 128:(i + 1) * 128, :], s_sy)
            for e in range(8):
                ip = e % 2
                sl = slice(ip * DM, (ip + 1) * DM)
                if ADD_T[ip]:
                    G.wait_ge(s_dve.h, ADD_T[ip])
                G.indirect_dma_start(
                    out=gat1[:, sl], out_offset=None,
                    in_=contribAG[b],
                    in_offset=IndirectOffsetOnAxis(addr8[:, e * NBLK + i:e * NBLK + i + 1], 0),
                ).then_inc(s_fg[ip].h, 16)
                s_fg[ip].v += 16
                V.wait_ge(s_fg[ip].h, s_fg[ip].v)
                if e == 0:
                    vv()
                    ADD_T[ip] = mark(V, V.tensor_copy(accr[:], gat1[:, sl]), s_dve)
                else:
                    vv()
                    ADD_T[ip] = mark(V, V.tensor_add(accr[:], accr[:], gat1[:, sl]), s_dve)
            V.wait_ge(s_sy.h, t_x1)
            V.wait_ge(s_act.h, s_act.v)
            V.wait_ge(s_outd.h, s_outd.v)
            vv()
            PRV = mark(V, V.tensor_add(z2b[:], x1r[:, 0:DM], accr[:]), s_dve)
            A.wait_ge(s_dve.h, s_dve.v)
            aa()
            mark(A, A.activation(zsc[:], z2b[:], AF.Copy, accum_out=statb[:, 0:1]), s_act)
            aa()
            mark(A, A.activation(zsc[:], z2b[:], AF.Square, accum_out=statb[:, 1:2]), s_act)
            V.wait_ge(s_act.h, s_act.v)
            vv()
            mark(V, V.tensor_scalar_mul(statb[:, 2:3], statb[:, 0:1], 1.0 / DM), s_dve)
            vv()
            mark(V, V.tensor_mul(statb[:, 3:4], statb[:, 2:3], statb[:, 2:3]), s_dve)
            vv()
            mark(V, V.tensor_scalar(statb[:, 4:5], statb[:, 1:2], 1.0 / DM, EPS,
                                    OP.mult, OP.add), s_dve)
            vv()
            mark(V, V.tensor_sub(statb[:, 4:5], statb[:, 4:5], statb[:, 3:4]), s_dve)
            emit_rsqrt(statb[:, 4:5], statb[:, 5:6], statb[:, 6:7])
            vv()
            mark(V, V.tensor_scalar(outb[:], z2b[:], statb[:, 2:3],
                                    statb[:, 5:6], OP.subtract, OP.mult), s_dve)
            Y.wait_ge(s_dve.h, s_dve.v)
            dma(Y, out[b, i], outb[:], s_outd)
            yield

    def chain2(*gens):
        for gg in gens:
            for _ in gg:
                yield

    SA = cfg.stop_after
    if SA is not None:
        for b in range(B):
            for _ in mamba_scan(b):
                pass
            if SA == "mamba":
                continue
            for _ in owner_ln1(b):
                pass
            if SA == "owner":
                continue
            for _ in topk(b):
                pass
            if SA == "topk":
                continue
            cctok = ffn(b, state["IDSTOK"][b])
            if SA == "ffn":
                continue
            for _ in finalize(b, cctok):
                pass
    else:
        g0 = mamba_scan(0)
        for _ in g0:
            pass
        for _ in owner_ln1(0):
            pass
        for _ in topk(0):
            pass
        aux0 = chain2(mamba_scan(1), owner_ln1(1), topk(1))
        c0 = ffn(0, state["IDSTOK"][0], aux=aux0)
        for _ in aux0:
            pass
        f0 = finalize(0, c0)
        c1 = ffn(1, state["IDSTOK"][1], aux=f0, aux_every=2)
        for _ in f0:
            pass
        for _ in finalize(1, c1):
            pass

    if SA is not None:
        for e_, sm in ((V, s_dve), (A, s_act), (T, s_pe), (G, s_gp), (Y, s_sy)):
            e_.wait_ge(sm.h, sm.v)
        G.wait_ge(s_cc.h, s_cc.v)
        Y.wait_ge(s_dve.h, s_dve.v)
        dma(Y, out[0, 0], outb[:], s_outd)
    Y.wait_ge(s_outd.h, s_outd.v)
    ctx.close()
    return nc, tap




def make_inputs(cfg, x, A_log, W_delta, W_dt, b_dt, W_B, W_C, Dv, gate_w, w1, w2):
    x = np.asarray(x, np.float32)
    onehots = np.kron(np.eye(16, dtype=np.float32), np.ones((1, 128), np.float32))
    gm = np.zeros((128, 128), np.float32)
    for p in range(128):
        gm[(p // 16) * 16:(p // 16) * 16 + 16, p] = 1.0
    tm = np.zeros((128, 128), np.float32)
    for p2 in range(128):
        for p in range((p2 // 16) * 16, p2):
            tm[p, p2] = 1.0
    tokmap = np.zeros((16, cfg.FB), np.float32)
    for s in range(16):
        for f in range(cfg.FB):
            tokmap[s, f] = agrow_of_token(cfg, tok_of(cfg, s, f))
    tokmap128 = np.ascontiguousarray(
        tokmap.reshape(16, 8, cfg.FB // 8).transpose(0, 1, 2).reshape(128, cfg.FB // 8))
    maps = []
    for c in range(8):
        jv = np.zeros((128, cfg.NTAU), np.int32)
        for tau in range(cfg.NTAU):
            for p in range(128):
                jv[p, tau] = cfg.CROW * c + 128 + 128 * tau + p
        sl = slice(128 * c, 128 * (c + 1))
        meta = np.zeros((1, 16), np.int32)
        meta[0, 0] = c
        for g in range(8):
            meta[0, 1 + g] = (g + c) % 8
        maps.append(dict(
            xT=np.ascontiguousarray(x[:, :, sl].transpose(0, 2, 1)),
            projW=np.ascontiguousarray(np.concatenate(
                [np.asarray(W_delta, np.float32)[sl],
                 np.asarray(W_B, np.float32)[sl],
                 np.asarray(W_C, np.float32)[sl]], axis=1)),
            wdt_ext=np.ascontiguousarray(np.concatenate(
                [np.asarray(W_dt, np.float32)[:, sl],
                 np.asarray(b_dt, np.float32)[None, sl]], axis=0)),
            a_sl=np.ascontiguousarray(np.asarray(A_log, np.float32)[sl]),
            d_sl=np.ascontiguousarray(np.asarray(Dv, np.float32)[sl, None]),
            gate_w=np.asarray(gate_w, np.float32),
            w1e=np.ascontiguousarray(np.asarray(w1, np.float32)[c]),
            w2e=np.ascontiguousarray(np.asarray(w2, np.float32)[c]),
            idn32=np.eye(128, dtype=np.float32),
            idnbf=np.eye(128, dtype=BF),
            onehots=onehots, gmask=gm, tmask=tm, tokmap=tokmap,
            tokmap128=tokmap128, jvals=jv,
            meta=meta,
        ))
    return maps


def assemble_out(cfg, outs):
    full = np.zeros((cfg.B, cfg.L, cfg.DM), np.float32)
    for c in range(8):
        o = outs[c]
        for i in range(cfg.NBLK):
            blk = 8 * i + c
            full[:, blk * 128:(blk + 1) * 128, :] = o[:, i]
    return full


def kernel(L0_attn_x, L0_attn_A_log, L0_attn_W_delta, L0_attn_W_dt, L0_attn_b_dt,
           L0_attn_W_B, L0_attn_W_C, L0_attn_D, L0_ffn_gate_w, L0_ffn_w1, L0_ffn_w2):
    cfg = Cfg()
    nc, _ = build_program(cfg)
    in_maps = make_inputs(cfg, L0_attn_x, L0_attn_A_log, L0_attn_W_delta, L0_attn_W_dt,
                          L0_attn_b_dt, L0_attn_W_B, L0_attn_W_C, L0_attn_D,
                          L0_ffn_gate_w, L0_ffn_w1, L0_ffn_w2)
    res = run_bass_kernel_spmd(nc, in_maps, core_ids=list(range(8)))
    return assemble_out(cfg, [r["out"] for r in res.results])


# version 10
# speedup vs baseline: 1.0141x; 1.0027x over previous
import numpy as np
import ml_dtypes
import concourse.bass as bass
import concourse.mybir as mybir
from concourse.bass_core_rust import IndirectOffsetOnAxis
from concourse.bass_utils import run_bass_kernel_spmd
from contextlib import ExitStack

F32 = mybir.dt.float32
BF16 = mybir.dt.bfloat16
I32 = mybir.dt.int32
AF = mybir.ActivationFunctionType
OP = mybir.AluOpType
ET = mybir.EngineType
BF = ml_dtypes.bfloat16

SQ2PI = 0.7978845608028654
GK = 0.044715 * SQ2PI
EPS = 1e-5
BISECT_ITERS = 36


class Cfg:
    def __init__(self, L=16384, HID=4096, CH=512, taps=False, stop_after=None):
        self.stop_after = stop_after
        self.B = 2
        self.L = L
        self.DM = 1024
        self.N = 16
        self.HID = HID
        self.HQ = HID // 4
        self.NHT = self.HQ // 128
        self.K = L // 4
        self.TOK = L // 8
        self.CH = CH
        self.NCH = L // CH
        self.NBLK = self.TOK // 128
        self.FB = self.TOK // 2
        self.PD = L // 128
        self.NTAU = self.K // 128
        self.GQ = 4 if self.NTAU % 8 == 0 else (2 if self.NTAU % 4 == 0 else 1)
        self.NG = self.NTAU // self.GQ
        self.CROW = self.K + 128
        self.QAG = max(1, self.NCH // 4)
        self.taps = taps
        assert self.NBLK % 2 == 0 and self.FB >= 128 and self.CH <= 512


def tok_of(cfg, s, f):
    r = s >> 1
    i = (s & 1) * (cfg.NBLK // 2) + (f >> 7)
    j = f & 127
    return 128 * (8 * i + r) + j


def agrow_of_token(cfg, t):
    i, r, j = t // 1024 if False else (t // 128) // 8, (t // 128) % 8, t % 128
    return cfg.TOK * r + 128 * i + j


class Sem:
    def __init__(self, nc, name):
        self.h = nc.alloc_semaphore(name)
        self.v = 0


def build_program(cfg):
    nc = bass.Bass("TRN2", target_bir_lowering=False, dynamic_dma_scratch_size=49152)
    B, L, DM, N = cfg.B, cfg.L, cfg.DM, cfg.N
    CH, NCH, K, TOK, NBLK = cfg.CH, cfg.NCH, cfg.K, cfg.TOK, cfg.NBLK
    HID, HQ, NHT, NTAU = cfg.HID, cfg.HQ, cfg.NHT, cfg.NTAU
    GQ, NG, CROW, QAG = cfg.GQ, cfg.NG, cfg.CROW, cfg.QAG
    FB, PD = cfg.FB, cfg.PD
    RG = [list(range(8))]
    AGTOT = 8 * TOK

    def ein(name, shape, dt=F32):
        return nc.dram_tensor(name, shape, dt, kind="ExternalInput").ap()

    xT = ein("xT", [B, 128, L])
    projW = ein("projW", [128, 96])
    wdt_ext = ein("wdt_ext", [65, 128])
    a_sl = ein("a_sl", [128, N])
    d_sl = ein("d_sl", [128, 1])
    gate_w = ein("gate_w", [DM, 8])
    w1e = ein("w1e", [DM, HID])
    w2e = ein("w2e", [HID, DM])
    idn32 = ein("idn32", [128, 128])
    idnbf = ein("idnbf", [128, 128], BF16)
    onehots = ein("onehots", [16, 2048])
    gmask = ein("gmask", [128, 128])
    tmask = ein("tmask", [128, 128])
    tokmap = ein("tokmap", [16, FB])
    tokmap128 = ein("tokmap128", [128, PD])
    jvals = ein("jvals", [128, NTAU], I32)
    meta = ein("meta", [1, 16], I32)

    out = nc.dram_tensor("out", [B, NBLK, 128, DM], F32, kind="ExternalOutput").ap()

    def idram(name, shape, dt=F32):
        return nc.dram_tensor(name, shape, dt, kind="Internal").ap()

    projpart = [idram(f"projpart{b}", [96, L]) for b in range(B)]
    projAR = [idram(f"projAR{b}", [96, L]) for b in range(B)]
    deltaD = [idram(f"delta{b}", [128, L]) for b in range(B)]
    zin = [idram(f"zin{b}", [128, L], BF16) for b in range(B)]
    zAG = [idram(f"zAG{b}", [DM, L], BF16) for b in range(B)]
    x1own = [idram(f"x1own{b}", [TOK, DM], BF16) for b in range(B)]
    x1AG = [idram(f"x1AG{b}", [AGTOT, DM], BF16) for b in range(B)]
    probsT = [idram(f"probsT{b}", [8, TOK]) for b in range(B)]
    probsAG = [idram(f"probsAG{b}", [64, TOK]) for b in range(B)]
    ubounce = [idram(f"ubounce{b}", [16, FB], I32) for b in range(B)]
    gbounce = [idram(f"gbounce{b}", [16, FB]) for b in range(B)]
    idg = [idram(f"idg{b}", [128, 2 * NTAU]) for b in range(B)]
    posdense = [idram(f"posdense{b}", [128, PD], I32) for b in range(B)]
    posAG = [idram(f"posAG{b}", [1024, PD], I32) for b in range(B)]
    contrib = [idram(f"contrib{b}", [CROW, DM], BF16) for b in range(B)]
    contribAG = [idram(f"contribAG{b}", [8 * CROW, DM], BF16) for b in range(B)]
    zownD = [idram(f"zownD{b}", [DM, NBLK * 128], BF16) for b in range(B)]
    probsAGx2 = [idram(f"probsAGx2{b}", [128, TOK]) for b in range(B)]
    posownD = [idram(f"posownD{b}", [1024, NBLK], I32) for b in range(B)]
    probsRot = [idram(f"probsRot{b}", [64, TOK]) for b in range(B)]

    tap = {}
    if cfg.taps:
        def tout(name, shape, dt=F32):
            tap[name] = nc.dram_tensor("tap_" + name, shape, dt, kind="ExternalOutput").ap()
        tout("delta", [B, 128, L])
        tout("y", [B, 128, L])
        tout("z", [B, DM, L])
        tout("x1", [B, AGTOT, DM])
        tout("probs", [B, 64, TOK])
        tout("bis", [B, 128, 8])
        tout("idg", [B, 128, 2 * NTAU])
        tout("contrib", [B, CROW, DM])
        tout("xgt", [1, 128, 2 * 8 * 64 * 128], BF16)
        tout("c0", [1, CROW, DM], BF16)
        tout("w1q", [1, 128, 8 * 1024], BF16)
        tout("wrow", [1, 128, 4 * 1024], BF16)
        tout("w2q", [1, 128, 8 * 1024], BF16)
        tout("hgel", [1, 128, 64 * 64 * 128], BF16)
        tout("pos", [B, 1024, PD], I32)

    ctx = ExitStack()
    sbt = lambda name, shape, dt=F32: ctx.enter_context(nc.sbuf_tensor(name, shape, dt))
    pst = lambda name, shape, dt=F32: ctx.enter_context(nc.psum_tensor(name, shape, dt))
    V, A, T, G, Y = nc.vector, nc.scalar, nc.tensor, nc.gpsimd, nc.sync

    acol = sbt("acol", [128, N])
    dcol = sbt("dcol", [128, 1])
    idb = sbt("idb", [128, 128], BF16)
    id32 = sbt("id32", [128, 128])
    onehb = sbt("onehb", [16, 2048], BF16)
    gmb = sbt("gmb", [128, 128])
    tmb = sbt("tmb", [128, 128])
    tokm = sbt("tokm", [16, FB])
    jvb = sbt("jvb", [128, NTAU], I32)
    metab = sbt("metab", [1, 16], I32)
    gwb = sbt("gwb", [128, 8 * 8], BF16)
    projwb = sbt("projwb", [128, 96])
    wdtb = sbt("wdtb", [65, 128])
    onesc = sbt("onesc", [128, max(FB, 1024)], BF16)
    zerobf = sbt("zerobf", [128, DM], BF16)
    zrow32 = sbt("zrow32", [128, PD])
    w1q = sbt("w1q", [128, 8 * HQ], BF16)
    w2q = sbt("w2q", [128, NHT * DM], BF16)
    ALIAS = (8 * HQ * 2 >= 2 * CH * 4) and (NHT * DM * 2 >= 2 * CH * 4)
    xc = sbt("xc", [128, 2 * CH])
    dc = sbt("dc", [128, 2 * CH])
    dxc = sbt("dxc", [128, 2 * CH], BF16)
    bccB = sbt("bccB", [16, 2 * CH], BF16)
    bccC = sbt("bccC", [16, 2 * CH], BF16)
    dab = sbt("dab", [128, 2 * CH], BF16)
    dbxb = sbt("dbxb", [128, 2 * CH], BF16)
    hb = sbt("hb", [128, N * CH], BF16)
    hcb = sbt("hcb", [128, 8 * CH], BF16)
    yhalf = sbt("yhalf", [128, CH])
    yb = sbt("yb", [128, 2 * CH])
    zbuf = sbt("zbuf", [128, 2 * CH], BF16)
    if ALIAS:
        w1f = w1q[:].bitcast(F32)
        w2f = w2q[:].bitcast(F32)
        prhs = w1f[0:65, 0:2 * CH]
        pevac = w2f[0:96, 0:2 * CH]
        sfp1 = w1f[:, 2 * CH:4 * CH] if 8 * HQ * 2 >= 4 * CH * 4 else None
        sfp2 = w2f[:, 2 * CH:4 * CH] if NHT * DM * 2 >= 4 * CH * 4 else None
        if sfp1 is None:
            sfp1 = sbt("sfp1", [128, 2 * CH])
        if sfp2 is None:
            sfp2 = sbt("sfp2", [128, 2 * CH])
    else:
        prhs = sbt("prhs", [65, 2 * CH])[:]
        pevac = sbt("pevac", [96, 2 * CH])[:]
        sfp1 = sbt("sfp1", [128, 2 * CH])[:]
        sfp2 = sbt("sfp2", [128, 2 * CH])[:]
    zblk = sbt("zblk", [128, 2 * 8 * 128], BF16)
    ztm = sbt("ztm", [128, 2 * DM], BF16)
    zsc = sbt("zsc", [128, DM], BF16)
    x1bf = sbt("x1bf", [128, 2 * DM], BF16)
    x1dm = sbt("x1dm", [128, 8 * 128], BF16)
    statb = sbt("statb", [128, 16])
    smv = sbt("smv", [128, 16])
    sct = sbt("sct", [8, 128])
    ptb = sbt("ptb", [128, 8], BF16)
    TALIAS = (N * CH * 2 >= 4 * FB * 4) and (8 * CH * 2 >= FB * 4 + 2 * NTAU * 4)
    if TALIAS:
        hbf = hb[:].bitcast(F32)
        hcf = hcb[:].bitcast(F32)
        pbis = hbf[:, 0:FB]
        bmask = hbf[:, FB:2 * FB]
        bmask2 = hbf[:, 2 * FB:3 * FB]
        cumb = hbf[:, 3 * FB:4 * FB]
        ucols = hcf[:, 0:max(FB, 2 * NTAU)]
    else:
        pbis = sbt("pbis", [128, FB])[:]
        bmask = sbt("bmask", [128, FB])[:]
        bmask2 = sbt("bmask2", [128, FB])[:]
        cumb = sbt("cumb", [128, FB])[:]
        ucols = sbt("ucols", [128, max(FB, 2 * NTAU)])[:]
    bcols = sbt("bcols", [128, 8])
    tk128 = sbt("tk128", [128, PD])
    g128 = sbt("g128", [128, PD])
    u32b = sbt("u32b", [128, PD], I32)
    pairb = sbt("pairb", [128, 2 * PD])
    posu = sbt("posu", [128, 2 * NTAU], I32)
    idsb = sbt("idsb", [128, 2 * NTAU], I32)
    gsb = sbt("gsb", [128, 2 * NTAU])
    xg = sbt("xg", [128, 2 * DM], BF16)
    xgt = sbt("xgt", [128, 2 * 8 * GQ * 128], BF16)
    hgel = sbt("hgel", [128, NHT * GQ * 128], BF16)
    gx = sbt("gx", [128, GQ * 128], BF16)
    gsc1 = sbt("gsc1", [128, GQ * 128], BF16)
    gsc2 = sbt("gsc2", [128, GQ * 128], BF16)
    wrow = sbt("wrow", [128, 4 * DM], BF16)
    addr8 = sbt("addr8", [128, 8 * NBLK], I32)
    gat1 = sbt("gat1", [128, 2 * DM], BF16)
    accr = sbt("accr", [128, DM])
    x1r = sbt("x1r", [128, DM], BF16)
    z2b = sbt("z2b", [128, DM])
    outb = sbt("outb", [128, DM])

    pp_brep = pst("pp_brep", [128, CH])
    pp_crep = pst("pp_crep", [128, CH])
    O1S = max(GQ * 128, 512)
    pp_o1 = pst("pp_o1", [128, 2 * O1S])
    pp_o2 = pst("pp_o2", [128, DM])
    pp_tr = pst("pp_tr", [128, 1024], BF16)
    pp_msc = pst("pp_msc", [128, CH])

    S = lambda name: Sem(nc, name)
    s_ld = S("s_ld")
    s_ldg = S("s_ldg")
    s_xc = [S("s_xc0"), S("s_xc1")]
    s_dc = [S("s_dc0"), S("s_dc1")]
    s_bc = [S("s_bc0"), S("s_bc1")]
    s_pe = S("s_pe")
    s_act = S("s_act")
    s_dve = S("s_dve")
    s_gp = S("s_gp")
    s_sy = S("s_sy")
    s_cc = S("s_cc")
    s_z = [S("s_z0"), S("s_z1")]
    s_w1 = S("s_w1")
    s_w2 = S("s_w2")
    s_ctb = [S("s_ctb0"), S("s_ctb1")]
    s_outd = S("s_outd")
    s_pv = [S("s_pv0"), S("s_pv1")]
    s_zb = [S("s_zb0"), S("s_zb1")]
    s_xg = [S("s_xg0"), S("s_xg1")]
    s_fg = [S("s_fg0"), S("s_fg1")]
    s_ty = [S("s_ty0"), S("s_ty1")] if cfg.taps else None
    s_ps = S("s_ps")

    def dma(eng, o, i_, sem):
        eng.dma_start(out=o, in_=i_).then_inc(sem.h, 16)
        sem.v += 16
        return sem.v

    def dma_acc(o, i_, sem):
        G.dma_start(out=o, in_=i_, accum_op=OP.add).then_inc(sem.h, 16)
        sem.v += 16
        return sem.v

    dma_acc2 = dma_acc

    def mark(eng_unused, instr, sem):
        instr.then_inc(sem.h, 1)
        sem.v += 1
        return sem.v

    def vv():
        V.wait_ge(s_dve.h, s_dve.v)

    def aa():
        A.wait_ge(s_act.h, s_act.v)

    def emit_rsqrt(vin, vout, scr):
        vi = vin.bitcast(I32)
        yi = vout.bitcast(I32)
        vv()
        mark(V, V.tensor_single_scalar(yi, vi, 1, OP.arith_shift_right), s_dve)
        vv()
        mark(V, V.tensor_single_scalar(yi, yi, -1, OP.mult), s_dve)
        vv()
        mark(V, V.tensor_single_scalar(yi, yi, 0x5F3759DF, OP.add), s_dve)
        for _ in range(3):
            vv()
            mark(V, V.tensor_mul(scr, vout, vout), s_dve)
            vv()
            mark(V, V.tensor_mul(scr, scr, vin), s_dve)
            vv()
            mark(V, V.tensor_scalar(scr, scr, -0.5, 1.5, OP.mult, OP.add), s_dve)
            vv()
            mark(V, V.tensor_mul(vout, vout, scr), s_dve)

    for (dst, src) in ((acol, a_sl), (dcol, d_sl), (id32, idn32), (tokm, tokmap),
                       (metab, meta), (projwb, projW), (wdtb, wdt_ext), (jvb, jvals)):
        dma(Y, dst[:], src, s_ld)
    dma(G, gwb[:].rearrange("p (dt e) -> p dt e", e=8),
        gate_w.rearrange("(dt p) e -> p dt e", p=128), s_ldg)
    for (dst, src) in ((onehb, onehots), (idb, idnbf)):
        dma(G, dst[:], src, s_ldg)
    for (dst, src) in ((gmb, gmask), (tmb, tmask), (tk128, tokmap128)):
        dma(Y, dst[:], src, s_ld)
    LD0 = s_ld.v
    LD0G = s_ldg.v

    V.wait_ge(s_ld.h, LD0)
    V.wait_ge(s_ldg.h, LD0G)
    vv()
    mark(V, V.memset(onesc[:], 1.0), s_dve)
    vv()
    mark(V, V.memset(zerobf[:], 0.0), s_dve)
    vv()
    mark(V, V.memset(zrow32[:], 0.0), s_dve)
    vv()
    mark(V, V.memset(prhs[:], 1.0), s_dve)
    vv()
    mark(V, V.memset(smv[:], 0.0), s_dve)
    G.wait_ge(s_dve.h, s_dve.v)
    for b in range(B):
        dma(G, contrib[b][0:128, :], zerobf[:], s_gp)
        dma(G, posdense[b], zrow32[:], s_gp)
        dma(G, idg[b], zrow32[:, 0:2 * NTAU], s_gp)
    ZIDG = s_gp.v
    G.wait_ge(s_gp.h, ZIDG)

    vv()
    mark(V, V.tensor_scalar(dcol[:], dcol[:], 1.0, None, OP.add), s_dve)
    A.wait_ge(s_ld.h, LD0)
    aa()
    mark(A, A.activation(acol[:], acol[:], AF.Exp), s_act)
    V.wait_ge(s_act.h, s_act.v)
    vv()
    mark(V, V.tensor_scalar_mul(acol[:], acol[:], -1.0), s_dve)
    ACOL_RDY = s_dve.v

    G.wait_ge(s_ld.h, LD0)
    G.wait_ge(s_ldg.h, LD0G)
    bc128 = G.to_reg(128 * NTAU - 1)
    bcpd = G.to_reg(128 * PD - 1)
    regs = {"c": nc.values_load(metab[0:1, 0:1], engines=[ET.Pool],
                                min_val=0, max_val=7,
                                skip_runtime_bounds_check=True)}

    PEV = [0, 0]
    PEX = [0, 0]
    PEVD = [0, 0]
    for b in range(B):
        for ci in range(NCH):
            p = ci % 2
            csl = slice(p * CH, (p + 1) * CH)
            if b > 0 or ci >= 2:
                Y.wait_ge(s_pe.h, PEX[p])
            t_x = dma(Y, xc[:, csl], xT[b, :, ci * CH:(ci + 1) * CH], s_xc[p])
            T.wait_ge(s_xc[p].h, t_x)
            T.wait_ge(s_ld.h, LD0)
            T.wait_ge(s_ldg.h, LD0G)
            if b > 0 or ci >= 2:
                T.wait_ge(s_dve.h, PEV[p])
            PEX[p] = mark(T, T.matmul(pp_msc[0:96, 0:CH], projwb[:], xc[:, csl],
                                      start=True, stop=True), s_pe)
            V.wait_ge(s_pe.h, s_pe.v)
            if b > 0 or ci >= 2:
                V.wait_ge(s_pv[p].h, PEVD[p])
            vv()
            PEV[p] = mark(V, V.tensor_copy(pevac[:, p * CH:(p + 1) * CH],
                                           pp_msc[0:96, 0:CH]), s_dve)
            Y.wait_ge(s_dve.h, PEV[p])
            PEVD[p] = dma(Y, projpart[b][:, ci * CH:(ci + 1) * CH],
                          pevac[:, p * CH:(p + 1) * CH], s_pv[p])
        G.wait_ge(s_pv[0].h, s_pv[0].v)
        G.wait_ge(s_pv[1].h, s_pv[1].v)
        G.collective_compute("AllReduce", OP.add, RG,
                             [projpart[b]], [projAR[b]]).then_inc(s_cc.h, 1)
        s_cc.v += 1
    AR_DONE = s_cc.v

    DPRE = [0, 0]
    DSF = [0, 0]
    PRH = [0, 0]
    Y.wait_ge(s_cc.h, AR_DONE)
    for b in range(B):
        for ci in range(NCH):
            p = ci % 2
            csl = slice(p * CH, (p + 1) * CH)
            if b > 0 or ci >= 2:
                Y.wait_ge(s_pe.h, PRH[p])
            t_p = dma(Y, prhs[0:64, csl], projAR[b][0:64, ci * CH:(ci + 1) * CH], s_dc[p])
            T.wait_ge(s_dc[p].h, t_p)
            T.wait_ge(s_cc.h, AR_DONE)
            if b > 0 or ci >= 2:
                T.wait_ge(s_act.h, DPRE[p])
            PRH[p] = mark(T, T.matmul(pp_msc[:, 0:CH], wdtb[:], prhs[:, csl],
                                      start=True, stop=True), s_pe)
            A.wait_ge(s_pe.h, s_pe.v)
            if b > 0 or ci >= 2:
                A.wait_ge(s_pv[p].h, DSF[p])
            aa()
            mark(A, A.activation(sfp1[:, csl], pp_msc[:, 0:CH], AF.Abs), s_act)
            A.wait_ge(s_act.h, s_act.v)
            aa()
            mark(A, A.activation(sfp1[:, csl], sfp1[:, csl], AF.Exp, scale=-1.0), s_act)
            A.wait_ge(s_act.h, s_act.v)
            aa()
            mark(A, A.activation(sfp1[:, csl], sfp1[:, csl], AF.Ln, bias=1.0), s_act)
            aa()
            DPRE[p] = mark(A, A.activation(sfp2[:, csl], pp_msc[:, 0:CH], AF.Relu), s_act)
            V.wait_ge(s_act.h, DPRE[p])
            vv()
            mark(V, V.tensor_add(sfp2[:, csl], sfp2[:, csl], sfp1[:, csl]), s_dve)
            Y.wait_ge(s_dve.h, s_dve.v)
            DSF[p] = dma(Y, deltaD[b][:, ci * CH:(ci + 1) * CH], sfp2[:, csl], s_pv[p])
            if cfg.taps:
                DSF[p] = dma(Y, tap["delta"][b, :, ci * CH:(ci + 1) * CH], sfp2[:, csl], s_pv[p])

    state = dict(XCF=[0, 0], DCF=[0, 0], BCF=[0, 0], ZBF=[0, 0], BRC=0,
                 TRT=[0, 0], ZTME=[0, 0], YT=[0, 0], X1O=[0, 0], PST=0,
                 POST=0, ZAGTOK={}, AGTOK={}, IDSTOK={})

    def mamba_scan(b):
        G.wait_ge(s_cc.h, AR_DONE)
        Y.wait_ge(s_pv[0].h, s_pv[0].v)
        Y.wait_ge(s_pv[1].h, s_pv[1].v)
        vv()
        mark(V, V.memset(hb[:], 0.0), s_dve)
        for ci in range(NCH):
            p = ci % 2
            csl = slice(p * CH, (p + 1) * CH)
            if state["XCF"][p]:
                Y.wait_ge(s_dve.h, state["XCF"][p])
            t_x = dma(Y, xc[:, csl], xT[b, :, ci * CH:(ci + 1) * CH], s_xc[p])
            if state["DCF"][p]:
                Y.wait_ge(s_act.h, state["DCF"][p])
            t_d = dma(Y, dc[:, csl], deltaD[b][:, ci * CH:(ci + 1) * CH], s_dc[p])
            if state["BCF"][p]:
                G.wait_ge(s_pe.h, state["BCF"][p])
            dma(G, bccB[:, csl], projAR[b][64:80, ci * CH:(ci + 1) * CH], s_bc[p])
            t_b = dma(G, bccC[:, csl], projAR[b][80:96, ci * CH:(ci + 1) * CH], s_bc[p])
            V.wait_ge(s_xc[p].h, t_x)
            V.wait_ge(s_dc[p].h, t_d)
            vv()
            mark(V, V.tensor_mul(dxc[:, csl], dc[:, csl], xc[:, csl]), s_dve)
            DX_T = s_dve.v
            if cfg.taps and state["YT"][p]:
                V.wait_ge(s_ty[p].h, state["YT"][p])
            for half in range(2):
                for nn in range(8):
                    n = half * 8 + nn
                    npar = n % 2
                    nsl = slice(npar * CH, (npar + 1) * CH)
                    T.wait_ge(s_bc[p].h, t_b)
                    if state["BRC"]:
                        T.wait_ge(s_dve.h, state["BRC"])
                    mark(T, T.matmul(pp_brep[:, 0:CH], onehb[:, n * 128:(n + 1) * 128],
                                     bccB[:, csl], start=True, stop=True), s_pe)
                    PE_N = mark(T, T.matmul(pp_crep[:, 0:CH], onehb[:, n * 128:(n + 1) * 128],
                                            bccC[:, csl], start=True, stop=True), s_pe)
                    A.wait_ge(s_dc[p].h, t_d)
                    A.wait_ge(s_dve.h, ACOL_RDY)
                    if state["BRC"]:
                        A.wait_ge(s_dve.h, state["BRC"])
                    aa()
                    ACT_N = mark(A, A.activation(dab[:, nsl], dc[:, csl], AF.Exp,
                                                 scale=acol[:, n:n + 1]), s_act)
                    V.wait_ge(s_pe.h, PE_N)
                    vv()
                    mark(V, V.tensor_mul(dbxb[:, nsl], dxc[:, csl], pp_brep[:, 0:CH]), s_dve)
                    V.wait_ge(s_act.h, ACT_N)
                    vv()
                    mark(V, V.tensor_tensor_scan(
                        hb[:, n * CH:(n + 1) * CH], dab[:, nsl], dbxb[:, nsl],
                        hb[:, (n + 1) * CH - 1:(n + 1) * CH], OP.mult, OP.add), s_dve)
                    vv()
                    state["BRC"] = mark(V, V.tensor_mul(
                        hcb[:, nn * CH:(nn + 1) * CH],
                        hb[:, n * CH:(n + 1) * CH], pp_crep[:, 0:CH]), s_dve)
                for st in (4, 2, 1):
                    for kk in range(st):
                        vv()
                        mark(V, V.tensor_add(hcb[:, kk * CH:(kk + 1) * CH],
                                             hcb[:, kk * CH:(kk + 1) * CH],
                                             hcb[:, (kk + st) * CH:(kk + st + 1) * CH]), s_dve)
                if half == 0:
                    vv()
                    mark(V, V.tensor_copy(yhalf[:], hcb[:, 0:CH]), s_dve)
                else:
                    vv()
                    mark(V, V.tensor_add(yb[:, csl], yhalf[:], hcb[:, 0:CH]), s_dve)
            state["BCF"][p] = s_pe.v
            state["DCF"][p] = s_act.v
            if cfg.taps:
                Y.wait_ge(s_dve.h, s_dve.v)
                state["YT"][p] = dma(Y, tap["y"][b, :, ci * CH:(ci + 1) * CH],
                                     yb[:, csl], s_ty[p])
            if state["ZBF"][p]:
                V.wait_ge(s_z[p].h, state["ZBF"][p])
            vv()
            mark(V, V.scalar_tensor_tensor(zbuf[:, csl], xc[:, csl], dcol[:, 0:1],
                                           yb[:, csl], OP.mult, OP.add), s_dve)
            state["XCF"][p] = s_dve.v
            Y.wait_ge(s_dve.h, s_dve.v)
            for k in range(CH // 128):
                blk = ci * (CH // 128) + k
                dstcol = (blk % 8) * TOK + (blk // 8) * 128
                state["ZBF"][p] = dma(Y, zin[b][:, dstcol:dstcol + 128],
                                      zbuf[:, csl.start + k * 128:csl.start + (k + 1) * 128],
                                      s_z[p])
            yield
        G.wait_ge(s_z[0].h, s_z[0].v)
        G.wait_ge(s_z[1].h, s_z[1].v)
        G.collective_compute("AllGather", OP.bypass, RG,
                             [zin[b]], [zAG[b]]).then_inc(s_cc.h, 1)
        s_cc.v += 1
        state["ZAGTOK"][b] = s_cc.v

    def owner_ln1(b):
        yield
        G.wait_ge(s_cc.h, state["ZAGTOK"][b])
        t_zown = dma(G, zownD[b], zAG[b][:, bass.ts(regs["c"], TOK)], s_gp)
        zsrc = zownD[b].rearrange("(dt p) t -> p dt t", p=128)
        G.wait_ge(s_gp.h, t_zown)
        for i in range(NBLK):
            yield
            p = i % 2
            sl1 = slice(p * DM, (p + 1) * DM)
            if state["TRT"][p]:
                G.wait_ge(s_pe.h, state["TRT"][p])
            t_z = dma(G, zblk[:, p * 1024:(p + 1) * 1024],
                      zsrc[:, :, i * 128:(i + 1) * 128], s_zb[p])
            T.wait_ge(s_zb[p].h, t_z)
            T.wait_ge(s_dve.h, s_dve.v)
            T.wait_ge(s_act.h, s_act.v)
            for dt in range(8):
                mark(T, T.transpose(pp_tr[:, dt * 128:(dt + 1) * 128],
                                    zblk[:, p * 1024 + dt * 128:p * 1024 + (dt + 1) * 128],
                                    idb[:]), s_pe)
            state["TRT"][p] = s_pe.v
            V.wait_ge(s_pe.h, s_pe.v)
            vv()
            mark(V, V.tensor_copy(ztm[:, sl1], pp_tr[:, 0:1024]), s_dve)
            A.wait_ge(s_dve.h, s_dve.v)
            aa()
            mark(A, A.activation(zsc[:], ztm[:, sl1], AF.Copy, accum_out=statb[:, 0:1]), s_act)
            aa()
            mark(A, A.activation(zsc[:], ztm[:, sl1], AF.Square, accum_out=statb[:, 1:2]), s_act)
            V.wait_ge(s_act.h, s_act.v)
            vv()
            mark(V, V.tensor_scalar_mul(statb[:, 2:3], statb[:, 0:1], 1.0 / DM), s_dve)
            vv()
            mark(V, V.tensor_mul(statb[:, 3:4], statb[:, 2:3], statb[:, 2:3]), s_dve)
            vv()
            mark(V, V.tensor_scalar(statb[:, 4:5], statb[:, 1:2], 1.0 / DM, EPS,
                                    OP.mult, OP.add), s_dve)
            vv()
            mark(V, V.tensor_sub(statb[:, 4:5], statb[:, 4:5], statb[:, 3:4]), s_dve)
            emit_rsqrt(statb[:, 4:5], statb[:, 5:6], statb[:, 6:7])
            vv()
            if state["X1O"][p]:
                V.wait_ge(s_pv[p].h, state["X1O"][p])
            mark(V, V.tensor_scalar(x1bf[:, sl1], ztm[:, sl1], statb[:, 2:3],
                                    statb[:, 5:6], OP.subtract, OP.mult), s_dve)
            X1_T = s_dve.v
            Y.wait_ge(s_dve.h, X1_T)
            state["X1O"][p] = dma(Y, x1own[b][i * 128:(i + 1) * 128, :], x1bf[:, sl1], s_pv[p])
            T.wait_ge(s_dve.h, X1_T)
            for dt in range(8):
                mark(T, T.transpose(pp_tr[:, dt * 128:(dt + 1) * 128],
                                    x1bf[:, p * DM + dt * 128:p * DM + (dt + 1) * 128],
                                    idb[:]), s_pe)
            V.wait_ge(s_pe.h, s_pe.v)
            vv()
            mark(V, V.tensor_copy(x1dm[:], pp_tr[:, 0:1024]), s_dve)
            T.wait_ge(s_dve.h, s_dve.v)
            for dt in range(8):
                mark(T, T.matmul(pp_msc[0:8, 0:128], gwb[:, dt * 8:(dt + 1) * 8],
                                 x1dm[:, dt * 128:(dt + 1) * 128],
                                 start=(dt == 0), stop=(dt == 7)), s_pe)
            V.wait_ge(s_pe.h, s_pe.v)
            vv()
            if state["PST"]:
                V.wait_ge(s_ps.h, state["PST"])
            mark(V, V.tensor_copy(sct[:], pp_msc[0:8, 0:128]), s_dve)
            T.wait_ge(s_dve.h, s_dve.v)
            mark(T, T.transpose(pp_msc[0:128, 128:136], sct[:], id32[0:8, 0:8]), s_pe)
            V.wait_ge(s_pe.h, s_pe.v)
            vv()
            mark(V, V.tensor_copy(smv[:, 0:8], pp_msc[0:128, 128:136]), s_dve)
            vv()
            mark(V, V.reduce_max(smv[:, 8:9], smv[:, 0:8], axis=mybir.AxisListType.X), s_dve)
            vv()
            mark(V, V.tensor_scalar_mul(smv[:, 9:10], smv[:, 8:9], -1.0), s_dve)
            A.wait_ge(s_dve.h, s_dve.v)
            aa()
            mark(A, A.activation(smv[:, 0:8], smv[:, 0:8], AF.Exp,
                                 bias=smv[:, 9:10], accum_out=smv[:, 10:11]), s_act)
            V.wait_ge(s_act.h, s_act.v)
            vv()
            mark(V, V.reciprocal(smv[:, 11:12], smv[:, 10:11]), s_dve)
            vv()
            mark(V, V.tensor_scalar(ptb[:, 0:8], smv[:, 0:8], smv[:, 11:12], None,
                                    OP.mult), s_dve)
            T.wait_ge(s_dve.h, s_dve.v)
            T.wait_ge(s_act.h, s_act.v)
            mark(T, T.transpose(pp_tr[0:8, 0:128], ptb[:, 0:8], idb[:]), s_pe)
            V.wait_ge(s_pe.h, s_pe.v)
            vv()
            mark(V, V.tensor_copy(sct[:], pp_tr[0:8, 0:128]), s_dve)
            Y.wait_ge(s_dve.h, s_dve.v)
            state["PST"] = dma(Y, probsT[b][:, i * 128:(i + 1) * 128], sct[:], s_ps)
        G.wait_ge(s_pv[0].h, s_pv[0].v)
        G.wait_ge(s_pv[1].h, s_pv[1].v)
        G.wait_ge(s_ps.h, s_ps.v)
        G.collective_compute("AllGather", OP.bypass, RG, [x1own[b]], [x1AG[b]]).then_inc(s_cc.h, 1)
        s_cc.v += 1
        G.collective_compute("AllGather", OP.bypass, RG, [probsT[b]], [probsAG[b]]).then_inc(s_cc.h, 1)
        s_cc.v += 1
        if cfg.taps:
            G.wait_ge(s_cc.h, s_cc.v)
            dma(G, tap["x1"][b], x1AG[b], s_gp)
            dma(G, tap["probs"][b], probsAG[b], s_gp)
            dma(G, tap["z"][b], zAG[b], s_gp)
            G.wait_ge(s_gp.h, s_gp.v)
        state["AGTOK"][b] = s_cc.v

    def topk(b):
        boff = (b % 2) * NTAU
        yield
        G.wait_ge(s_cc.h, state["AGTOK"][b])
        dma(G, probsAGx2[b][0:64, :], probsAG[b], s_gp)
        dma(G, probsAGx2[b][64:128, :], probsAG[b], s_gp)
        t_x2 = s_gp.v
        G.wait_ge(s_gp.h, t_x2)
        dma(G, probsRot[b], probsAGx2[b][bass.ds(regs["c"], 64), :], s_gp)
        G.wait_ge(s_gp.h, s_gp.v)
        srcv = probsRot[b].rearrange("(r g) (h f) -> g r h f", g=8, h=2)
        for g in range(8):
            dma(G, pbis[16 * g:16 * (g + 1), :], srcv[g:g + 1, :, :, :], s_gp)
        PB_T = s_gp.v
        V.wait_ge(s_gp.h, PB_T)
        vv()
        mark(V, V.memset(bcols[:, 0:1], 0.0), s_dve)
        vv()
        mark(V, V.memset(bcols[:, 1:2], 1.0), s_dve)
        for it in range(BISECT_ITERS):
            yield
            vv()
            mark(V, V.tensor_add(bcols[:, 2:3], bcols[:, 0:1], bcols[:, 1:2]), s_dve)
            vv()
            mark(V, V.tensor_scalar_mul(bcols[:, 2:3], bcols[:, 2:3], 0.5), s_dve)
            vv()
            mark(V, V.tensor_scalar(bmask[:], pbis[:], bcols[:, 2:3], 0.0,
                                    OP.is_gt, OP.add, accum_out=bcols[:, 3:4]), s_dve)
            T.wait_ge(s_dve.h, s_dve.v)
            mark(T, T.matmul(pp_msc[0:128, 0:1], gmb[:], bcols[:, 3:4],
                             start=True, stop=True), s_pe)
            V.wait_ge(s_pe.h, s_pe.v)
            vv()
            mark(V, V.tensor_copy(bcols[:, 4:5], pp_msc[0:128, 0:1]), s_dve)
            vv()
            mark(V, V.tensor_scalar(bmask2[:, 0:1], bcols[:, 4:5], float(K), None,
                                    OP.is_ge), s_dve)
            vv()
            mark(V, V.tensor_sub(bcols[:, 5:6], bcols[:, 2:3], bcols[:, 0:1]), s_dve)
            vv()
            mark(V, V.tensor_mul(bcols[:, 5:6], bcols[:, 5:6], bmask2[:, 0:1]), s_dve)
            vv()
            mark(V, V.tensor_add(bcols[:, 0:1], bcols[:, 0:1], bcols[:, 5:6]), s_dve)
            vv()
            mark(V, V.tensor_sub(bcols[:, 5:6], bcols[:, 1:2], bcols[:, 2:3]), s_dve)
            vv()
            mark(V, V.tensor_mul(bcols[:, 5:6], bcols[:, 5:6], bmask2[:, 0:1]), s_dve)
            vv()
            mark(V, V.tensor_add(bcols[:, 1:2], bcols[:, 2:3], bcols[:, 5:6]), s_dve)
        yield
        vv()
        mark(V, V.tensor_scalar(bmask[:], pbis[:], bcols[:, 1:2], 0.0,
                                OP.is_gt, OP.add, accum_out=bcols[:, 3:4]), s_dve)
        T.wait_ge(s_dve.h, s_dve.v)
        mark(T, T.matmul(pp_msc[0:128, 0:1], gmb[:], bcols[:, 3:4],
                         start=True, stop=True), s_pe)
        V.wait_ge(s_pe.h, s_pe.v)
        vv()
        mark(V, V.tensor_copy(bcols[:, 5:6], pp_msc[0:128, 0:1]), s_dve)
        vv()
        mark(V, V.tensor_scalar(bmask[:], pbis[:], bcols[:, 1:2], None, OP.is_gt), s_dve)
        vv()
        mark(V, V.tensor_scalar(bmask2[:], pbis[:], bcols[:, 0:1], None, OP.is_gt), s_dve)
        vv()
        mark(V, V.tensor_sub(bmask2[:], bmask2[:], bmask[:]), s_dve)
        vv()
        mark(V, V.tensor_tensor_scan(cumb[:], onesc[:, 0:FB], bmask2[:], 0.0,
                                     OP.mult, OP.add), s_dve)
        vv()
        mark(V, V.tensor_copy(bcols[:, 6:7], cumb[:, FB - 1:FB]), s_dve)
        T.wait_ge(s_dve.h, s_dve.v)
        mark(T, T.matmul(pp_msc[0:128, 0:1], tmb[:], bcols[:, 6:7],
                         start=True, stop=True), s_pe)
        V.wait_ge(s_pe.h, s_pe.v)
        vv()
        mark(V, V.tensor_copy(bcols[:, 7:8], pp_msc[0:128, 0:1]), s_dve)
        vv()
        mark(V, V.tensor_scalar(cumb[:], cumb[:], bcols[:, 7:8], -1.0, OP.add, OP.add), s_dve)
        vv()
        mark(V, V.tensor_scalar_mul(bcols[:, 6:7], bcols[:, 5:6], -1.0), s_dve)
        vv()
        mark(V, V.tensor_scalar(bcols[:, 6:7], bcols[:, 6:7], float(K), None, OP.add), s_dve)
        vv()
        mark(V, V.tensor_scalar(cumb[:], cumb[:], bcols[:, 6:7], None, OP.is_lt), s_dve)
        vv()
        mark(V, V.tensor_mul(cumb[:], cumb[:], bmask2[:]), s_dve)
        vv()
        mark(V, V.tensor_add(bmask[:], bmask[:], cumb[:]), s_dve)
        if cfg.taps:
            Y.wait_ge(s_dve.h, s_dve.v)
            dma(Y, tap["bis"][b], bcols[:], s_sy)
        yield
        vv()
        mark(V, V.tensor_tensor_scan(cumb[:], onesc[:, 0:FB], bmask[:], 0.0,
                                     OP.mult, OP.add), s_dve)
        vv()
        mark(V, V.tensor_copy(bcols[:, 6:7], cumb[:, FB - 1:FB]), s_dve)
        T.wait_ge(s_dve.h, s_dve.v)
        mark(T, T.matmul(pp_msc[0:128, 0:1], tmb[:], bcols[:, 6:7],
                         start=True, stop=True), s_pe)
        V.wait_ge(s_pe.h, s_pe.v)
        vv()
        mark(V, V.tensor_copy(bcols[:, 7:8], pp_msc[0:128, 0:1]), s_dve)
        vv()
        mark(V, V.tensor_scalar(cumb[:], cumb[:], bcols[:, 7:8], -1.0, OP.add, OP.add), s_dve)
        vv()
        mark(V, V.tensor_mul(ucols[:, 0:FB], cumb[:], bmask[:]), s_dve)
        vv()
        mark(V, V.tensor_scalar(bmask2[:], bmask[:], 0.5, 1e9, OP.is_lt, OP.mult), s_dve)
        vv()
        mark(V, V.tensor_add(ucols[:, 0:FB], ucols[:, 0:FB], bmask2[:]), s_dve)
        ui = ucols.bitcast(I32)
        ci2 = cumb.bitcast(I32)
        vv()
        mark(V, V.tensor_copy(ui[:, 0:FB], ucols[:, 0:FB]), s_dve)
        vv()
        mark(V, V.tensor_single_scalar(ci2[:, 0:FB], ui[:, 0:FB], 127, OP.bitwise_and), s_dve)
        vv()
        mark(V, V.tensor_single_scalar(ui[:, 0:FB], ui[:, 0:FB], 7, OP.arith_shift_right), s_dve)
        vv()
        mark(V, V.tensor_single_scalar(ci2[:, 0:FB], ci2[:, 0:FB], NTAU, OP.mult), s_dve)
        vv()
        mark(V, V.tensor_tensor(ui[:, 0:FB], ui[:, 0:FB], ci2[:, 0:FB], OP.add), s_dve)
        UP_T = s_dve.v
        Y.wait_ge(s_dve.h, UP_T)
        dma(Y, ubounce[b], ui[0:16, 0:FB], s_sy)
        dma(Y, gbounce[b], pbis[0:16, :], s_sy)
        Y.wait_ge(s_sy.h, s_sy.v)
        dma(Y, u32b[:], ubounce[b].rearrange("a (p c) -> (a p) c", p=8), s_sy)
        t_g128 = dma(Y, g128[:], gbounce[b].rearrange("a (p c) -> (a p) c", p=8), s_sy)
        V.wait_ge(s_sy.h, t_g128)
        V.wait_ge(s_ld.h, LD0)
        vv()
        mark(V, V.tensor_copy(pairb[:, 0:2 * PD:2], tk128[:]), s_dve)
        vv()
        mark(V, V.tensor_copy(pairb[:, 1:2 * PD:2], g128[:]), s_dve)
        t_rb = s_sy.v
        Y.wait_ge(s_dve.h, s_dve.v)
        G.wait_ge(s_sy.h, t_rb)
        G.wait_ge(s_dve.h, s_dve.v)
        G.wait_ge(s_gp.h, ZIDG)
        idgflat = idg[b].rearrange("p c -> (p c)").rearrange("(x o) -> x o", o=2)
        for q in range(PD):
            if q % 16 == 0:
                yield
            G.indirect_dma_start(
                out=idgflat, out_offset=IndirectOffsetOnAxis(u32b[:, q:q + 1], 0),
                in_=pairb[:, 2 * q:2 * q + 2], in_offset=None,
                bounds_check=bc128, oob_is_err=False,
            ).then_inc(s_gp.h, 16)
            s_gp.v += 16
        IDG_T = s_gp.v
        if cfg.taps:
            G.wait_ge(s_gp.h, IDG_T)
            dma(G, tap["idg"][b], idg[b], s_gp)
            G.wait_ge(s_gp.h, s_gp.v)
        Y.wait_ge(s_gp.h, IDG_T)
        t_ig = dma(Y, ucols[:, 0:2 * NTAU], idg[b], s_sy)
        V.wait_ge(s_sy.h, t_ig)
        vv()
        mark(V, V.tensor_copy(idsb[:, boff:boff + NTAU], ucols[:, 0:2 * NTAU:2]), s_dve)
        vv()
        mark(V, V.tensor_copy(gsb[:, boff:boff + NTAU], ucols[:, 1:2 * NTAU:2]), s_dve)
        state["IDSTOK"][b] = s_dve.v

    def ffn(b, ids_tok, aux=None, aux_every=1):
        boff = (b % 2) * NTAU
        step = 0
        O2C = [0]
        HGU = [0]
        WR_EV = [0, 0]
        XGU = [0, 0]
        XGP = [0, 0]
        GEL_T = [0, 0]
        PSR_T = [0, 0]
        GEL_ALL = 0
        O1_T = [0, 0]
        ctb_tok = [0] * (NTAU // 2)
        for hq in range(4):
            G.wait_ge(s_pe.h, s_pe.v)
            w1_tok = dma(G, w1q[:].rearrange("p (dt h) -> p dt h", dt=8),
                         w1e.rearrange("(dt p) h -> p dt h", p=128)
                         [:, :, hq * HQ:(hq + 1) * HQ], s_w1)
            w2_tok = dma(G, w2q[:].rearrange("p (a d) -> p a d", d=DM),
                         w2e.rearrange("(a p) d -> p a d", p=128)
                         [:, hq * NHT:(hq + 1) * NHT, :], s_w2)
            if cfg.taps and b == 0 and hq == 1:
                G.wait_ge(s_ctb[0].h, s_ctb[0].v)
                G.wait_ge(s_ctb[1].h, s_ctb[1].v)
                dma(G, tap["c0"][0], contrib[b], s_gp)
                G.wait_ge(s_gp.h, s_gp.v)
            if cfg.taps and b == 0 and hq == 0:
                G.wait_ge(s_w1.h, w1_tok)
                G.wait_ge(s_w2.h, w2_tok)
                dma(G, tap["w1q"][0][:, 0:8 * HQ], w1q[:], s_gp)
                dma(G, tap["w2q"][0][:, 0:NHT * DM], w2q[:], s_gp)
                G.wait_ge(s_gp.h, s_gp.v)
            for g in range(NG):
                if aux is not None and step % aux_every == 0:
                    next(aux, None)
                step += 1
                gp = g % 2
                for tt in range(GQ):
                    tau = g * GQ + tt
                    tp = tau % 2
                    if aux is not None and hq >= 2:
                        next(aux, None)
                    G.wait_ge(s_dve.h, ids_tok)
                    if XGP[tp]:
                        G.wait_ge(s_pe.h, XGP[tp])
                    G.indirect_dma_start(
                        out=xg[:, tp * DM:(tp + 1) * DM], out_offset=None,
                        in_=x1AG[b],
                        in_offset=IndirectOffsetOnAxis(idsb[:, boff + tau:boff + tau + 1], 0),
                    ).then_inc(s_xg[tp].h, 16)
                    s_xg[tp].v += 16
                    t_xg = s_xg[tp].v
                    T.wait_ge(s_xg[tp].h, t_xg)
                    T.wait_ge(s_dve.h, s_dve.v)
                    T.wait_ge(s_act.h, s_act.v)
                    for dt in range(8):
                        mark(T, T.transpose(pp_tr[:, dt * 128:(dt + 1) * 128],
                                            xg[:, tp * DM + dt * 128:tp * DM + (dt + 1) * 128],
                                            idb[:]), s_pe)
                    XGP[tp] = s_pe.v
                    A.wait_ge(s_pe.h, s_pe.v)
                    if XGU[gp]:
                        A.wait_ge(s_pe.h, XGU[gp])
                    aa()
                    xgt_dst = xgt[:, gp * 8 * GQ * 128:(gp + 1) * 8 * GQ * 128] \
                        .rearrange("p (dt t c) -> p dt t c", dt=8, t=GQ, c=128)[:, :, tt, :]
                    mark(A, A.activation(
                        xgt_dst,
                        pp_tr[:, 0:1024].rearrange("p (dt c) -> p dt c", dt=8, c=128),
                        AF.Copy), s_act)
                XGT_T = s_act.v
                for ht in range(NHT):
                    hp = ht % 2
                    o1sl = slice(hp * O1S, hp * O1S + GQ * 128)
                    if ht == 0:
                        T.wait_ge(s_w1.h, w1_tok)
                        T.wait_ge(s_act.h, XGT_T)
                    if PSR_T[hp]:
                        T.wait_ge(s_act.h, PSR_T[hp])
                    for dt in range(8):
                        mark(T, T.matmul(pp_o1[:, o1sl],
                                         w1q[:, dt * HQ + ht * 128:dt * HQ + (ht + 1) * 128],
                                         xgt[:, (gp * 8 + dt) * GQ * 128:
                                                (gp * 8 + dt + 1) * GQ * 128],
                                         start=(dt == 0), stop=(dt == 7)), s_pe)
                    O1_T[hp] = s_pe.v
                    XGU[gp] = s_pe.v
                    A.wait_ge(s_pe.h, O1_T[hp])
                    if GEL_ALL:
                        A.wait_ge(s_dve.h, GEL_ALL)
                    aa()
                    mark(A, A.activation(gx[:], pp_o1[:, o1sl], AF.Copy), s_act)
                    aa()
                    PSR_T[hp] = mark(A, A.activation(gsc1[:], pp_o1[:, o1sl], AF.Square), s_act)
                    aa()
                    mark(A, A.activation(gsc1[:], gsc1[:], AF.Copy, scale=GK, bias=SQ2PI), s_act)
                    V.wait_ge(s_act.h, s_act.v)
                    vv()
                    mark(V, V.tensor_mul(gsc2[:], gx[:], gsc1[:]), s_dve)
                    A.wait_ge(s_dve.h, s_dve.v)
                    aa()
                    mark(A, A.activation(gsc2[:], gsc2[:], AF.Tanh), s_act)
                    aa()
                    TH_T = mark(A, A.activation(gsc2[:], gsc2[:], AF.Copy,
                                                scale=0.5, bias=0.5), s_act)
                    V.wait_ge(s_act.h, TH_T)
                    vv()
                    if HGU[0]:
                        V.wait_ge(s_pe.h, HGU[0])
                    GEL_T[hp] = mark(V, V.tensor_mul(hgel[:, ht * GQ * 128:(ht + 1) * GQ * 128],
                                                     gx[:], gsc2[:]), s_dve)
                    GEL_ALL = GEL_T[hp]
                if cfg.taps and b == 0 and hq == 0 and g == 0:
                    Y.wait_ge(s_dve.h, s_dve.v)
                    dma(Y, tap["xgt"][0][:, 0:8 * GQ * 128],
                        xgt[:, 0:8 * GQ * 128], s_sy)
                    dma(Y, tap["hgel"][0][:, 0:NHT * GQ * 128], hgel[:], s_sy)
                    Y.wait_ge(s_sy.h, s_sy.v)
                    V.wait_ge(s_sy.h, s_sy.v)
                for tt in range(GQ):
                    tau = g * GQ + tt
                    pr = (tau // 2) % 2
                    pi = tau % 2
                    for ht in range(NHT):
                        if ht == 0:
                            T.wait_ge(s_dve.h, GEL_ALL)
                            T.wait_ge(s_w2.h, w2_tok)
                            if O2C[0]:
                                T.wait_ge(s_act.h, O2C[0])
                            if WR_EV[pr] and pi == 0:
                                T.wait_ge(s_ctb[pr].h, WR_EV[pr])
                        for h2 in range(2):
                            mark(T, T.matmul(pp_o2[:, h2 * 512:(h2 + 1) * 512],
                                             hgel[:, ht * GQ * 128 + tt * 128:
                                                    ht * GQ * 128 + (tt + 1) * 128],
                                             w2q[:, ht * DM + h2 * 512:ht * DM + (h2 + 1) * 512],
                                             start=(ht == 0), stop=(ht == NHT - 1)), s_pe)
                    HGU[0] = s_pe.v
                    A.wait_ge(s_pe.h, s_pe.v)
                    if WR_EV[pr] and pi == 0:
                        A.wait_ge(s_ctb[pr].h, WR_EV[pr])
                    aa()
                    O2C[0] = mark(A, A.activation(
                        wrow[:, (pr * 2 + pi) * DM:(pr * 2 + pi + 1) * DM],
                        pp_o2[:, 0:DM], AF.Copy), s_act)
                    if cfg.taps and b == 0 and hq == 0 and g == 0 and tau == 0:
                        Y.wait_ge(s_dve.h, s_dve.v)
                        dma(Y, tap["wrow"][0][:, 0:DM], wrow[:, 0:DM], s_sy)
                        Y.wait_ge(s_sy.h, s_sy.v)
                        V.wait_ge(s_sy.h, s_sy.v)
                    if pi == 1:
                        G.wait_ge(s_act.h, s_act.v)
                        G.wait_ge(s_ctb[pr].h, s_ctb[pr].v)
                        rows = slice(128 + (tau - 1) * 128, 128 + (tau + 1) * 128)
                        srcv = wrow[:, pr * 2 * DM:(pr + 1) * 2 * DM].rearrange(
                            "p (t d) -> p t d", d=DM)
                        dstv = contrib[b][rows, :].rearrange("(t p) d -> p t d", p=128)
                        if hq == 0:
                            ctb_tok[tau // 2] = dma(G, dstv, srcv, s_ctb[pr])
                        else:
                            ctb_tok[tau // 2] = dma_acc(dstv, srcv, s_ctb[pr])
                        WR_EV[pr] = ctb_tok[tau // 2]
        GT_T = [0, 0]
        for pp2 in range(NTAU // 2):
            pr = pp2 % 2
            rows = slice(128 + pp2 * 256, 128 + (pp2 + 1) * 256)
            G.wait_ge(s_ctb[pr].h, s_ctb[pr].v)
            t_r = dma(G, wrow[:, pr * 2 * DM:(pr + 1) * 2 * DM].rearrange(
                "p (t d) -> p t d", d=DM),
                contrib[b][rows, :].rearrange("(t p) d -> p t d", p=128), s_gp)
            V.wait_ge(s_gp.h, t_r)
            for tt in range(2):
                tau = pp2 * 2 + tt
                vv()
                mark(V, V.tensor_scalar(
                    wrow[:, (pr * 2 + tt) * DM:(pr * 2 + tt + 1) * DM],
                    wrow[:, (pr * 2 + tt) * DM:(pr * 2 + tt + 1) * DM],
                    gsb[:, boff + tau:boff + tau + 1], None, OP.mult), s_dve)
            G.wait_ge(s_dve.h, s_dve.v)
            GT_T[pr] = dma(G, contrib[b][rows, :].rearrange("(t p) d -> p t d", p=128),
                           wrow[:, pr * 2 * DM:(pr + 1) * 2 * DM].rearrange(
                               "p (t d) -> p t d", d=DM), s_ctb[pr])
            ctb_tok[pp2] = GT_T[pr]
        uu = posu[:]
        if state["POST"]:
            V.wait_ge(s_gp.h, state["POST"])
        vv()
        mark(V, V.tensor_single_scalar(uu[:, 0:NTAU], idsb[:, boff:boff + NTAU],
                                       127, OP.bitwise_and), s_dve)
        vv()
        mark(V, V.tensor_single_scalar(uu[:, 0:NTAU], uu[:, 0:NTAU], PD, OP.mult), s_dve)
        vv()
        mark(V, V.tensor_single_scalar(uu[:, NTAU:2 * NTAU], idsb[:, boff:boff + NTAU], 7,
                                       OP.arith_shift_right), s_dve)
        vv()
        mark(V, V.tensor_tensor(uu[:, 0:NTAU], uu[:, 0:NTAU], uu[:, NTAU:2 * NTAU],
                                OP.add), s_dve)
        PJ_T = s_dve.v
        G.wait_ge(s_dve.h, PJ_T)
        pdflat = posdense[b].rearrange("p c -> (p c)").rearrange("(x o) -> x o", o=1)
        for tau in range(NTAU):
            G.indirect_dma_start(
                out=pdflat, out_offset=IndirectOffsetOnAxis(uu[:, tau:tau + 1], 0),
                in_=jvb[:, tau:tau + 1], in_offset=None,
                bounds_check=bcpd, oob_is_err=False,
            ).then_inc(s_gp.h, 16)
            s_gp.v += 16
        POS_T = s_gp.v
        state["POST"] = POS_T
        G.wait_ge(s_ctb[0].h, s_ctb[0].v)
        G.wait_ge(s_ctb[1].h, s_ctb[1].v)
        G.collective_compute("AllGather", OP.bypass, RG,
                             [contrib[b]], [contribAG[b]]).then_inc(s_cc.h, 1)
        s_cc.v += 1
        G.wait_ge(s_gp.h, POS_T)
        G.collective_compute("AllGather", OP.bypass, RG,
                             [posdense[b]], [posAG[b]]).then_inc(s_cc.h, 1)
        s_cc.v += 1
        if cfg.taps:
            G.wait_ge(s_cc.h, s_cc.v)
            dma(G, tap["contrib"][b], contrib[b], s_gp)
            dma(G, tap["pos"][b], posAG[b], s_gp)
            G.wait_ge(s_gp.h, s_gp.v)
        return s_cc.v

    def finalize(b, cc_tok):
        Y.wait_ge(s_sy.h, s_sy.v)
        G.wait_ge(s_cc.h, cc_tok)
        t_pos = dma(G, posownD[b], posAG[b][:, bass.ts(regs["c"], NBLK)], s_gp)
        G.wait_ge(s_gp.h, t_pos)
        t_a8 = dma(G, addr8[:].rearrange("p (e n) -> p e n", n=NBLK),
                   posownD[b].rearrange("(e p) n -> p e n", p=128), s_gp)
        G.wait_ge(s_gp.h, t_a8)
        ADD_T = [0, 0]
        PRV = 0
        for i in range(NBLK):
            if PRV:
                Y.wait_ge(s_dve.h, PRV)
            t_x1 = dma(Y, x1r[:, 0:DM], x1own[b][i * 128:(i + 1) * 128, :], s_sy)
            for e in range(8):
                ip = e % 2
                sl = slice(ip * DM, (ip + 1) * DM)
                if ADD_T[ip]:
                    G.wait_ge(s_dve.h, ADD_T[ip])
                G.indirect_dma_start(
                    out=gat1[:, sl], out_offset=None,
                    in_=contribAG[b],
                    in_offset=IndirectOffsetOnAxis(addr8[:, e * NBLK + i:e * NBLK + i + 1], 0),
                ).then_inc(s_fg[ip].h, 16)
                s_fg[ip].v += 16
                V.wait_ge(s_fg[ip].h, s_fg[ip].v)
                if e == 0:
                    vv()
                    ADD_T[ip] = mark(V, V.tensor_copy(accr[:], gat1[:, sl]), s_dve)
                else:
                    vv()
                    ADD_T[ip] = mark(V, V.tensor_add(accr[:], accr[:], gat1[:, sl]), s_dve)
            V.wait_ge(s_sy.h, t_x1)
            V.wait_ge(s_act.h, s_act.v)
            V.wait_ge(s_outd.h, s_outd.v)
            vv()
            PRV = mark(V, V.tensor_add(z2b[:], x1r[:, 0:DM], accr[:]), s_dve)
            A.wait_ge(s_dve.h, s_dve.v)
            aa()
            mark(A, A.activation(zsc[:], z2b[:], AF.Copy, accum_out=statb[:, 0:1]), s_act)
            aa()
            mark(A, A.activation(zsc[:], z2b[:], AF.Square, accum_out=statb[:, 1:2]), s_act)
            V.wait_ge(s_act.h, s_act.v)
            vv()
            mark(V, V.tensor_scalar_mul(statb[:, 2:3], statb[:, 0:1], 1.0 / DM), s_dve)
            vv()
            mark(V, V.tensor_mul(statb[:, 3:4], statb[:, 2:3], statb[:, 2:3]), s_dve)
            vv()
            mark(V, V.tensor_scalar(statb[:, 4:5], statb[:, 1:2], 1.0 / DM, EPS,
                                    OP.mult, OP.add), s_dve)
            vv()
            mark(V, V.tensor_sub(statb[:, 4:5], statb[:, 4:5], statb[:, 3:4]), s_dve)
            emit_rsqrt(statb[:, 4:5], statb[:, 5:6], statb[:, 6:7])
            vv()
            mark(V, V.tensor_scalar(outb[:], z2b[:], statb[:, 2:3],
                                    statb[:, 5:6], OP.subtract, OP.mult), s_dve)
            Y.wait_ge(s_dve.h, s_dve.v)
            dma(Y, out[b, i], outb[:], s_outd)
            yield

    def chain2(*gens):
        for gg in gens:
            for _ in gg:
                yield

    SA = cfg.stop_after
    if SA is not None:
        for b in range(B):
            for _ in mamba_scan(b):
                pass
            if SA == "mamba":
                continue
            for _ in owner_ln1(b):
                pass
            if SA == "owner":
                continue
            for _ in topk(b):
                pass
            if SA == "topk":
                continue
            cctok = ffn(b, state["IDSTOK"][b])
            if SA == "ffn":
                continue
            for _ in finalize(b, cctok):
                pass
    else:
        g0 = mamba_scan(0)
        for _ in g0:
            pass
        for _ in owner_ln1(0):
            pass
        for _ in topk(0):
            pass
        aux0 = chain2(mamba_scan(1), owner_ln1(1), topk(1))
        c0 = ffn(0, state["IDSTOK"][0], aux=aux0)
        for _ in aux0:
            pass
        f0 = finalize(0, c0)
        c1 = ffn(1, state["IDSTOK"][1], aux=f0, aux_every=2)
        for _ in f0:
            pass
        for _ in finalize(1, c1):
            pass

    if SA is not None:
        for e_, sm in ((V, s_dve), (A, s_act), (T, s_pe), (G, s_gp), (Y, s_sy)):
            e_.wait_ge(sm.h, sm.v)
        G.wait_ge(s_cc.h, s_cc.v)
        Y.wait_ge(s_dve.h, s_dve.v)
        dma(Y, out[0, 0], outb[:], s_outd)
    Y.wait_ge(s_outd.h, s_outd.v)
    ctx.close()
    return nc, tap




def make_inputs(cfg, x, A_log, W_delta, W_dt, b_dt, W_B, W_C, Dv, gate_w, w1, w2):
    x = np.asarray(x, np.float32)
    onehots = np.kron(np.eye(16, dtype=np.float32), np.ones((1, 128), np.float32))
    gm = np.zeros((128, 128), np.float32)
    for p in range(128):
        gm[(p // 16) * 16:(p // 16) * 16 + 16, p] = 1.0
    tm = np.zeros((128, 128), np.float32)
    for p2 in range(128):
        for p in range((p2 // 16) * 16, p2):
            tm[p, p2] = 1.0
    tokmap = np.zeros((16, cfg.FB), np.float32)
    for s in range(16):
        for f in range(cfg.FB):
            tokmap[s, f] = agrow_of_token(cfg, tok_of(cfg, s, f))
    tokmap128 = np.ascontiguousarray(
        tokmap.reshape(16, 8, cfg.FB // 8).transpose(0, 1, 2).reshape(128, cfg.FB // 8))
    maps = []
    for c in range(8):
        jv = np.zeros((128, cfg.NTAU), np.int32)
        for tau in range(cfg.NTAU):
            for p in range(128):
                jv[p, tau] = cfg.CROW * c + 128 + 128 * tau + p
        sl = slice(128 * c, 128 * (c + 1))
        meta = np.zeros((1, 16), np.int32)
        meta[0, 0] = c
        for g in range(8):
            meta[0, 1 + g] = (g + c) % 8
        maps.append(dict(
            xT=np.ascontiguousarray(x[:, :, sl].transpose(0, 2, 1)),
            projW=np.ascontiguousarray(np.concatenate(
                [np.asarray(W_delta, np.float32)[sl],
                 np.asarray(W_B, np.float32)[sl],
                 np.asarray(W_C, np.float32)[sl]], axis=1)),
            wdt_ext=np.ascontiguousarray(np.concatenate(
                [np.asarray(W_dt, np.float32)[:, sl],
                 np.asarray(b_dt, np.float32)[None, sl]], axis=0)),
            a_sl=np.ascontiguousarray(np.asarray(A_log, np.float32)[sl]),
            d_sl=np.ascontiguousarray(np.asarray(Dv, np.float32)[sl, None]),
            gate_w=np.asarray(gate_w, np.float32),
            w1e=np.ascontiguousarray(np.asarray(w1, np.float32)[c]),
            w2e=np.ascontiguousarray(np.asarray(w2, np.float32)[c]),
            idn32=np.eye(128, dtype=np.float32),
            idnbf=np.eye(128, dtype=BF),
            onehots=onehots, gmask=gm, tmask=tm, tokmap=tokmap,
            tokmap128=tokmap128, jvals=jv,
            meta=meta,
        ))
    return maps


def assemble_out(cfg, outs):
    full = np.zeros((cfg.B, cfg.L, cfg.DM), np.float32)
    for c in range(8):
        o = outs[c]
        for i in range(cfg.NBLK):
            blk = 8 * i + c
            full[:, blk * 128:(blk + 1) * 128, :] = o[:, i]
    return full


def kernel(L0_attn_x, L0_attn_A_log, L0_attn_W_delta, L0_attn_W_dt, L0_attn_b_dt,
           L0_attn_W_B, L0_attn_W_C, L0_attn_D, L0_ffn_gate_w, L0_ffn_w1, L0_ffn_w2):
    cfg = Cfg()
    nc, _ = build_program(cfg)
    in_maps = make_inputs(cfg, L0_attn_x, L0_attn_A_log, L0_attn_W_delta, L0_attn_W_dt,
                          L0_attn_b_dt, L0_attn_W_B, L0_attn_W_C, L0_attn_D,
                          L0_ffn_gate_w, L0_ffn_w1, L0_ffn_w2)
    res = run_bass_kernel_spmd(nc, in_maps, core_ids=list(range(8)))
    return assemble_out(cfg, [r["out"] for r in res.results])


# version 11 (final)
# speedup vs baseline: 1.0238x; 1.0095x over previous
import numpy as np
import ml_dtypes
import concourse.bass as bass
import concourse.mybir as mybir
from concourse.bass_core_rust import IndirectOffsetOnAxis
from concourse.bass_utils import run_bass_kernel_spmd
from contextlib import ExitStack

F32 = mybir.dt.float32
BF16 = mybir.dt.bfloat16
I32 = mybir.dt.int32
AF = mybir.ActivationFunctionType
OP = mybir.AluOpType
ET = mybir.EngineType
BF = ml_dtypes.bfloat16

SQ2PI = 0.7978845608028654
GK = 0.044715 * SQ2PI
EPS = 1e-5
BISECT_ITERS = 36


class Cfg:
    def __init__(self, L=16384, HID=4096, CH=512, taps=False, stop_after=None):
        self.stop_after = stop_after
        self.B = 2
        self.L = L
        self.DM = 1024
        self.N = 16
        self.HID = HID
        self.HQ = HID // 4
        self.NHT = self.HQ // 128
        self.K = L // 4
        self.TOK = L // 8
        self.CH = CH
        self.NCH = L // CH
        self.NBLK = self.TOK // 128
        self.FB = self.TOK // 2
        self.PD = L // 128
        self.NTAU = self.K // 128
        self.GQ = 4 if self.NTAU % 8 == 0 else (2 if self.NTAU % 4 == 0 else 1)
        self.NG = self.NTAU // self.GQ
        self.CROW = self.K + 128
        self.QAG = max(1, self.NCH // 4)
        self.taps = taps
        assert self.NBLK % 2 == 0 and self.FB >= 128 and self.CH <= 512


def tok_of(cfg, s, f):
    r = s >> 1
    i = (s & 1) * (cfg.NBLK // 2) + (f >> 7)
    j = f & 127
    return 128 * (8 * i + r) + j


def agrow_of_token(cfg, t):
    i, r, j = t // 1024 if False else (t // 128) // 8, (t // 128) % 8, t % 128
    return cfg.TOK * r + 128 * i + j


class Sem:
    def __init__(self, nc, name):
        self.h = nc.alloc_semaphore(name)
        self.v = 0


def build_program(cfg):
    nc = bass.Bass("TRN2", target_bir_lowering=False, dynamic_dma_scratch_size=49152)
    B, L, DM, N = cfg.B, cfg.L, cfg.DM, cfg.N
    CH, NCH, K, TOK, NBLK = cfg.CH, cfg.NCH, cfg.K, cfg.TOK, cfg.NBLK
    HID, HQ, NHT, NTAU = cfg.HID, cfg.HQ, cfg.NHT, cfg.NTAU
    GQ, NG, CROW, QAG = cfg.GQ, cfg.NG, cfg.CROW, cfg.QAG
    FB, PD = cfg.FB, cfg.PD
    RG = [list(range(8))]
    AGTOT = 8 * TOK

    def ein(name, shape, dt=F32):
        return nc.dram_tensor(name, shape, dt, kind="ExternalInput").ap()

    xT = ein("xT", [B, 128, L])
    projW = ein("projW", [128, 96])
    wdt_ext = ein("wdt_ext", [65, 128])
    a_sl = ein("a_sl", [128, N])
    d_sl = ein("d_sl", [128, 1])
    gate_w = ein("gate_w", [DM, 8])
    w1e = ein("w1e", [DM, HID])
    w2e = ein("w2e", [HID, DM])
    idn32 = ein("idn32", [128, 128])
    idnbf = ein("idnbf", [128, 128], BF16)
    onehots = ein("onehots", [16, 2048])
    gmask = ein("gmask", [128, 128])
    tmask = ein("tmask", [128, 128])
    tokmap = ein("tokmap", [16, FB])
    tokmap128 = ein("tokmap128", [128, PD])
    jvals = ein("jvals", [128, NTAU], I32)
    meta = ein("meta", [1, 16], I32)

    out = nc.dram_tensor("out", [B, NBLK, 128, DM], F32, kind="ExternalOutput").ap()

    def idram(name, shape, dt=F32):
        return nc.dram_tensor(name, shape, dt, kind="Internal").ap()

    projpart = [idram(f"projpart{b}", [96, L]) for b in range(B)]
    projAR = [idram(f"projAR{b}", [96, L]) for b in range(B)]
    deltaD = [idram(f"delta{b}", [128, L]) for b in range(B)]
    zin = [idram(f"zin{b}", [128, L], BF16) for b in range(B)]
    zAG = [idram(f"zAG{b}", [DM, L], BF16) for b in range(B)]
    x1own = [idram(f"x1own{b}", [TOK, DM], BF16) for b in range(B)]
    x1AG = [idram(f"x1AG{b}", [AGTOT, DM], BF16) for b in range(B)]
    probsT = [idram(f"probsT{b}", [8, TOK]) for b in range(B)]
    probsAG = [idram(f"probsAG{b}", [64, TOK]) for b in range(B)]
    ubounce = [idram(f"ubounce{b}", [16, FB], I32) for b in range(B)]
    gbounce = [idram(f"gbounce{b}", [16, FB]) for b in range(B)]
    idg = [idram(f"idg{b}", [128, 2 * NTAU]) for b in range(B)]
    posdense = [idram(f"posdense{b}", [128, PD], I32) for b in range(B)]
    posAG = [idram(f"posAG{b}", [1024, PD], I32) for b in range(B)]
    contrib = [idram(f"contrib{b}", [CROW, DM], BF16) for b in range(B)]
    contribAG = [idram(f"contribAG{b}", [8 * CROW, DM], BF16) for b in range(B)]
    zownD = [idram(f"zownD{b}", [DM, NBLK * 128], BF16) for b in range(B)]
    probsAGx2 = [idram(f"probsAGx2{b}", [128, TOK]) for b in range(B)]
    posownD = [idram(f"posownD{b}", [1024, NBLK], I32) for b in range(B)]
    probsRot = [idram(f"probsRot{b}", [64, TOK]) for b in range(B)]

    tap = {}
    if cfg.taps:
        def tout(name, shape, dt=F32):
            tap[name] = nc.dram_tensor("tap_" + name, shape, dt, kind="ExternalOutput").ap()
        tout("delta", [B, 128, L])
        tout("y", [B, 128, L])
        tout("z", [B, DM, L])
        tout("x1", [B, AGTOT, DM])
        tout("probs", [B, 64, TOK])
        tout("bis", [B, 128, 8])
        tout("idg", [B, 128, 2 * NTAU])
        tout("contrib", [B, CROW, DM])
        tout("xgt", [1, 128, 2 * 8 * 64 * 128], BF16)
        tout("c0", [1, CROW, DM], BF16)
        tout("w1q", [1, 128, 8 * 1024], BF16)
        tout("wrow", [1, 128, 4 * 1024], BF16)
        tout("w2q", [1, 128, 8 * 1024], BF16)
        tout("hgel", [1, 128, 64 * 64 * 128], BF16)
        tout("pos", [B, 1024, PD], I32)

    ctx = ExitStack()
    sbt = lambda name, shape, dt=F32: ctx.enter_context(nc.sbuf_tensor(name, shape, dt))
    pst = lambda name, shape, dt=F32: ctx.enter_context(nc.psum_tensor(name, shape, dt))
    V, A, T, G, Y = nc.vector, nc.scalar, nc.tensor, nc.gpsimd, nc.sync

    acol = sbt("acol", [128, N])
    dcol = sbt("dcol", [128, 1])
    idb = sbt("idb", [128, 128], BF16)
    id32 = sbt("id32", [128, 128])
    onehb = sbt("onehb", [16, 2048], BF16)
    gmb = sbt("gmb", [128, 128])
    tmb = sbt("tmb", [128, 128])
    tokm = sbt("tokm", [16, FB])
    jvb = sbt("jvb", [128, NTAU], I32)
    metab = sbt("metab", [1, 16], I32)
    gwb = sbt("gwb", [128, 8 * 8], BF16)
    projwb = sbt("projwb", [128, 96])
    wdtb = sbt("wdtb", [65, 128])
    onesc = sbt("onesc", [128, max(FB, 1024)], BF16)
    zerobf = sbt("zerobf", [128, DM], BF16)
    zrow32 = sbt("zrow32", [128, PD])
    w1q = sbt("w1q", [128, 8 * HQ], BF16)
    w2q = sbt("w2q", [128, NHT * DM], BF16)
    ALIAS = (8 * HQ * 2 >= 2 * CH * 4) and (NHT * DM * 2 >= 2 * CH * 4)
    xc = sbt("xc", [128, 2 * CH])
    dc = sbt("dc", [128, 2 * CH])
    dxc = sbt("dxc", [128, 2 * CH], BF16)
    bccB = sbt("bccB", [16, 2 * CH], BF16)
    bccC = sbt("bccC", [16, 2 * CH], BF16)
    dab = sbt("dab", [128, 2 * CH], BF16)
    dbxb = sbt("dbxb", [128, 2 * CH], BF16)
    hb = sbt("hb", [128, N * CH], BF16)
    hcb = sbt("hcb", [128, 8 * CH], BF16)
    yhalf = sbt("yhalf", [128, CH])
    yb = sbt("yb", [128, 2 * CH])
    zbuf = sbt("zbuf", [128, 2 * CH], BF16)
    if ALIAS:
        w1f = w1q[:].bitcast(F32)
        w2f = w2q[:].bitcast(F32)
        prhs = w1f[0:65, 0:2 * CH]
        pevac = w2f[0:96, 0:2 * CH]
        sfp1 = w1f[:, 2 * CH:4 * CH] if 8 * HQ * 2 >= 4 * CH * 4 else None
        sfp2 = w2f[:, 2 * CH:4 * CH] if NHT * DM * 2 >= 4 * CH * 4 else None
        if sfp1 is None:
            sfp1 = sbt("sfp1", [128, 2 * CH])
        if sfp2 is None:
            sfp2 = sbt("sfp2", [128, 2 * CH])
    else:
        prhs = sbt("prhs", [65, 2 * CH])[:]
        pevac = sbt("pevac", [96, 2 * CH])[:]
        sfp1 = sbt("sfp1", [128, 2 * CH])[:]
        sfp2 = sbt("sfp2", [128, 2 * CH])[:]
    zblk = sbt("zblk", [128, 2 * 8 * 128], BF16)
    ztm = sbt("ztm", [128, 2 * DM], BF16)
    zsc = sbt("zsc", [128, DM], BF16)
    x1bf = sbt("x1bf", [128, 2 * DM], BF16)
    x1dm = sbt("x1dm", [128, 8 * 128], BF16)
    statb = sbt("statb", [128, 16])
    smv = sbt("smv", [128, 16])
    sct = sbt("sct", [8, 128])
    ptb = sbt("ptb", [128, 8], BF16)
    TALIAS = (N * CH * 2 >= 4 * FB * 4) and (8 * CH * 2 >= FB * 4 + 2 * NTAU * 4)
    if TALIAS:
        hbf = hb[:].bitcast(F32)
        hcf = hcb[:].bitcast(F32)
        pbis = hbf[:, 0:FB]
        bmask = hbf[:, FB:2 * FB]
        bmask2 = hbf[:, 2 * FB:3 * FB]
        cumb = hbf[:, 3 * FB:4 * FB]
        ucols = hcf[:, 0:max(FB, 2 * NTAU)]
    else:
        pbis = sbt("pbis", [128, FB])[:]
        bmask = sbt("bmask", [128, FB])[:]
        bmask2 = sbt("bmask2", [128, FB])[:]
        cumb = sbt("cumb", [128, FB])[:]
        ucols = sbt("ucols", [128, max(FB, 2 * NTAU)])[:]
    bcols = sbt("bcols", [128, 8])
    tk128 = sbt("tk128", [128, PD])
    g128 = sbt("g128", [128, PD])
    u32b = sbt("u32b", [128, PD], I32)
    pairb = sbt("pairb", [128, 2 * PD])
    posu = sbt("posu", [128, 2 * NTAU], I32)
    idsb = sbt("idsb", [128, 2 * NTAU], I32)
    gsb = sbt("gsb", [128, 2 * NTAU])
    xg = sbt("xg", [128, 2 * DM], BF16)
    xgt = sbt("xgt", [128, 2 * 8 * GQ * 128], BF16)
    hgel = sbt("hgel", [128, NHT * GQ * 128], BF16)
    gx = sbt("gx", [128, GQ * 128], BF16)
    gsc1 = sbt("gsc1", [128, GQ * 128], BF16)
    gsc2 = sbt("gsc2", [128, GQ * 128], BF16)
    wrow = sbt("wrow", [128, 4 * DM], BF16)
    addr8 = sbt("addr8", [128, 8 * NBLK], I32)
    gat1 = sbt("gat1", [128, 2 * DM], BF16)
    accr = sbt("accr", [128, DM])
    x1r = sbt("x1r", [128, DM], BF16)
    z2b = sbt("z2b", [128, DM])
    outb = sbt("outb", [128, DM])

    pp_brep = pst("pp_brep", [128, CH])
    pp_crep = pst("pp_crep", [128, CH])
    O1S = max(GQ * 128, 512)
    pp_o1 = pst("pp_o1", [128, 2 * O1S])
    pp_o2 = pst("pp_o2", [128, DM])
    pp_tr = pst("pp_tr", [128, 1024], BF16)
    pp_msc = pst("pp_msc", [128, CH])

    S = lambda name: Sem(nc, name)
    s_ld = S("s_ld")
    s_ldg = S("s_ldg")
    s_xc = [S("s_xc0"), S("s_xc1")]
    s_dc = [S("s_dc0"), S("s_dc1")]
    s_bc = [S("s_bc0"), S("s_bc1")]
    s_pe = S("s_pe")
    s_act = S("s_act")
    s_dve = S("s_dve")
    s_gp = S("s_gp")
    s_sy = S("s_sy")
    s_cc = S("s_cc")
    s_z = [S("s_z0"), S("s_z1")]
    s_w1 = S("s_w1")
    s_w2 = S("s_w2")
    s_ctb = [S("s_ctb0"), S("s_ctb1")]
    s_outd = S("s_outd")
    s_pv = [S("s_pv0"), S("s_pv1")]
    s_zb = [S("s_zb0"), S("s_zb1")]
    s_xg = [S("s_xg0"), S("s_xg1")]
    s_fg = [S("s_fg0"), S("s_fg1")]
    s_ty = [S("s_ty0"), S("s_ty1")] if cfg.taps else None
    s_ps = S("s_ps")

    def dma(eng, o, i_, sem):
        eng.dma_start(out=o, in_=i_).then_inc(sem.h, 16)
        sem.v += 16
        return sem.v

    def dma_acc(o, i_, sem):
        G.dma_start(out=o, in_=i_, accum_op=OP.add).then_inc(sem.h, 16)
        sem.v += 16
        return sem.v

    dma_acc2 = dma_acc

    def mark(eng_unused, instr, sem):
        instr.then_inc(sem.h, 1)
        sem.v += 1
        return sem.v

    def vv():
        V.wait_ge(s_dve.h, s_dve.v)

    def aa():
        A.wait_ge(s_act.h, s_act.v)

    def emit_rsqrt(vin, vout, scr):
        vi = vin.bitcast(I32)
        yi = vout.bitcast(I32)
        vv()
        mark(V, V.tensor_single_scalar(yi, vi, 1, OP.arith_shift_right), s_dve)
        vv()
        mark(V, V.tensor_single_scalar(yi, yi, -1, OP.mult), s_dve)
        vv()
        mark(V, V.tensor_single_scalar(yi, yi, 0x5F3759DF, OP.add), s_dve)
        for _ in range(3):
            vv()
            mark(V, V.tensor_mul(scr, vout, vout), s_dve)
            vv()
            mark(V, V.tensor_mul(scr, scr, vin), s_dve)
            vv()
            mark(V, V.tensor_scalar(scr, scr, -0.5, 1.5, OP.mult, OP.add), s_dve)
            vv()
            mark(V, V.tensor_mul(vout, vout, scr), s_dve)

    for (dst, src) in ((acol, a_sl), (dcol, d_sl), (id32, idn32), (tokm, tokmap),
                       (metab, meta), (projwb, projW), (wdtb, wdt_ext), (jvb, jvals)):
        dma(Y, dst[:], src, s_ld)
    dma(G, gwb[:].rearrange("p (dt e) -> p dt e", e=8),
        gate_w.rearrange("(dt p) e -> p dt e", p=128), s_ldg)
    for (dst, src) in ((onehb, onehots), (idb, idnbf)):
        dma(G, dst[:], src, s_ldg)
    for (dst, src) in ((gmb, gmask), (tmb, tmask), (tk128, tokmap128)):
        dma(Y, dst[:], src, s_ld)
    LD0 = s_ld.v
    LD0G = s_ldg.v

    V.wait_ge(s_ld.h, LD0)
    V.wait_ge(s_ldg.h, LD0G)
    vv()
    mark(V, V.memset(onesc[:], 1.0), s_dve)
    vv()
    mark(V, V.memset(zerobf[:], 0.0), s_dve)
    vv()
    mark(V, V.memset(zrow32[:], 0.0), s_dve)
    vv()
    mark(V, V.memset(prhs[:], 1.0), s_dve)
    vv()
    mark(V, V.memset(smv[:], 0.0), s_dve)
    G.wait_ge(s_dve.h, s_dve.v)
    for b in range(B):
        dma(G, contrib[b][0:128, :], zerobf[:], s_gp)
        dma(G, posdense[b], zrow32[:], s_gp)
        dma(G, idg[b], zrow32[:, 0:2 * NTAU], s_gp)
    ZIDG = s_gp.v
    G.wait_ge(s_gp.h, ZIDG)

    vv()
    mark(V, V.tensor_scalar(dcol[:], dcol[:], 1.0, None, OP.add), s_dve)
    A.wait_ge(s_ld.h, LD0)
    aa()
    mark(A, A.activation(acol[:], acol[:], AF.Exp), s_act)
    V.wait_ge(s_act.h, s_act.v)
    vv()
    mark(V, V.tensor_scalar_mul(acol[:], acol[:], -1.0), s_dve)
    ACOL_RDY = s_dve.v

    G.wait_ge(s_ld.h, LD0)
    G.wait_ge(s_ldg.h, LD0G)
    bc128 = G.to_reg(128 * NTAU - 1)
    bcpd = G.to_reg(128 * PD - 1)
    regs = {"c": nc.values_load(metab[0:1, 0:1], engines=[ET.Pool],
                                min_val=0, max_val=7,
                                skip_runtime_bounds_check=True)}

    PEV = [0, 0]
    PEX = [0, 0]
    PEVD = [0, 0]
    for b in range(B):
        for ci in range(NCH):
            p = ci % 2
            csl = slice(p * CH, (p + 1) * CH)
            if b > 0 or ci >= 2:
                Y.wait_ge(s_pe.h, PEX[p])
            t_x = dma(Y, xc[:, csl], xT[b, :, ci * CH:(ci + 1) * CH], s_xc[p])
            T.wait_ge(s_xc[p].h, t_x)
            T.wait_ge(s_ld.h, LD0)
            T.wait_ge(s_ldg.h, LD0G)
            if b > 0 or ci >= 2:
                T.wait_ge(s_dve.h, PEV[p])
            PEX[p] = mark(T, T.matmul(pp_msc[0:96, 0:CH], projwb[:], xc[:, csl],
                                      start=True, stop=True), s_pe)
            V.wait_ge(s_pe.h, s_pe.v)
            if b > 0 or ci >= 2:
                V.wait_ge(s_pv[p].h, PEVD[p])
            vv()
            PEV[p] = mark(V, V.tensor_copy(pevac[:, p * CH:(p + 1) * CH],
                                           pp_msc[0:96, 0:CH]), s_dve)
            Y.wait_ge(s_dve.h, PEV[p])
            PEVD[p] = dma(Y, projpart[b][:, ci * CH:(ci + 1) * CH],
                          pevac[:, p * CH:(p + 1) * CH], s_pv[p])
        G.wait_ge(s_pv[0].h, s_pv[0].v)
        G.wait_ge(s_pv[1].h, s_pv[1].v)
        G.collective_compute("AllReduce", OP.add, RG,
                             [projpart[b]], [projAR[b]]).then_inc(s_cc.h, 1)
        s_cc.v += 1
    AR_DONE = s_cc.v

    DPRE = [0, 0]
    DSF = [0, 0]
    PRH = [0, 0]
    Y.wait_ge(s_cc.h, AR_DONE)
    for b in range(B):
        for ci in range(NCH):
            p = ci % 2
            csl = slice(p * CH, (p + 1) * CH)
            if b > 0 or ci >= 2:
                Y.wait_ge(s_pe.h, PRH[p])
            t_p = dma(Y, prhs[0:64, csl], projAR[b][0:64, ci * CH:(ci + 1) * CH], s_dc[p])
            T.wait_ge(s_dc[p].h, t_p)
            T.wait_ge(s_cc.h, AR_DONE)
            if b > 0 or ci >= 2:
                T.wait_ge(s_act.h, DPRE[p])
            PRH[p] = mark(T, T.matmul(pp_msc[:, 0:CH], wdtb[:], prhs[:, csl],
                                      start=True, stop=True), s_pe)
            A.wait_ge(s_pe.h, s_pe.v)
            if b > 0 or ci >= 2:
                A.wait_ge(s_pv[p].h, DSF[p])
            aa()
            mark(A, A.activation(sfp1[:, csl], pp_msc[:, 0:CH], AF.Abs), s_act)
            A.wait_ge(s_act.h, s_act.v)
            aa()
            mark(A, A.activation(sfp1[:, csl], sfp1[:, csl], AF.Exp, scale=-1.0), s_act)
            A.wait_ge(s_act.h, s_act.v)
            aa()
            mark(A, A.activation(sfp1[:, csl], sfp1[:, csl], AF.Ln, bias=1.0), s_act)
            aa()
            DPRE[p] = mark(A, A.activation(sfp2[:, csl], pp_msc[:, 0:CH], AF.Relu), s_act)
            V.wait_ge(s_act.h, DPRE[p])
            vv()
            mark(V, V.tensor_add(sfp2[:, csl], sfp2[:, csl], sfp1[:, csl]), s_dve)
            Y.wait_ge(s_dve.h, s_dve.v)
            DSF[p] = dma(Y, deltaD[b][:, ci * CH:(ci + 1) * CH], sfp2[:, csl], s_pv[p])
            if cfg.taps:
                DSF[p] = dma(Y, tap["delta"][b, :, ci * CH:(ci + 1) * CH], sfp2[:, csl], s_pv[p])

    state = dict(XCF=[0, 0], DCF=[0, 0], BCF=[0, 0], ZBF=[0, 0], BRC=0,
                 TRT=[0, 0], ZTME=[0, 0], YT=[0, 0], X1O=[0, 0], PST=0,
                 POST=0, ZAGTOK={}, AGTOK={}, IDSTOK={})

    def mamba_scan(b):
        G.wait_ge(s_cc.h, AR_DONE)
        Y.wait_ge(s_pv[0].h, s_pv[0].v)
        Y.wait_ge(s_pv[1].h, s_pv[1].v)
        vv()
        mark(V, V.memset(hb[:], 0.0), s_dve)
        for ci in range(NCH):
            p = ci % 2
            csl = slice(p * CH, (p + 1) * CH)
            if state["XCF"][p]:
                Y.wait_ge(s_dve.h, state["XCF"][p])
            t_x = dma(Y, xc[:, csl], xT[b, :, ci * CH:(ci + 1) * CH], s_xc[p])
            if state["DCF"][p]:
                Y.wait_ge(s_act.h, state["DCF"][p])
            t_d = dma(Y, dc[:, csl], deltaD[b][:, ci * CH:(ci + 1) * CH], s_dc[p])
            if state["BCF"][p]:
                G.wait_ge(s_pe.h, state["BCF"][p])
            dma(G, bccB[:, csl], projAR[b][64:80, ci * CH:(ci + 1) * CH], s_bc[p])
            t_b = dma(G, bccC[:, csl], projAR[b][80:96, ci * CH:(ci + 1) * CH], s_bc[p])
            V.wait_ge(s_xc[p].h, t_x)
            V.wait_ge(s_dc[p].h, t_d)
            vv()
            mark(V, V.tensor_mul(dxc[:, csl], dc[:, csl], xc[:, csl]), s_dve)
            DX_T = s_dve.v
            if cfg.taps and state["YT"][p]:
                V.wait_ge(s_ty[p].h, state["YT"][p])
            for half in range(2):
                for nn in range(8):
                    n = half * 8 + nn
                    npar = n % 2
                    nsl = slice(npar * CH, (npar + 1) * CH)
                    T.wait_ge(s_bc[p].h, t_b)
                    if state["BRC"]:
                        T.wait_ge(s_dve.h, state["BRC"])
                    mark(T, T.matmul(pp_brep[:, 0:CH], onehb[:, n * 128:(n + 1) * 128],
                                     bccB[:, csl], start=True, stop=True), s_pe)
                    PE_N = mark(T, T.matmul(pp_crep[:, 0:CH], onehb[:, n * 128:(n + 1) * 128],
                                            bccC[:, csl], start=True, stop=True), s_pe)
                    A.wait_ge(s_dc[p].h, t_d)
                    A.wait_ge(s_dve.h, ACOL_RDY)
                    if state["BRC"]:
                        A.wait_ge(s_dve.h, state["BRC"])
                    aa()
                    ACT_N = mark(A, A.activation(dab[:, nsl], dc[:, csl], AF.Exp,
                                                 scale=acol[:, n:n + 1]), s_act)
                    V.wait_ge(s_pe.h, PE_N)
                    vv()
                    mark(V, V.tensor_mul(dbxb[:, nsl], dxc[:, csl], pp_brep[:, 0:CH]), s_dve)
                    V.wait_ge(s_act.h, ACT_N)
                    vv()
                    mark(V, V.tensor_tensor_scan(
                        hb[:, n * CH:(n + 1) * CH], dab[:, nsl], dbxb[:, nsl],
                        hb[:, (n + 1) * CH - 1:(n + 1) * CH], OP.mult, OP.add), s_dve)
                    vv()
                    state["BRC"] = mark(V, V.tensor_mul(
                        hcb[:, nn * CH:(nn + 1) * CH],
                        hb[:, n * CH:(n + 1) * CH], pp_crep[:, 0:CH]), s_dve)
                for st in (4, 2, 1):
                    for kk in range(st):
                        vv()
                        mark(V, V.tensor_add(hcb[:, kk * CH:(kk + 1) * CH],
                                             hcb[:, kk * CH:(kk + 1) * CH],
                                             hcb[:, (kk + st) * CH:(kk + st + 1) * CH]), s_dve)
                if half == 0:
                    vv()
                    mark(V, V.tensor_copy(yhalf[:], hcb[:, 0:CH]), s_dve)
                else:
                    vv()
                    mark(V, V.tensor_add(yb[:, csl], yhalf[:], hcb[:, 0:CH]), s_dve)
            state["BCF"][p] = s_pe.v
            state["DCF"][p] = s_act.v
            if cfg.taps:
                Y.wait_ge(s_dve.h, s_dve.v)
                state["YT"][p] = dma(Y, tap["y"][b, :, ci * CH:(ci + 1) * CH],
                                     yb[:, csl], s_ty[p])
            if state["ZBF"][p]:
                V.wait_ge(s_z[p].h, state["ZBF"][p])
            vv()
            mark(V, V.scalar_tensor_tensor(zbuf[:, csl], xc[:, csl], dcol[:, 0:1],
                                           yb[:, csl], OP.mult, OP.add), s_dve)
            state["XCF"][p] = s_dve.v
            Y.wait_ge(s_dve.h, s_dve.v)
            for k in range(CH // 128):
                blk = ci * (CH // 128) + k
                dstcol = (blk % 8) * TOK + (blk // 8) * 128
                state["ZBF"][p] = dma(Y, zin[b][:, dstcol:dstcol + 128],
                                      zbuf[:, csl.start + k * 128:csl.start + (k + 1) * 128],
                                      s_z[p])
            yield
        G.wait_ge(s_z[0].h, s_z[0].v)
        G.wait_ge(s_z[1].h, s_z[1].v)
        G.collective_compute("AllGather", OP.bypass, RG,
                             [zin[b]], [zAG[b]]).then_inc(s_cc.h, 1)
        s_cc.v += 1
        state["ZAGTOK"][b] = s_cc.v

    def owner_ln1(b):
        yield
        G.wait_ge(s_cc.h, state["ZAGTOK"][b])
        t_zown = dma(G, zownD[b], zAG[b][:, bass.ts(regs["c"], TOK)], s_gp)
        zsrc = zownD[b].rearrange("(dt p) t -> p dt t", p=128)
        G.wait_ge(s_gp.h, t_zown)
        for i in range(NBLK):
            yield
            p = i % 2
            sl1 = slice(p * DM, (p + 1) * DM)
            if state["TRT"][p]:
                G.wait_ge(s_pe.h, state["TRT"][p])
            t_z = dma(G, zblk[:, p * 1024:(p + 1) * 1024],
                      zsrc[:, :, i * 128:(i + 1) * 128], s_zb[p])
            T.wait_ge(s_zb[p].h, t_z)
            T.wait_ge(s_dve.h, s_dve.v)
            T.wait_ge(s_act.h, s_act.v)
            for dt in range(8):
                mark(T, T.transpose(pp_tr[:, dt * 128:(dt + 1) * 128],
                                    zblk[:, p * 1024 + dt * 128:p * 1024 + (dt + 1) * 128],
                                    idb[:]), s_pe)
            state["TRT"][p] = s_pe.v
            V.wait_ge(s_pe.h, s_pe.v)
            vv()
            mark(V, V.tensor_copy(ztm[:, sl1], pp_tr[:, 0:1024]), s_dve)
            A.wait_ge(s_dve.h, s_dve.v)
            aa()
            mark(A, A.activation(zsc[:], ztm[:, sl1], AF.Copy, accum_out=statb[:, 0:1]), s_act)
            aa()
            mark(A, A.activation(zsc[:], ztm[:, sl1], AF.Square, accum_out=statb[:, 1:2]), s_act)
            V.wait_ge(s_act.h, s_act.v)
            vv()
            mark(V, V.tensor_scalar_mul(statb[:, 2:3], statb[:, 0:1], 1.0 / DM), s_dve)
            vv()
            mark(V, V.tensor_mul(statb[:, 3:4], statb[:, 2:3], statb[:, 2:3]), s_dve)
            vv()
            mark(V, V.tensor_scalar(statb[:, 4:5], statb[:, 1:2], 1.0 / DM, EPS,
                                    OP.mult, OP.add), s_dve)
            vv()
            mark(V, V.tensor_sub(statb[:, 4:5], statb[:, 4:5], statb[:, 3:4]), s_dve)
            emit_rsqrt(statb[:, 4:5], statb[:, 5:6], statb[:, 6:7])
            vv()
            if state["X1O"][p]:
                V.wait_ge(s_pv[p].h, state["X1O"][p])
            mark(V, V.tensor_scalar(x1bf[:, sl1], ztm[:, sl1], statb[:, 2:3],
                                    statb[:, 5:6], OP.subtract, OP.mult), s_dve)
            X1_T = s_dve.v
            Y.wait_ge(s_dve.h, X1_T)
            state["X1O"][p] = dma(Y, x1own[b][i * 128:(i + 1) * 128, :], x1bf[:, sl1], s_pv[p])
            T.wait_ge(s_dve.h, X1_T)
            for dt in range(8):
                mark(T, T.transpose(pp_tr[:, dt * 128:(dt + 1) * 128],
                                    x1bf[:, p * DM + dt * 128:p * DM + (dt + 1) * 128],
                                    idb[:]), s_pe)
            V.wait_ge(s_pe.h, s_pe.v)
            vv()
            mark(V, V.tensor_copy(x1dm[:], pp_tr[:, 0:1024]), s_dve)
            T.wait_ge(s_dve.h, s_dve.v)
            for dt in range(8):
                mark(T, T.matmul(pp_msc[0:8, 0:128], gwb[:, dt * 8:(dt + 1) * 8],
                                 x1dm[:, dt * 128:(dt + 1) * 128],
                                 start=(dt == 0), stop=(dt == 7)), s_pe)
            V.wait_ge(s_pe.h, s_pe.v)
            vv()
            if state["PST"]:
                V.wait_ge(s_ps.h, state["PST"])
            mark(V, V.tensor_copy(sct[:], pp_msc[0:8, 0:128]), s_dve)
            T.wait_ge(s_dve.h, s_dve.v)
            mark(T, T.transpose(pp_msc[0:128, 128:136], sct[:], id32[0:8, 0:8]), s_pe)
            V.wait_ge(s_pe.h, s_pe.v)
            vv()
            mark(V, V.tensor_copy(smv[:, 0:8], pp_msc[0:128, 128:136]), s_dve)
            vv()
            mark(V, V.reduce_max(smv[:, 8:9], smv[:, 0:8], axis=mybir.AxisListType.X), s_dve)
            vv()
            mark(V, V.tensor_scalar_mul(smv[:, 9:10], smv[:, 8:9], -1.0), s_dve)
            A.wait_ge(s_dve.h, s_dve.v)
            aa()
            mark(A, A.activation(smv[:, 0:8], smv[:, 0:8], AF.Exp,
                                 bias=smv[:, 9:10], accum_out=smv[:, 10:11]), s_act)
            V.wait_ge(s_act.h, s_act.v)
            vv()
            mark(V, V.reciprocal(smv[:, 11:12], smv[:, 10:11]), s_dve)
            vv()
            mark(V, V.tensor_scalar(ptb[:, 0:8], smv[:, 0:8], smv[:, 11:12], None,
                                    OP.mult), s_dve)
            T.wait_ge(s_dve.h, s_dve.v)
            T.wait_ge(s_act.h, s_act.v)
            mark(T, T.transpose(pp_tr[0:8, 0:128], ptb[:, 0:8], idb[:]), s_pe)
            V.wait_ge(s_pe.h, s_pe.v)
            vv()
            mark(V, V.tensor_copy(sct[:], pp_tr[0:8, 0:128]), s_dve)
            Y.wait_ge(s_dve.h, s_dve.v)
            state["PST"] = dma(Y, probsT[b][:, i * 128:(i + 1) * 128], sct[:], s_ps)
        G.wait_ge(s_pv[0].h, s_pv[0].v)
        G.wait_ge(s_pv[1].h, s_pv[1].v)
        G.wait_ge(s_ps.h, s_ps.v)
        G.collective_compute("AllGather", OP.bypass, RG, [x1own[b]], [x1AG[b]]).then_inc(s_cc.h, 1)
        s_cc.v += 1
        G.collective_compute("AllGather", OP.bypass, RG, [probsT[b]], [probsAG[b]]).then_inc(s_cc.h, 1)
        s_cc.v += 1
        if cfg.taps:
            G.wait_ge(s_cc.h, s_cc.v)
            dma(G, tap["x1"][b], x1AG[b], s_gp)
            dma(G, tap["probs"][b], probsAG[b], s_gp)
            dma(G, tap["z"][b], zAG[b], s_gp)
            G.wait_ge(s_gp.h, s_gp.v)
        state["AGTOK"][b] = s_cc.v

    def topk(b):
        boff = (b % 2) * NTAU
        yield
        for _ in range(12):
            yield
        G.wait_ge(s_cc.h, state["AGTOK"][b])
        dma(G, probsAGx2[b][0:64, :], probsAG[b], s_gp)
        dma(G, probsAGx2[b][64:128, :], probsAG[b], s_gp)
        t_x2 = s_gp.v
        G.wait_ge(s_gp.h, t_x2)
        dma(G, probsRot[b], probsAGx2[b][bass.ds(regs["c"], 64), :], s_gp)
        G.wait_ge(s_gp.h, s_gp.v)
        srcv = probsRot[b].rearrange("(r g) (h f) -> g r h f", g=8, h=2)
        for g in range(8):
            dma(G, pbis[16 * g:16 * (g + 1), :], srcv[g:g + 1, :, :, :], s_gp)
        PB_T = s_gp.v
        V.wait_ge(s_gp.h, PB_T)
        vv()
        mark(V, V.memset(bcols[:, 0:1], 0.0), s_dve)
        vv()
        mark(V, V.memset(bcols[:, 1:2], 1.0), s_dve)
        for it in range(BISECT_ITERS):
            yield
            vv()
            mark(V, V.tensor_add(bcols[:, 2:3], bcols[:, 0:1], bcols[:, 1:2]), s_dve)
            vv()
            mark(V, V.tensor_scalar_mul(bcols[:, 2:3], bcols[:, 2:3], 0.5), s_dve)
            vv()
            mark(V, V.tensor_scalar(bmask[:], pbis[:], bcols[:, 2:3], 0.0,
                                    OP.is_gt, OP.add, accum_out=bcols[:, 3:4]), s_dve)
            T.wait_ge(s_dve.h, s_dve.v)
            mark(T, T.matmul(pp_msc[0:128, 0:1], gmb[:], bcols[:, 3:4],
                             start=True, stop=True), s_pe)
            V.wait_ge(s_pe.h, s_pe.v)
            vv()
            mark(V, V.tensor_copy(bcols[:, 4:5], pp_msc[0:128, 0:1]), s_dve)
            vv()
            mark(V, V.tensor_scalar(bmask2[:, 0:1], bcols[:, 4:5], float(K), None,
                                    OP.is_ge), s_dve)
            vv()
            mark(V, V.tensor_sub(bcols[:, 5:6], bcols[:, 2:3], bcols[:, 0:1]), s_dve)
            vv()
            mark(V, V.tensor_mul(bcols[:, 5:6], bcols[:, 5:6], bmask2[:, 0:1]), s_dve)
            vv()
            mark(V, V.tensor_add(bcols[:, 0:1], bcols[:, 0:1], bcols[:, 5:6]), s_dve)
            vv()
            mark(V, V.tensor_sub(bcols[:, 5:6], bcols[:, 1:2], bcols[:, 2:3]), s_dve)
            vv()
            mark(V, V.tensor_mul(bcols[:, 5:6], bcols[:, 5:6], bmask2[:, 0:1]), s_dve)
            vv()
            mark(V, V.tensor_add(bcols[:, 1:2], bcols[:, 2:3], bcols[:, 5:6]), s_dve)
        yield
        vv()
        mark(V, V.tensor_scalar(bmask[:], pbis[:], bcols[:, 1:2], 0.0,
                                OP.is_gt, OP.add, accum_out=bcols[:, 3:4]), s_dve)
        T.wait_ge(s_dve.h, s_dve.v)
        mark(T, T.matmul(pp_msc[0:128, 0:1], gmb[:], bcols[:, 3:4],
                         start=True, stop=True), s_pe)
        V.wait_ge(s_pe.h, s_pe.v)
        vv()
        mark(V, V.tensor_copy(bcols[:, 5:6], pp_msc[0:128, 0:1]), s_dve)
        vv()
        mark(V, V.tensor_scalar(bmask[:], pbis[:], bcols[:, 1:2], None, OP.is_gt), s_dve)
        vv()
        mark(V, V.tensor_scalar(bmask2[:], pbis[:], bcols[:, 0:1], None, OP.is_gt), s_dve)
        vv()
        mark(V, V.tensor_sub(bmask2[:], bmask2[:], bmask[:]), s_dve)
        vv()
        mark(V, V.tensor_tensor_scan(cumb[:], onesc[:, 0:FB], bmask2[:], 0.0,
                                     OP.mult, OP.add), s_dve)
        vv()
        mark(V, V.tensor_copy(bcols[:, 6:7], cumb[:, FB - 1:FB]), s_dve)
        T.wait_ge(s_dve.h, s_dve.v)
        mark(T, T.matmul(pp_msc[0:128, 0:1], tmb[:], bcols[:, 6:7],
                         start=True, stop=True), s_pe)
        V.wait_ge(s_pe.h, s_pe.v)
        vv()
        mark(V, V.tensor_copy(bcols[:, 7:8], pp_msc[0:128, 0:1]), s_dve)
        vv()
        mark(V, V.tensor_scalar(cumb[:], cumb[:], bcols[:, 7:8], -1.0, OP.add, OP.add), s_dve)
        vv()
        mark(V, V.tensor_scalar_mul(bcols[:, 6:7], bcols[:, 5:6], -1.0), s_dve)
        vv()
        mark(V, V.tensor_scalar(bcols[:, 6:7], bcols[:, 6:7], float(K), None, OP.add), s_dve)
        vv()
        mark(V, V.tensor_scalar(cumb[:], cumb[:], bcols[:, 6:7], None, OP.is_lt), s_dve)
        vv()
        mark(V, V.tensor_mul(cumb[:], cumb[:], bmask2[:]), s_dve)
        vv()
        mark(V, V.tensor_add(bmask[:], bmask[:], cumb[:]), s_dve)
        if cfg.taps:
            Y.wait_ge(s_dve.h, s_dve.v)
            dma(Y, tap["bis"][b], bcols[:], s_sy)
        yield
        vv()
        mark(V, V.tensor_tensor_scan(cumb[:], onesc[:, 0:FB], bmask[:], 0.0,
                                     OP.mult, OP.add), s_dve)
        vv()
        mark(V, V.tensor_copy(bcols[:, 6:7], cumb[:, FB - 1:FB]), s_dve)
        T.wait_ge(s_dve.h, s_dve.v)
        mark(T, T.matmul(pp_msc[0:128, 0:1], tmb[:], bcols[:, 6:7],
                         start=True, stop=True), s_pe)
        V.wait_ge(s_pe.h, s_pe.v)
        vv()
        mark(V, V.tensor_copy(bcols[:, 7:8], pp_msc[0:128, 0:1]), s_dve)
        vv()
        mark(V, V.tensor_scalar(cumb[:], cumb[:], bcols[:, 7:8], -1.0, OP.add, OP.add), s_dve)
        vv()
        mark(V, V.tensor_mul(ucols[:, 0:FB], cumb[:], bmask[:]), s_dve)
        vv()
        mark(V, V.tensor_scalar(bmask2[:], bmask[:], 0.5, 1e9, OP.is_lt, OP.mult), s_dve)
        vv()
        mark(V, V.tensor_add(ucols[:, 0:FB], ucols[:, 0:FB], bmask2[:]), s_dve)
        ui = ucols.bitcast(I32)
        ci2 = cumb.bitcast(I32)
        vv()
        mark(V, V.tensor_copy(ui[:, 0:FB], ucols[:, 0:FB]), s_dve)
        vv()
        mark(V, V.tensor_single_scalar(ci2[:, 0:FB], ui[:, 0:FB], 127, OP.bitwise_and), s_dve)
        vv()
        mark(V, V.tensor_single_scalar(ui[:, 0:FB], ui[:, 0:FB], 7, OP.arith_shift_right), s_dve)
        vv()
        mark(V, V.tensor_single_scalar(ci2[:, 0:FB], ci2[:, 0:FB], NTAU, OP.mult), s_dve)
        vv()
        mark(V, V.tensor_tensor(ui[:, 0:FB], ui[:, 0:FB], ci2[:, 0:FB], OP.add), s_dve)
        UP_T = s_dve.v
        Y.wait_ge(s_dve.h, UP_T)
        dma(Y, ubounce[b], ui[0:16, 0:FB], s_sy)
        dma(Y, gbounce[b], pbis[0:16, :], s_sy)
        Y.wait_ge(s_sy.h, s_sy.v)
        dma(Y, u32b[:], ubounce[b].rearrange("a (p c) -> (a p) c", p=8), s_sy)
        t_g128 = dma(Y, g128[:], gbounce[b].rearrange("a (p c) -> (a p) c", p=8), s_sy)
        V.wait_ge(s_sy.h, t_g128)
        V.wait_ge(s_ld.h, LD0)
        vv()
        mark(V, V.tensor_copy(pairb[:, 0:2 * PD:2], tk128[:]), s_dve)
        vv()
        mark(V, V.tensor_copy(pairb[:, 1:2 * PD:2], g128[:]), s_dve)
        t_rb = s_sy.v
        Y.wait_ge(s_dve.h, s_dve.v)
        G.wait_ge(s_sy.h, t_rb)
        G.wait_ge(s_dve.h, s_dve.v)
        G.wait_ge(s_gp.h, ZIDG)
        idgflat = idg[b].rearrange("p c -> (p c)").rearrange("(x o) -> x o", o=2)
        for q in range(PD):
            if q % 16 == 0:
                yield
            G.indirect_dma_start(
                out=idgflat, out_offset=IndirectOffsetOnAxis(u32b[:, q:q + 1], 0),
                in_=pairb[:, 2 * q:2 * q + 2], in_offset=None,
                bounds_check=bc128, oob_is_err=False,
            ).then_inc(s_gp.h, 16)
            s_gp.v += 16
        IDG_T = s_gp.v
        if cfg.taps:
            G.wait_ge(s_gp.h, IDG_T)
            dma(G, tap["idg"][b], idg[b], s_gp)
            G.wait_ge(s_gp.h, s_gp.v)
        Y.wait_ge(s_gp.h, IDG_T)
        t_ig = dma(Y, ucols[:, 0:2 * NTAU], idg[b], s_sy)
        V.wait_ge(s_sy.h, t_ig)
        vv()
        mark(V, V.tensor_copy(idsb[:, boff:boff + NTAU], ucols[:, 0:2 * NTAU:2]), s_dve)
        vv()
        mark(V, V.tensor_copy(gsb[:, boff:boff + NTAU], ucols[:, 1:2 * NTAU:2]), s_dve)
        state["IDSTOK"][b] = s_dve.v

    def ffn(b, ids_tok, aux=None, aux_every=1):
        boff = (b % 2) * NTAU
        step = 0
        O2C = [0]
        HGU = [0]
        WR_EV = [0, 0]
        XGU = [0, 0]
        XGP = [0, 0]
        GEL_T = [0, 0]
        PSR_T = [0, 0]
        GEL_ALL = 0
        O1_T = [0, 0]
        ctb_tok = [0] * (NTAU // 2)
        for hq in range(4):
            G.wait_ge(s_pe.h, s_pe.v)
            w1_tok = dma(G, w1q[:].rearrange("p (dt h) -> p dt h", dt=8),
                         w1e.rearrange("(dt p) h -> p dt h", p=128)
                         [:, :, hq * HQ:(hq + 1) * HQ], s_w1)
            w2_tok = dma(G, w2q[:].rearrange("p (a d) -> p a d", d=DM),
                         w2e.rearrange("(a p) d -> p a d", p=128)
                         [:, hq * NHT:(hq + 1) * NHT, :], s_w2)
            if cfg.taps and b == 0 and hq == 1:
                G.wait_ge(s_ctb[0].h, s_ctb[0].v)
                G.wait_ge(s_ctb[1].h, s_ctb[1].v)
                dma(G, tap["c0"][0], contrib[b], s_gp)
                G.wait_ge(s_gp.h, s_gp.v)
            if cfg.taps and b == 0 and hq == 0:
                G.wait_ge(s_w1.h, w1_tok)
                G.wait_ge(s_w2.h, w2_tok)
                dma(G, tap["w1q"][0][:, 0:8 * HQ], w1q[:], s_gp)
                dma(G, tap["w2q"][0][:, 0:NHT * DM], w2q[:], s_gp)
                G.wait_ge(s_gp.h, s_gp.v)
            for g in range(NG):
                if aux is not None and step % aux_every == 0:
                    next(aux, None)
                step += 1
                gp = g % 2
                for tt in range(GQ):
                    tau = g * GQ + tt
                    tp = tau % 2
                    if aux is not None and hq >= 1:
                        next(aux, None)
                    G.wait_ge(s_dve.h, ids_tok)
                    if XGP[tp]:
                        G.wait_ge(s_pe.h, XGP[tp])
                    G.indirect_dma_start(
                        out=xg[:, tp * DM:(tp + 1) * DM], out_offset=None,
                        in_=x1AG[b],
                        in_offset=IndirectOffsetOnAxis(idsb[:, boff + tau:boff + tau + 1], 0),
                    ).then_inc(s_xg[tp].h, 16)
                    s_xg[tp].v += 16
                    t_xg = s_xg[tp].v
                    T.wait_ge(s_xg[tp].h, t_xg)
                    T.wait_ge(s_dve.h, s_dve.v)
                    T.wait_ge(s_act.h, s_act.v)
                    for dt in range(8):
                        mark(T, T.transpose(pp_tr[:, dt * 128:(dt + 1) * 128],
                                            xg[:, tp * DM + dt * 128:tp * DM + (dt + 1) * 128],
                                            idb[:]), s_pe)
                    XGP[tp] = s_pe.v
                    A.wait_ge(s_pe.h, s_pe.v)
                    if XGU[gp]:
                        A.wait_ge(s_pe.h, XGU[gp])
                    aa()
                    xgt_dst = xgt[:, gp * 8 * GQ * 128:(gp + 1) * 8 * GQ * 128] \
                        .rearrange("p (dt t c) -> p dt t c", dt=8, t=GQ, c=128)[:, :, tt, :]
                    mark(A, A.activation(
                        xgt_dst,
                        pp_tr[:, 0:1024].rearrange("p (dt c) -> p dt c", dt=8, c=128),
                        AF.Copy), s_act)
                XGT_T = s_act.v
                for ht in range(NHT):
                    hp = ht % 2
                    o1sl = slice(hp * O1S, hp * O1S + GQ * 128)
                    if ht == 0:
                        T.wait_ge(s_w1.h, w1_tok)
                        T.wait_ge(s_act.h, XGT_T)
                    if PSR_T[hp]:
                        T.wait_ge(s_act.h, PSR_T[hp])
                    for dt in range(8):
                        mark(T, T.matmul(pp_o1[:, o1sl],
                                         w1q[:, dt * HQ + ht * 128:dt * HQ + (ht + 1) * 128],
                                         xgt[:, (gp * 8 + dt) * GQ * 128:
                                                (gp * 8 + dt + 1) * GQ * 128],
                                         start=(dt == 0), stop=(dt == 7)), s_pe)
                    O1_T[hp] = s_pe.v
                    XGU[gp] = s_pe.v
                    A.wait_ge(s_pe.h, O1_T[hp])
                    if GEL_ALL:
                        A.wait_ge(s_dve.h, GEL_ALL)
                    aa()
                    mark(A, A.activation(gx[:], pp_o1[:, o1sl], AF.Copy), s_act)
                    aa()
                    PSR_T[hp] = mark(A, A.activation(gsc1[:], pp_o1[:, o1sl], AF.Square), s_act)
                    aa()
                    mark(A, A.activation(gsc1[:], gsc1[:], AF.Copy, scale=GK, bias=SQ2PI), s_act)
                    V.wait_ge(s_act.h, s_act.v)
                    vv()
                    mark(V, V.tensor_mul(gsc2[:], gx[:], gsc1[:]), s_dve)
                    A.wait_ge(s_dve.h, s_dve.v)
                    aa()
                    mark(A, A.activation(gsc2[:], gsc2[:], AF.Tanh), s_act)
                    aa()
                    TH_T = mark(A, A.activation(gsc2[:], gsc2[:], AF.Copy,
                                                scale=0.5, bias=0.5), s_act)
                    V.wait_ge(s_act.h, TH_T)
                    vv()
                    if HGU[0]:
                        V.wait_ge(s_pe.h, HGU[0])
                    GEL_T[hp] = mark(V, V.tensor_mul(hgel[:, ht * GQ * 128:(ht + 1) * GQ * 128],
                                                     gx[:], gsc2[:]), s_dve)
                    GEL_ALL = GEL_T[hp]
                if cfg.taps and b == 0 and hq == 0 and g == 0:
                    Y.wait_ge(s_dve.h, s_dve.v)
                    dma(Y, tap["xgt"][0][:, 0:8 * GQ * 128],
                        xgt[:, 0:8 * GQ * 128], s_sy)
                    dma(Y, tap["hgel"][0][:, 0:NHT * GQ * 128], hgel[:], s_sy)
                    Y.wait_ge(s_sy.h, s_sy.v)
                    V.wait_ge(s_sy.h, s_sy.v)
                for tt in range(GQ):
                    tau = g * GQ + tt
                    pr = (tau // 2) % 2
                    pi = tau % 2
                    for ht in range(NHT):
                        if ht == 0:
                            T.wait_ge(s_dve.h, GEL_ALL)
                            T.wait_ge(s_w2.h, w2_tok)
                            if O2C[0]:
                                T.wait_ge(s_act.h, O2C[0])
                            if WR_EV[pr] and pi == 0:
                                T.wait_ge(s_ctb[pr].h, WR_EV[pr])
                        for h2 in range(2):
                            mark(T, T.matmul(pp_o2[:, h2 * 512:(h2 + 1) * 512],
                                             hgel[:, ht * GQ * 128 + tt * 128:
                                                    ht * GQ * 128 + (tt + 1) * 128],
                                             w2q[:, ht * DM + h2 * 512:ht * DM + (h2 + 1) * 512],
                                             start=(ht == 0), stop=(ht == NHT - 1)), s_pe)
                    HGU[0] = s_pe.v
                    A.wait_ge(s_pe.h, s_pe.v)
                    if WR_EV[pr] and pi == 0:
                        A.wait_ge(s_ctb[pr].h, WR_EV[pr])
                    aa()
                    O2C[0] = mark(A, A.activation(
                        wrow[:, (pr * 2 + pi) * DM:(pr * 2 + pi + 1) * DM],
                        pp_o2[:, 0:DM], AF.Copy), s_act)
                    if cfg.taps and b == 0 and hq == 0 and g == 0 and tau == 0:
                        Y.wait_ge(s_dve.h, s_dve.v)
                        dma(Y, tap["wrow"][0][:, 0:DM], wrow[:, 0:DM], s_sy)
                        Y.wait_ge(s_sy.h, s_sy.v)
                        V.wait_ge(s_sy.h, s_sy.v)
                    if pi == 1:
                        G.wait_ge(s_act.h, s_act.v)
                        G.wait_ge(s_ctb[pr].h, s_ctb[pr].v)
                        rows = slice(128 + (tau - 1) * 128, 128 + (tau + 1) * 128)
                        srcv = wrow[:, pr * 2 * DM:(pr + 1) * 2 * DM].rearrange(
                            "p (t d) -> p t d", d=DM)
                        dstv = contrib[b][rows, :].rearrange("(t p) d -> p t d", p=128)
                        if hq == 0:
                            ctb_tok[tau // 2] = dma(G, dstv, srcv, s_ctb[pr])
                        else:
                            ctb_tok[tau // 2] = dma_acc(dstv, srcv, s_ctb[pr])
                        WR_EV[pr] = ctb_tok[tau // 2]
        GT_T = [0, 0]
        for pp2 in range(NTAU // 2):
            pr = pp2 % 2
            rows = slice(128 + pp2 * 256, 128 + (pp2 + 1) * 256)
            G.wait_ge(s_ctb[pr].h, s_ctb[pr].v)
            t_r = dma(G, wrow[:, pr * 2 * DM:(pr + 1) * 2 * DM].rearrange(
                "p (t d) -> p t d", d=DM),
                contrib[b][rows, :].rearrange("(t p) d -> p t d", p=128), s_gp)
            V.wait_ge(s_gp.h, t_r)
            for tt in range(2):
                tau = pp2 * 2 + tt
                vv()
                mark(V, V.tensor_scalar(
                    wrow[:, (pr * 2 + tt) * DM:(pr * 2 + tt + 1) * DM],
                    wrow[:, (pr * 2 + tt) * DM:(pr * 2 + tt + 1) * DM],
                    gsb[:, boff + tau:boff + tau + 1], None, OP.mult), s_dve)
            G.wait_ge(s_dve.h, s_dve.v)
            GT_T[pr] = dma(G, contrib[b][rows, :].rearrange("(t p) d -> p t d", p=128),
                           wrow[:, pr * 2 * DM:(pr + 1) * 2 * DM].rearrange(
                               "p (t d) -> p t d", d=DM), s_ctb[pr])
            ctb_tok[pp2] = GT_T[pr]
        uu = posu[:]
        if state["POST"]:
            V.wait_ge(s_gp.h, state["POST"])
        vv()
        mark(V, V.tensor_single_scalar(uu[:, 0:NTAU], idsb[:, boff:boff + NTAU],
                                       127, OP.bitwise_and), s_dve)
        vv()
        mark(V, V.tensor_single_scalar(uu[:, 0:NTAU], uu[:, 0:NTAU], PD, OP.mult), s_dve)
        vv()
        mark(V, V.tensor_single_scalar(uu[:, NTAU:2 * NTAU], idsb[:, boff:boff + NTAU], 7,
                                       OP.arith_shift_right), s_dve)
        vv()
        mark(V, V.tensor_tensor(uu[:, 0:NTAU], uu[:, 0:NTAU], uu[:, NTAU:2 * NTAU],
                                OP.add), s_dve)
        PJ_T = s_dve.v
        G.wait_ge(s_dve.h, PJ_T)
        pdflat = posdense[b].rearrange("p c -> (p c)").rearrange("(x o) -> x o", o=1)
        for tau in range(NTAU):
            G.indirect_dma_start(
                out=pdflat, out_offset=IndirectOffsetOnAxis(uu[:, tau:tau + 1], 0),
                in_=jvb[:, tau:tau + 1], in_offset=None,
                bounds_check=bcpd, oob_is_err=False,
            ).then_inc(s_gp.h, 16)
            s_gp.v += 16
        POS_T = s_gp.v
        state["POST"] = POS_T
        G.wait_ge(s_ctb[0].h, s_ctb[0].v)
        G.wait_ge(s_ctb[1].h, s_ctb[1].v)
        G.collective_compute("AllGather", OP.bypass, RG,
                             [contrib[b]], [contribAG[b]]).then_inc(s_cc.h, 1)
        s_cc.v += 1
        G.wait_ge(s_gp.h, POS_T)
        G.collective_compute("AllGather", OP.bypass, RG,
                             [posdense[b]], [posAG[b]]).then_inc(s_cc.h, 1)
        s_cc.v += 1
        if cfg.taps:
            G.wait_ge(s_cc.h, s_cc.v)
            dma(G, tap["contrib"][b], contrib[b], s_gp)
            dma(G, tap["pos"][b], posAG[b], s_gp)
            G.wait_ge(s_gp.h, s_gp.v)
        return s_cc.v

    def finalize(b, cc_tok):
        Y.wait_ge(s_sy.h, s_sy.v)
        G.wait_ge(s_cc.h, cc_tok)
        t_pos = dma(G, posownD[b], posAG[b][:, bass.ts(regs["c"], NBLK)], s_gp)
        G.wait_ge(s_gp.h, t_pos)
        t_a8 = dma(G, addr8[:].rearrange("p (e n) -> p e n", n=NBLK),
                   posownD[b].rearrange("(e p) n -> p e n", p=128), s_gp)
        G.wait_ge(s_gp.h, t_a8)
        ADD_T = [0, 0]
        PRV = 0
        for i in range(NBLK):
            if PRV:
                Y.wait_ge(s_dve.h, PRV)
            t_x1 = dma(Y, x1r[:, 0:DM], x1own[b][i * 128:(i + 1) * 128, :], s_sy)
            for e in range(8):
                ip = e % 2
                sl = slice(ip * DM, (ip + 1) * DM)
                if ADD_T[ip]:
                    G.wait_ge(s_dve.h, ADD_T[ip])
                G.indirect_dma_start(
                    out=gat1[:, sl], out_offset=None,
                    in_=contribAG[b],
                    in_offset=IndirectOffsetOnAxis(addr8[:, e * NBLK + i:e * NBLK + i + 1], 0),
                ).then_inc(s_fg[ip].h, 16)
                s_fg[ip].v += 16
                V.wait_ge(s_fg[ip].h, s_fg[ip].v)
                if e == 0:
                    vv()
                    ADD_T[ip] = mark(V, V.tensor_copy(accr[:], gat1[:, sl]), s_dve)
                else:
                    vv()
                    ADD_T[ip] = mark(V, V.tensor_add(accr[:], accr[:], gat1[:, sl]), s_dve)
            V.wait_ge(s_sy.h, t_x1)
            V.wait_ge(s_act.h, s_act.v)
            V.wait_ge(s_outd.h, s_outd.v)
            vv()
            PRV = mark(V, V.tensor_add(z2b[:], x1r[:, 0:DM], accr[:]), s_dve)
            A.wait_ge(s_dve.h, s_dve.v)
            aa()
            mark(A, A.activation(zsc[:], z2b[:], AF.Copy, accum_out=statb[:, 0:1]), s_act)
            aa()
            mark(A, A.activation(zsc[:], z2b[:], AF.Square, accum_out=statb[:, 1:2]), s_act)
            V.wait_ge(s_act.h, s_act.v)
            vv()
            mark(V, V.tensor_scalar_mul(statb[:, 2:3], statb[:, 0:1], 1.0 / DM), s_dve)
            vv()
            mark(V, V.tensor_mul(statb[:, 3:4], statb[:, 2:3], statb[:, 2:3]), s_dve)
            vv()
            mark(V, V.tensor_scalar(statb[:, 4:5], statb[:, 1:2], 1.0 / DM, EPS,
                                    OP.mult, OP.add), s_dve)
            vv()
            mark(V, V.tensor_sub(statb[:, 4:5], statb[:, 4:5], statb[:, 3:4]), s_dve)
            emit_rsqrt(statb[:, 4:5], statb[:, 5:6], statb[:, 6:7])
            vv()
            mark(V, V.tensor_scalar(outb[:], z2b[:], statb[:, 2:3],
                                    statb[:, 5:6], OP.subtract, OP.mult), s_dve)
            Y.wait_ge(s_dve.h, s_dve.v)
            dma(Y, out[b, i], outb[:], s_outd)
            yield

    def chain2(*gens):
        for gg in gens:
            for _ in gg:
                yield

    SA = cfg.stop_after
    if SA is not None:
        for b in range(B):
            for _ in mamba_scan(b):
                pass
            if SA == "mamba":
                continue
            for _ in owner_ln1(b):
                pass
            if SA == "owner":
                continue
            for _ in topk(b):
                pass
            if SA == "topk":
                continue
            cctok = ffn(b, state["IDSTOK"][b])
            if SA == "ffn":
                continue
            for _ in finalize(b, cctok):
                pass
    else:
        g0 = mamba_scan(0)
        for _ in g0:
            pass
        for _ in owner_ln1(0):
            pass
        for _ in topk(0):
            pass
        aux0 = chain2(mamba_scan(1), owner_ln1(1), topk(1))
        c0 = ffn(0, state["IDSTOK"][0], aux=aux0)
        for _ in aux0:
            pass
        f0 = finalize(0, c0)
        c1 = ffn(1, state["IDSTOK"][1], aux=f0, aux_every=2)
        for _ in f0:
            pass
        for _ in finalize(1, c1):
            pass

    if SA is not None:
        for e_, sm in ((V, s_dve), (A, s_act), (T, s_pe), (G, s_gp), (Y, s_sy)):
            e_.wait_ge(sm.h, sm.v)
        G.wait_ge(s_cc.h, s_cc.v)
        Y.wait_ge(s_dve.h, s_dve.v)
        dma(Y, out[0, 0], outb[:], s_outd)
    Y.wait_ge(s_outd.h, s_outd.v)
    ctx.close()
    return nc, tap




def make_inputs(cfg, x, A_log, W_delta, W_dt, b_dt, W_B, W_C, Dv, gate_w, w1, w2):
    x = np.asarray(x, np.float32)
    onehots = np.kron(np.eye(16, dtype=np.float32), np.ones((1, 128), np.float32))
    gm = np.zeros((128, 128), np.float32)
    for p in range(128):
        gm[(p // 16) * 16:(p // 16) * 16 + 16, p] = 1.0
    tm = np.zeros((128, 128), np.float32)
    for p2 in range(128):
        for p in range((p2 // 16) * 16, p2):
            tm[p, p2] = 1.0
    tokmap = np.zeros((16, cfg.FB), np.float32)
    for s in range(16):
        for f in range(cfg.FB):
            tokmap[s, f] = agrow_of_token(cfg, tok_of(cfg, s, f))
    tokmap128 = np.ascontiguousarray(
        tokmap.reshape(16, 8, cfg.FB // 8).transpose(0, 1, 2).reshape(128, cfg.FB // 8))
    maps = []
    for c in range(8):
        jv = np.zeros((128, cfg.NTAU), np.int32)
        for tau in range(cfg.NTAU):
            for p in range(128):
                jv[p, tau] = cfg.CROW * c + 128 + 128 * tau + p
        sl = slice(128 * c, 128 * (c + 1))
        meta = np.zeros((1, 16), np.int32)
        meta[0, 0] = c
        for g in range(8):
            meta[0, 1 + g] = (g + c) % 8
        maps.append(dict(
            xT=np.ascontiguousarray(x[:, :, sl].transpose(0, 2, 1)),
            projW=np.ascontiguousarray(np.concatenate(
                [np.asarray(W_delta, np.float32)[sl],
                 np.asarray(W_B, np.float32)[sl],
                 np.asarray(W_C, np.float32)[sl]], axis=1)),
            wdt_ext=np.ascontiguousarray(np.concatenate(
                [np.asarray(W_dt, np.float32)[:, sl],
                 np.asarray(b_dt, np.float32)[None, sl]], axis=0)),
            a_sl=np.ascontiguousarray(np.asarray(A_log, np.float32)[sl]),
            d_sl=np.ascontiguousarray(np.asarray(Dv, np.float32)[sl, None]),
            gate_w=np.asarray(gate_w, np.float32),
            w1e=np.ascontiguousarray(np.asarray(w1, np.float32)[c]),
            w2e=np.ascontiguousarray(np.asarray(w2, np.float32)[c]),
            idn32=np.eye(128, dtype=np.float32),
            idnbf=np.eye(128, dtype=BF),
            onehots=onehots, gmask=gm, tmask=tm, tokmap=tokmap,
            tokmap128=tokmap128, jvals=jv,
            meta=meta,
        ))
    return maps


def assemble_out(cfg, outs):
    full = np.zeros((cfg.B, cfg.L, cfg.DM), np.float32)
    for c in range(8):
        o = outs[c]
        for i in range(cfg.NBLK):
            blk = 8 * i + c
            full[:, blk * 128:(blk + 1) * 128, :] = o[:, i]
    return full


def kernel(L0_attn_x, L0_attn_A_log, L0_attn_W_delta, L0_attn_W_dt, L0_attn_b_dt,
           L0_attn_W_B, L0_attn_W_C, L0_attn_D, L0_ffn_gate_w, L0_ffn_w1, L0_ffn_w2):
    cfg = Cfg()
    nc, _ = build_program(cfg)
    in_maps = make_inputs(cfg, L0_attn_x, L0_attn_A_log, L0_attn_W_delta, L0_attn_W_dt,
                          L0_attn_b_dt, L0_attn_W_B, L0_attn_W_C, L0_attn_D,
                          L0_ffn_gate_w, L0_ffn_w1, L0_ffn_w2)
    res = run_bass_kernel_spmd(nc, in_maps, core_ids=list(range(8)))
    return assemble_out(cfg, [r["out"] for r in res.results])
